# version 07
# baseline (speedup 1.0000x reference)
import math
from contextlib import ExitStack
import numpy as np
import ml_dtypes
import concourse.bass as bass
import concourse.mybir as mybir
from concourse.bass_utils import run_bass_kernel_spmd

F32 = mybir.dt.float32
BF16 = mybir.dt.bfloat16
I32 = mybir.dt.int32
AF = mybir.ActivationFunctionType
ALU = mybir.AluOpType
AX = mybir.AxisListType
BFnp = ml_dtypes.bfloat16

NCORES = 8
H = 2048
NH, HD = 16, 128
BLK = 256
CH = 2048
NB = CH // BLK
FC = H // 128
I8192 = 8192
OC = I8192 // 128
EPS_LN = 1e-5
EPS_ATTN = 1e-6
EPS_FFN = 1e-5

SLOPES = [2.0 ** (-(h + 1) / 2.0) for h in range(NH)]
WARM_DEPTH = [min(56, int(math.ceil(103.6 / (256.0 * s)))) for s in SLOPES]
WARM_NB = 56


class Sem:
    def __init__(self, handle, name):
        self.h = handle
        self.name = name
        self.val = 0


class Prog:
    ENGS = ("tensor", "vector", "scalar", "gpsimd", "sync")

    def __init__(self, nc, stack):
        self.nc = nc
        self.ops = {e: [] for e in self.ENGS}
        self._sems = {}
        self._stack = stack

    def sem(self, name):
        if name not in self._sems:
            handle = self._stack.enter_context(self.nc.semaphore(name))
            self._sems[name] = Sem(handle, name)
        return self._sems[name]

    def emit(self, eng, fn):
        self.ops[eng].append(fn)

    def dma(self, eng, out, in_, sem=None, **kw):
        if sem is not None:
            sem.val += 16
            self.emit(eng, lambda e, o=out, i=in_, s=sem, kw=kw:
                      e.dma_start(out=o, in_=i, **kw).then_inc(s.h, 16))
        else:
            self.emit(eng, lambda e, o=out, i=in_, kw=kw: e.dma_start(out=o, in_=i, **kw))

    def wait(self, eng, sem, val=None):
        v = sem.val if val is None else val
        self.emit(eng, lambda e, s=sem, v=v: e.wait_ge(s.h, v))

    def op(self, eng, fn, sem=None, inc=1):
        if sem is not None:
            sem.val += inc
            self.emit(eng, lambda e, f=fn, s=sem, k=inc: f(e).then_inc(s.h, k))
        else:
            self.emit(eng, fn)

    def barrier(self):
        snap = [(s, s.val) for s in self._sems.values() if s.val > 0]
        for eng in self.ENGS:
            for s, v in snap:
                self.emit(eng, lambda e, s=s, v=v: e.wait_ge(s.h, v))

    def chain_op(self, eng, fn):
        s = self.sem("chain")
        prev = s.val
        s.val += 1
        self.emit(eng, lambda e, s=s, v=prev: e.wait_ge(s.h, v))
        self.emit(eng, lambda e, f=fn, s=s: f(e).then_inc(s.h, 1))

    def chain_dma(self, eng, out, in_, **kw):
        s = self.sem("chain")
        prev = s.val
        s.val += 16
        self.emit(eng, lambda e, s=s, v=prev: e.wait_ge(s.h, v))
        self.emit(eng, lambda e, o=out, i=in_, s=s, kw=kw:
                  e.dma_start(out=o, in_=i, **kw).then_inc(s.h, 16))

    def chain_wait_all(self, eng):
        s = self.sem("chain")
        self.emit(eng, lambda e, s=s, v=s.val: e.wait_ge(s.h, v))


class SAlloc:
    def __init__(self, nc):
        self.nc = nc
        self.regions = {}
        self.cursor = {}

    def region(self, name, base, size):
        self.regions[name] = (base, size)
        self.cursor[name] = base

    def reset(self, name):
        self.cursor[name] = self.regions[name][0]

    def alloc(self, name, reg, shape, dtype, align=64):
        esz = {F32: 4, BF16: 2, I32: 4}[dtype]
        nb = shape[-1] * esz
        cur = (self.cursor[reg] + align - 1) // align * align
        base, size = self.regions[reg]
        if cur + nb > base + size:
            raise MemoryError(
                f"SBUF region {reg} overflow: {name} needs {nb}B at rel {cur - base}, size {size}")
        self.cursor[reg] = cur + nb
        return self.nc.alloc_sbuf_tensor_at(name, shape, dtype, offset=cur)


def build_program(mode, upto="full", debug=None):
    nc = bass.Bass("TRN2", target_bir_lowering=False)
    stack = ExitStack()
    p = Prog(nc, stack)

    xT = nc.dram_tensor("xT", [H, CH], F32, kind="ExternalInput")
    wqkvT = nc.dram_tensor("wqkvT", [H, 3 * NH * HD], F32, kind="ExternalInput")
    wgT = nc.dram_tensor("wgT", [H, H], F32, kind="ExternalInput")
    woT = nc.dram_tensor("woT", [H, H], F32, kind="ExternalInput")
    normw = nc.dram_tensor("normw", [128, FC], F32, kind="ExternalInput")
    wgateT = nc.dram_tensor("wgateT", [H, I8192], F32, kind="ExternalInput")
    wupT = nc.dram_tensor("wupT", [H, I8192], F32, kind="ExternalInput")
    wdownT = nc.dram_tensor("wdownT", [I8192, H], F32, kind="ExternalInput")
    slope = nc.dram_tensor("slope", [1, NH], F32, kind="ExternalInput")
    h128 = nc.dram_tensor("h128", [128, 128], F32, kind="ExternalInput")
    mtA = nc.dram_tensor("mtA", [128, 256], F32, kind="ExternalInput")
    mtB = nc.dram_tensor("mtB", [128, 256], F32, kind="ExternalInput")
    arange1 = nc.dram_tensor("arange1", [1, 256], F32, kind="ExternalInput")
    revrow = nc.dram_tensor("revrow", [1, 256], F32, kind="ExternalInput")
    onesrow = nc.dram_tensor("onesrow", [1, 128], F32, kind="ExternalInput")
    ident = nc.dram_tensor("ident", [128, 128], F32, kind="ExternalInput")

    outs = {}
    if mode == "v1a":
        outs["S"] = nc.dram_tensor("S", [NH * HD, HD], F32, kind="ExternalOutput")
    else:
        outs["outT"] = nc.dram_tensor("outT", [H, CH], F32, kind="ExternalOutput")
    Sg = dvec = xwinT = None
    if mode == "v1b":
        Sg = nc.dram_tensor("Sg", [NCORES * NH * HD, HD], F32, kind="ExternalInput")
        dvec = nc.dram_tensor("dvec", [1, NCORES], F32, kind="ExternalInput")
    if mode == "fused":
        xwinT = nc.dram_tensor("xwinT", [H, WARM_NB * BLK], F32, kind="ExternalInput")
    gate_sp = nc.dram_tensor("gate_sp", [H, CH], BF16)
    x2_sp = nc.dram_tensor("x2_sp", [H, CH], F32)
    mid_sp = nc.dram_tensor("mid_sp", [I8192, CH], BF16)
    wdt_sp = nc.dram_tensor("wdt_sp", [I8192, H], BF16)

    if debug:
        dbg = nc.dram_tensor("dbg", [128, FC * CH], F32, kind="ExternalOutput")
        outs["dbg"] = dbg

    psf = [nc.alloc_psum_tensor(f"psf{i}", [128, 512], F32) for i in range(7)]
    pst = nc.alloc_psum_tensor("pst", [128, 1024], BF16)

    sa = SAlloc(nc)
    SB0 = (nc.sbuf_base + 63) // 64 * 64
    SBTOP = nc.sbuf_top
    cur = SB0
    for nm, sz in (("CONST", 8 * 1024), ("ROWS", 24 * 1024), ("ATTN", 15 * 1024),
                   ("A", 64 * 1024), ("B", 64 * 1024)):
        sa.region(nm, cur, sz)
        cur += sz
    sa.region("W", cur, SBTOP - cur)

    ones_col_bf = sa.alloc("ones_col_bf", "CONST", [128, 1], BF16)
    ones_col_f = sa.alloc("ones_col_f", "CONST", [128, 1], F32)
    ones_row_f = sa.alloc("ones_row_f", "CONST", [1, 128], F32)
    id_bf = sa.alloc("id_bf", "CONST", [128, 128], BF16)
    id_f = sa.alloc("id_f", "CONST", [128, 128], F32)
    h128_bf = sa.alloc("h128_bf", "CONST", [128, 128], BF16)
    slope_s = sa.alloc("slope_s", "CONST", [1, NH], F32)
    slopeB = sa.alloc("slopeB", "CONST", [128, NH], F32)
    lamB = sa.alloc("lamB", "CONST", [128, NH], F32)
    normw_s = sa.alloc("normw_s", "CONST", [128, FC], F32)
    mt_s = sa.alloc("mt_s", "CONST", [128, 2 * 256], F32)
    ar_s = sa.alloc("ar_s", "CONST", [1, 256], F32)
    rv_s = sa.alloc("rv_s", "CONST", [1, 256], F32)
    rnd_off = sa.alloc("rnd_off", "CONST", [128, 2], F32)
    tmp_i32c = sa.alloc("tmp_i32c", "CONST", [128, 2], I32)
    wsrow = sa.alloc("wsrow", "CONST", [1, 8], F32)
    wscol = sa.alloc("wscol", "CONST", [128, 4], F32)
    thrcol = sa.alloc("thrcol", "CONST", [128, 8], F32)
    dv_s = sa.alloc("dv_s", "CONST", [1, NCORES], F32)
    wB = sa.alloc("wB", "CONST", [128, NH], F32)

    rowA = sa.alloc("rowA", "ROWS", [1, CH], F32)
    rowB_ = sa.alloc("rowB_", "ROWS", [1, CH], F32)
    rowC = sa.alloc("rowC", "ROWS", [1, CH], F32)

    kv_all = sa.alloc("kv_all", "ATTN", [128, NH * 128], F32)
    kv_bf = [sa.alloc(f"kv_bf{i}", "ATTN", [128, 128], BF16) for i in range(2)]
    qd_t = sa.alloc("qd_t", "ATTN", [128, 256], BF16)
    kd_t = sa.alloc("kd_t", "ATTN", [128, 256], BF16)
    vT_s = sa.alloc("vT_s", "ATTN", [128, 256], BF16)
    kdT_s = sa.alloc("kdT_s", "ATTN", [128, 256], BF16)
    qkd_s = sa.alloc("qkd_s", "ATTN", [128, 512], BF16)
    qdecB_h = sa.alloc("qdecB_h", "ATTN", [128, 256], BF16)
    kdecB_h = sa.alloc("kdecB_h", "ATTN", [128, 256], BF16)
    ddT_h = sa.alloc("ddT_h", "ATTN", [128, 512], BF16)
    ddscr = sa.alloc("ddscr", "ATTN", [128, 256], F32)
    sa.reset("ATTN")
    _attn_hold = sa.alloc("_attn_hold", "ATTN", [128, NH * 128], F32)
    sa.reset("ATTN")
    ybf_blk = sa.alloc("ybf_blk", "ATTN", [128, FC * 256], BF16)

    lnx = sa.alloc("lnx", "A", [128, FC * CH], BF16)
    sa.reset("A")
    woT_bf = sa.alloc("woT_bf", "A", [128, FC * CH], BF16)
    sa.reset("A")
    xq = sa.alloc("xq", "A", [128, FC * CH], BF16)
    sa.reset("A")
    y4_blk = sa.alloc("y4_blk", "A", [128, FC * 256], F32)
    ybf4 = sa.alloc("ybf4", "A", [128, FC * 256], BF16)
    wdf_par = [sa.alloc(f"wdf_par{i}", "A", [128, 2048], F32) for i in range(2)]
    ternd_a = sa.alloc("ternd_a", "A", [128, 2048], BF16)
    ternd_b = sa.alloc("ternd_b", "A", [128, 2048], BF16)
    wdt_ft = sa.alloc("wdt_ft", "A", [128, OC * 128], BF16)

    muB = sa.alloc("muB", "B", [128, CH], F32)
    rB = sa.alloc("rB", "B", [128, CH], F32)
    sa.reset("B")
    o_acc = sa.alloc("o_acc", "B", [128, NH * CH], BF16)
    sa.reset("B")
    x2st = sa.alloc("x2st", "B", [128, FC * 512], BF16)
    zbuf = sa.alloc("zbuf", "B", [128, FC * 512], BF16)
    zpong = sa.alloc("zpong", "B", [128, FC * 512], BF16)
    d1bc1 = sa.alloc("d1bc1", "B", [128, 512], F32)
    d1bc2 = sa.alloc("d1bc2", "B", [128, 512], F32)
    s1Bt = sa.alloc("s1Bt", "B", [128, 512], F32)
    rcp1B = sa.alloc("rcp1B", "B", [128, CH], F32)
    sa.reset("B")
    bufA = sa.alloc("bufA", "B", [128, OC * 256], BF16)
    bufB = sa.alloc("bufB", "B", [128, OC * 256], BF16)
    sa.reset("B")
    terng = sa.alloc("terng", "B", [128, 2048], BF16)
    ternu = sa.alloc("ternu", "B", [128, 2048], BF16)
    csgB = sa.alloc("csgB", "B", [128, CH], F32)
    csuB = sa.alloc("csuB", "B", [128, CH], F32)
    tb_tern = sa.alloc("tb_tern", "B", [128, 2048], BF16)
    wgu_par = [sa.alloc(f"wgu_par{i}", "B", [128, FC * 128], F32) for i in range(1)] * 2
    wuu_par = [sa.alloc(f"wuu_par{i}", "B", [128, FC * 128], F32) for i in range(1)] * 2
    mtmp1 = sa.alloc("mtmp1", "B", [128, 512], F32)
    mtmp2 = sa.alloc("mtmp2", "B", [128, 512], F32)
    mtmp3 = sa.alloc("mtmp3", "B", [128, 512], F32)
    mid_par = [sa.alloc(f"mid_par{i}", "B", [128, 512], BF16) for i in range(2)]

    xb_f = sa.alloc("xb_f", "W", [128, CH], F32)
    sq_par = [sa.alloc(f"sq_par{i}", "W", [128, 512], BF16) for i in range(2)]
    sgt = sa.alloc("sgt", "W", [128, 128], F32)
    wg_sl = [sa.alloc(f"wg_sl{i}", "W", [128, FC * 128], BF16) for i in range(2)]
    sa.reset("W")
    wslice = sa.alloc("wslice", "W", [128, FC * 384], BF16)
    qkv_h = sa.alloc("qkv_h", "W", [128, 3 * CH], BF16)
    sq_parA = [sa.alloc(f"sq_parA{i}", "W", [128, 512], BF16) for i in range(2)]
    sa.reset("W")
    gate_blk = sa.alloc("gate_blk", "W", [128, FC * 256], BF16)
    p_bf = sa.alloc("p_bf", "W", [128, FC * 256], BF16)
    sq_parE = [sa.alloc(f"sq_parE{i}", "W", [128, 256], BF16) for i in range(2)]
    bcq1 = sa.alloc("bcq1", "W", [128, 256], F32)
    bcq2 = sa.alloc("bcq2", "W", [128, 512], F32)
    ptmp = sa.alloc("ptmp", "W", [128, 256], F32)
    x4_par = [sa.alloc(f"x4_par{i}", "W", [128, 256], F32) for i in range(2)]
    o4_par = [sa.alloc(f"o4_par{i}", "W", [128, 256], F32) for i in range(2)]
    sa.reset("W")
    wstr = [sa.alloc(f"wstr{i}", "W", [128, 2048], F32) for i in range(2)]
    redcols = sa.alloc("redcols", "W", [128, 64], F32)
    qtmp1 = sa.alloc("qtmp1", "W", [128, 512], F32)
    qtmp2 = sa.alloc("qtmp2", "W", [128, 512], F32)
    qi32 = sa.alloc("qi32", "W", [128, 512], I32)
    scol = sa.alloc("scol", "W", [128, 8], F32)
    sq_parD = [sa.alloc(f"sq_parD{i}", "W", [128, 512], BF16) for i in range(2)]
    s2Bt = sa.alloc("s2Bt", "W", [128, 512], F32)
    rcp2Bt = sa.alloc("rcp2Bt", "W", [128, 512], F32)
    csdBt = sa.alloc("csdBt", "W", [128, 256], F32)
    e4bc = sa.alloc("e4bc", "W", [128, 512], F32)

    MAGIC = float(np.float32(12582912.0))

    sem_x = p.sem("sem_x")
    sem_g = p.sem("sem_g")
    sem_pe = p.sem("sem_pe")
    sem_dv = p.sem("sem_dv")
    sem_ac = p.sem("sem_ac")
    sem_pl = p.sem("sem_pl")
    sem_st = p.sem("sem_st")

    p.chain_dma("sync", ones_row_f[:, :], onesrow[:, :])
    p.chain_dma("sync", slope_s[:, :], slope[:, :])
    p.chain_dma("sync", mt_s[:, 0:256], mtA[:, :])
    p.chain_dma("sync", mt_s[:, 256:512], mtB[:, :])
    p.chain_dma("sync", ar_s[:, :], arange1[:, :])
    p.chain_dma("sync", rv_s[:, :], revrow[:, :])
    p.chain_dma("sync", normw_s[:, :], normw[:, :])
    p.chain_dma("sync", id_f[:, :], ident[:, :])
    p.chain_dma("gpsimd", ones_col_bf[:, :], onesrow.ap().rearrange("1 p -> p 1"))
    p.chain_dma("sync", ones_col_f[:, :], onesrow.ap().rearrange("1 p -> p 1"))
    p.chain_dma("gpsimd", id_bf[:, :], ident[:, :])
    p.chain_dma("gpsimd", h128_bf[:, :], h128[:, :])
    p.chain_op("tensor", lambda e: e.matmul(psf[0][:, 0:NH], ones_row_f[:, :],
                                            slope_s[:, :], start=True, stop=True))
    p.chain_op("vector", lambda e: e.tensor_copy(slopeB[:, :], psf[0][:, 0:NH]))
    p.chain_op("vector", lambda e: e.memset(rnd_off[:, 0:1], 0.7))
    p.chain_op("vector", lambda e: e.memset(kv_all[:, 0:NH * 64], 0.0))
    p.chain_op("vector", lambda e: e.tensor_copy(tmp_i32c[:, 0:1], rnd_off[:, 0:1]))
    p.chain_op("vector", lambda e: e.memset(kv_all[:, NH * 64:NH * 128], 0.0))
    p.chain_op("vector", lambda e: e.tensor_copy(rnd_off[:, 0:1], tmp_i32c[:, 0:1]))
    p.chain_op("scalar", lambda e: e.activation(rowA[0:1, 0:NH], slope_s[:, :],
                                                AF.Exp, scale=-float(BLK)))
    p.chain_op("vector", lambda e: e.tensor_scalar(rnd_off[:, 0:1], rnd_off[:, 0:1],
                                                   -1.0, 1.0, ALU.mult, ALU.add))
    p.chain_op("tensor", lambda e: e.matmul(psf[1][:, 0:NH], ones_row_f[:, :],
                                            rowA[0:1, 0:NH], start=True, stop=True))
    p.chain_op("vector", lambda e: e.tensor_scalar(rnd_off[:, 1:2], rnd_off[:, 0:1],
                                                   -0.5, None, ALU.mult))
    p.chain_op("vector", lambda e: e.tensor_copy(lamB[:, :], psf[1][:, 0:NH]))
    if mode == "v1b":
        p.chain_dma("sync", dv_s[:, :], dvec[:, :])
        for cp in range(NCORES):
            p.chain_op("vector", lambda e, cp=cp: e.tensor_scalar(
                rowA[0:1, 0:NH], slope_s[:, :], dv_s[0:1, cp:cp + 1], None, ALU.mult))
            p.chain_op("scalar", lambda e: e.activation(
                rowA[0:1, 256:256 + NH], rowA[0:1, 0:NH], AF.Exp, scale=-2048.0))
            p.chain_op("tensor", lambda e: e.matmul(
                psf[0][:, 0:NH], ones_row_f[:, :], rowA[0:1, 256:256 + NH],
                start=True, stop=True))
            p.chain_op("vector", lambda e: e.tensor_copy(wB[:, :], psf[0][:, 0:NH]))
            for h in range(NH):
                p.chain_dma("sync", sgt[:, :],
                            Sg[cp * NH * HD + h * HD:cp * NH * HD + (h + 1) * HD, :])
                p.chain_op("vector", lambda e, h=h: e.scalar_tensor_tensor(
                    kv_all[:, h * 128:(h + 1) * 128], sgt[:, :], wB[:, h:h + 1],
                    kv_all[:, h * 128:(h + 1) * 128], ALU.mult, ALU.add))
    for eng in Prog.ENGS:
        p.chain_wait_all(eng)

    def emit_ln_sweep(x_src, lnx_dst, tokens, eps, reuse_staged=False):
        ngr = tokens // 512
        for fc in range(FC):
            p.dma("gpsimd", lnx_dst[:, fc * tokens:(fc + 1) * tokens],
                  x_src(fc), sem_g)
        p.wait("scalar", sem_g)
        p.wait("tensor", sem_g)
        for g in range(ngr):
            bs = (2 * g) % 6
            p.wait("tensor", sem_dv)
            pe_base = sem_pe.val
            for fc in range(FC):
                if fc >= 2:
                    p.wait("scalar", sem_pe, pe_base + fc - 1)
                p.op("scalar", lambda e, g=g, fc=fc, t=tokens: e.activation(
                    sq_par[fc % 2][:, :],
                    lnx_dst[:, fc * t + g * 512:fc * t + (g + 1) * 512],
                    AF.Square), sem_ac)
                p.wait("tensor", sem_ac)
                p.op("tensor", lambda e, fc=fc, bs=bs: e.matmul(
                    psf[bs + 1][0:1, :], ones_col_bf[:, :], sq_par[fc % 2][:, :],
                    start=(fc == 0), stop=(fc == FC - 1)), sem_pe)
            for fc in range(FC):
                p.op("tensor", lambda e, g=g, fc=fc, bs=bs, t=tokens: e.matmul(
                    psf[bs][0:1, :], ones_col_bf[:, :],
                    lnx_dst[:, fc * t + g * 512:fc * t + (g + 1) * 512],
                    start=(fc == 0), stop=(fc == FC - 1)),
                    sem=(sem_pe if fc == FC - 1 else None))
            p.wait("vector", sem_pe)
            p.op("vector", lambda e, g=g, bs=bs: e.tensor_scalar(
                rowA[0:1, g * 512:(g + 1) * 512], psf[bs][0:1, :],
                1.0 / H, None, ALU.mult))
            p.op("vector", lambda e, g=g, bs=bs: e.tensor_scalar(
                rowB_[0:1, g * 512:(g + 1) * 512], psf[bs + 1][0:1, :],
                1.0 / H, None, ALU.mult), sem_dv)
        p.op("vector", lambda e, t=tokens: e.tensor_tensor(
            rowC[0:1, 0:t], rowA[0:1, 0:t], rowA[0:1, 0:t], ALU.mult))
        p.op("vector", lambda e, t=tokens: e.tensor_tensor(
            rowB_[0:1, 0:t], rowB_[0:1, 0:t], rowC[0:1, 0:t], ALU.subtract))
        p.op("vector", lambda e, t=tokens, eps=eps: e.tensor_scalar(
            rowB_[0:1, 0:t], rowB_[0:1, 0:t], float(eps), None, ALU.add), sem_dv)
        p.wait("scalar", sem_dv)
        p.op("scalar", lambda e, t=tokens: e.activation(
            rowB_[0:1, 0:t], rowB_[0:1, 0:t], AF.Ln), sem_ac)
        p.op("scalar", lambda e, t=tokens: e.activation(
            rowC[0:1, 0:t], rowB_[0:1, 0:t], AF.Exp, scale=-0.5), sem_ac)
        p.wait("tensor", sem_ac)
        for g in range(ngr):
            p.wait("tensor", sem_dv)
            p.op("tensor", lambda e, g=g: e.matmul(
                psf[0][:, :], ones_row_f[:, :], rowA[0:1, g * 512:(g + 1) * 512],
                start=True, stop=True))
            p.op("tensor", lambda e, g=g: e.matmul(
                psf[1][:, :], ones_row_f[:, :], rowC[0:1, g * 512:(g + 1) * 512],
                start=True, stop=True), sem_pe)
            p.wait("vector", sem_pe)
            p.op("vector", lambda e, g=g: e.tensor_copy(
                muB[:, g * 512:(g + 1) * 512], psf[0][:, :]))
            p.op("vector", lambda e, g=g: e.tensor_copy(
                rB[:, g * 512:(g + 1) * 512], psf[1][:, :]), sem_dv)
        p.wait("tensor", sem_dv)
        for fc in range(FC):
            if reuse_staged:
                p.op("vector", lambda e, fc=fc, t=tokens, dst=lnx_dst: e.tensor_tensor(
                    xb_f[:, 0:t], dst[:, fc * t:(fc + 1) * t], muB[:, 0:t],
                    ALU.subtract))
            else:
                p.wait("sync", sem_dv)
                p.dma("sync", xb_f[:, 0:tokens], x_src(fc), sem_x)
                p.wait("vector", sem_x)
                p.op("vector", lambda e, t=tokens: e.tensor_tensor(
                    xb_f[:, 0:t], xb_f[:, 0:t], muB[:, 0:t], ALU.subtract))
            p.op("vector", lambda e, fc=fc, t=tokens, dst=lnx_dst: e.tensor_tensor(
                dst[:, fc * t:(fc + 1) * t], xb_f[:, 0:t], rB[:, 0:t], ALU.mult), sem_dv)

    proj_pe = [0]

    def emit_head(h, kv_only, jrange=None, tgs=None):
        if jrange is None:
            jrange = range(NB)
        if tgs is None:
            tgs = range(4)
        p.wait("vector", sem_pe)
        if not kv_only:
            for nt in range(2):
                p.wait("vector", sem_ac)
                p.op("vector", lambda e, h=h, nt=nt: e.tensor_scalar(
                    ddscr[:, :], mt_s[:, nt * 256:(nt + 1) * 256],
                    slopeB[:, h:h + 1], None, ALU.mult), sem_dv)
                p.wait("scalar", sem_dv)
                p.op("scalar", lambda e, nt=nt: e.activation(
                    ddT_h[:, nt * 256:(nt + 1) * 256], ddscr[:, :],
                    AF.Exp, scale=-1.0), sem_ac)
            p.wait("vector", sem_ac)
        for src, dst in (((ar_s, qdecB_h),) if not kv_only else ()) + ((rv_s, kdecB_h),):
            p.wait("vector", sem_ac)
            p.op("vector", lambda e, src=src, h=h: e.tensor_scalar(
                rowA[0:1, 0:256], src[:, :], slopeB[0:1, h:h + 1], None, ALU.mult), sem_dv)
            p.wait("scalar", sem_dv)
            p.wait("scalar", sem_pe)
            p.op("scalar", lambda e: e.activation(
                rowA[0:1, 256:512], rowA[0:1, 0:256], AF.Exp, scale=-1.0), sem_ac)
            p.wait("tensor", sem_ac)
            p.op("tensor", lambda e: e.matmul(
                psf[6][:, 0:256], ones_row_f[:, :], rowA[0:1, 256:512],
                start=True, stop=True), sem_pe)
            p.wait("vector", sem_pe)
            p.op("vector", lambda e, dst=dst: e.tensor_copy(dst[:, :], psf[6][:, 0:256]),
                 sem_dv)
        p.wait("gpsimd", sem_pe, proj_pe[0])
        wwid = 256 if kv_only else 384
        woff = 384 * h + (128 if kv_only else 0)
        p.dma("gpsimd", wslice[:, 0:FC * wwid],
              bass.AP(wqkvT, woff,
                      [[3 * NH * HD, 128], [128 * 3 * NH * HD, FC], [1, wwid]]), sem_g)
        p.wait("tensor", sem_g)
        ots = (1, 2) if kv_only else (0, 1, 2)
        for oi, ot in enumerate(ots):
            for tg in tgs:
                bank = (oi * 4 + tg) % 2
                p.wait("tensor", sem_ac)
                for fc in range(FC):
                    off = fc * wwid + (ot - (1 if kv_only else 0)) * 128
                    p.op("tensor", lambda e, off=off, tg=tg, fc=fc, bank=bank: e.matmul(
                        psf[bank][:, :], wslice[:, off:off + 128],
                        lnx[:, fc * CH + tg * 512:fc * CH + (tg + 1) * 512],
                        start=(fc == 0), stop=(fc == FC - 1)),
                        sem=(sem_pe if fc == FC - 1 else None))
                p.wait("scalar", sem_pe)
                p.op("scalar", lambda e, ot=ot, tg=tg, bank=bank: e.activation(
                    qkv_h[:, ot * CH + tg * 512:ot * CH + (tg + 1) * 512],
                    psf[bank][:, :], AF.Silu), sem_ac)
        proj_pe[0] = sem_pe.val
        p.wait("vector", sem_ac)
        p.op("vector", lambda e, h=h: e.tensor_copy(
            kv_bf[0][:, :], kv_all[:, h * 128:(h + 1) * 128]), sem_dv)
        p.wait("tensor", sem_dv)
        for j in jrange:
            q0, k0, v0 = (0 * CH + j * 256, 1 * CH + j * 256, 2 * CH + j * 256)
            qkb = 2 + (j % 2)
            oab = 4 + (j % 2)
            if not kv_only:
                p.op("vector", lambda e, q0=q0: e.tensor_tensor(
                    qd_t[:, :], qkv_h[:, q0:q0 + 256], qdecB_h[:, :], ALU.mult), sem_dv)
            p.op("vector", lambda e, k0=k0: e.tensor_tensor(
                kd_t[:, :], qkv_h[:, k0:k0 + 256], kdecB_h[:, :], ALU.mult), sem_dv)
            p.wait("tensor", sem_dv)
            for nt in range(2):
                p.op("tensor", lambda e, v0=v0, nt=nt: e.transpose(
                    pst[:, nt * 128:(nt + 1) * 128],
                    qkv_h[:, v0 + nt * 128:v0 + (nt + 1) * 128], id_bf[:, :]))
            for nt in range(2):
                p.op("tensor", lambda e, nt=nt: e.transpose(
                    pst[:, 256 + nt * 128:256 + (nt + 1) * 128],
                    kd_t[:, nt * 128:(nt + 1) * 128], id_bf[:, :]),
                    sem=(sem_pe if nt == 1 else None))
            if not kv_only:
                for nt in range(2):
                    p.op("tensor", lambda e, k0=k0, q0=q0, nt=nt, qkb=qkb: e.matmul(
                        psf[qkb][:, nt * 256:(nt + 1) * 256],
                        qkv_h[:, k0 + nt * 128:k0 + (nt + 1) * 128],
                        qkv_h[:, q0:q0 + 256], start=True, stop=True),
                        sem=(sem_pe if nt == 1 else None))
            p.wait("vector", sem_pe)
            p.op("vector", lambda e: e.tensor_copy(vT_s[:, :], pst[:, 0:256]))
            p.op("vector", lambda e: e.tensor_copy(kdT_s[:, :], pst[:, 256:512]), sem_dv)
            if not kv_only:
                p.op("vector", lambda e, qkb=qkb: e.tensor_tensor(
                    qkd_s[:, :], psf[qkb][:, :], ddT_h[:, :], ALU.mult), sem_dv)
            p.wait("tensor", sem_dv)
            if not kv_only:
                p.op("tensor", lambda e, j=j, oab=oab: e.matmul(
                    psf[oab][:, 0:256], kv_bf[j % 2][:, :], qd_t[:, :],
                    start=True, stop=False))
                for nt in range(2):
                    p.op("tensor", lambda e, nt=nt, oab=oab: e.matmul(
                        psf[oab][:, 0:256], vT_s[:, nt * 128:(nt + 1) * 128],
                        qkd_s[:, nt * 256:(nt + 1) * 256],
                        start=False, stop=(nt == 1)))
            for nt in range(2):
                p.op("tensor", lambda e, nt=nt, oab=oab: e.matmul(
                    psf[oab][:, 256:384], kdT_s[:, nt * 128:(nt + 1) * 128],
                    vT_s[:, nt * 128:(nt + 1) * 128],
                    start=(nt == 0), stop=(nt == 1)), sem=(sem_pe if nt == 1 else None))
            p.wait("vector", sem_pe)
            if not kv_only:
                p.op("vector", lambda e, h=h, j=j, oab=oab: e.tensor_copy(
                    o_acc[:, h * CH + j * 256:h * CH + (j + 1) * 256],
                    psf[oab][:, 0:256]))
            p.op("vector", lambda e, h=h, oab=oab: e.scalar_tensor_tensor(
                kv_all[:, h * 128:(h + 1) * 128], kv_all[:, h * 128:(h + 1) * 128],
                lamB[:, h:h + 1], psf[oab][:, 256:384], ALU.mult, ALU.add))
            p.op("vector", lambda e, h=h, j=j: e.tensor_copy(
                kv_bf[(j + 1) % 2][:, :], kv_all[:, h * 128:(h + 1) * 128]), sem_dv)
            p.wait("tensor", sem_dv)


    if mode == "fused":
        for slab in range(WARM_NB // NB):
            heads = [h for h in range(NH)
                     if (slab + 1) * NB > WARM_NB - WARM_DEPTH[h]]
            if not heads:
                continue
            emit_ln_sweep(lambda fc, slab=slab: xwinT[
                fc * 128:(fc + 1) * 128, slab * CH:(slab + 1) * CH],
                lnx, CH, EPS_LN, reuse_staged=True)
            p.barrier()
            for h in heads:
                jr = [j for j in range(NB)
                      if slab * NB + j >= WARM_NB - WARM_DEPTH[h]]
                emit_head(h, kv_only=True, jrange=jr,
                          tgs=sorted({j // 2 for j in jr}))
            p.barrier()
    emit_ln_sweep(lambda fc: xT[fc * 128:(fc + 1) * 128, :], lnx, CH, EPS_LN)
    p.barrier()
    if upto == "ln":
        if debug == "lnx":
            for fc in range(FC):
                p.chain_op("vector", lambda e, fc=fc: e.tensor_copy(
                    xb_f[:, 0:CH], lnx[:, fc * CH:(fc + 1) * CH]))
                p.chain_dma("sync", outs["dbg"][:, fc * CH:(fc + 1) * CH], xb_f[:, 0:CH])
        return _finish(nc, p, stack)

    if mode != "v1a":
        for og in range(FC):
            b = og % 2
            p.wait("gpsimd", sem_pe)
            p.dma("gpsimd", wg_sl[b][:, :],
                  bass.AP(wgT, og * 128, [[H, 128], [128 * H, FC], [1, 128]]), sem_g)
            p.wait("tensor", sem_g)
            for tg in range(4):
                bank = 2 + (tg % 2)
                p.wait("tensor", sem_ac)
                for fc in range(FC):
                    p.op("tensor", lambda e, b=b, tg=tg, fc=fc, bank=bank: e.matmul(
                        psf[bank][:, :], wg_sl[b][:, fc * 128:(fc + 1) * 128],
                        lnx[:, fc * CH + tg * 512:fc * CH + (tg + 1) * 512],
                        start=(fc == 0), stop=(fc == FC - 1)),
                        sem=(sem_pe if fc == FC - 1 else None))
                p.wait("scalar", sem_pe)
                p.wait("scalar", sem_g)
                p.op("scalar", lambda e, tg=tg, bank=bank: e.activation(
                    sq_par[tg % 2][:, :], psf[bank][:, :], AF.Sigmoid), sem_ac)
                p.wait("gpsimd", sem_ac)
                p.dma("gpsimd", gate_sp[og * 128:(og + 1) * 128, tg * 512:(tg + 1) * 512],
                      sq_par[tg % 2][:, :], sem_g)
        p.barrier()
    if upto == "gate":
        if debug == "gate":
            for fc in range(FC):
                p.chain_dma("sync", sq_par[0][:, :], gate_sp[fc * 128:(fc + 1) * 128, 0:512])
                p.chain_op("vector", lambda e: e.tensor_copy(qtmp1[:, :], sq_par[0][:, :]))
                p.chain_dma("sync", outs["dbg"][:, fc * 512:(fc + 1) * 512], qtmp1[:, :])
        return _finish(nc, p, stack)

    for h in range(NH):
        emit_head(h, kv_only=(mode == "v1a"))
    p.barrier()
    if mode == "v1a":
        for h in range(NH):
            p.dma("sync", outs["S"][h * HD:(h + 1) * HD, :],
                  kv_all[:, h * 128:(h + 1) * 128], sem_st)
        p.wait("sync", sem_st)
        return _finish(nc, p, stack)

    if upto == "attn":
        if debug == "oacc":
            for hh in range(NH):
                p.chain_op("vector", lambda e, hh=hh: e.tensor_copy(
                    wstr[0][:, :], o_acc[:, hh * CH:(hh + 1) * CH]))
                p.chain_dma("sync", outs["dbg"][:, hh * CH:(hh + 1) * CH], wstr[0][:, :])
        return _finish(nc, p, stack)

    for og in range(FC):
        p.dma("gpsimd", woT_bf[:, og * CH:(og + 1) * CH],
              bass.AP(woT, og * 128 * H, [[H, 128], [128, FC], [1, 128]]), sem_g)
    p.wait("tensor", sem_g)
    p.wait("scalar", sem_g)
    p.wait("vector", sem_g)
    for j in range(NB):
        p.wait("gpsimd", sem_dv)
        p.dma("gpsimd", gate_blk[:, :],
              bass.AP(gate_sp, j * 256, [[CH, 128], [128 * CH, FC], [1, 256]]), sem_g)
        pe_base = sem_pe.val
        for hh in range(NH):
            if hh >= 2:
                p.wait("scalar", sem_pe, pe_base + hh - 1)
            p.op("scalar", lambda e, hh=hh, j=j: e.activation(
                sq_parE[hh % 2][:, :],
                o_acc[:, hh * CH + j * 256:hh * CH + (j + 1) * 256], AF.Square), sem_ac)
            p.wait("tensor", sem_ac)
            p.op("tensor", lambda e, hh=hh: e.matmul(
                psf[6][0:1, 0:256], ones_col_bf[:, :], sq_parE[hh % 2][:, :],
                start=(hh == 0), stop=(hh == NH - 1)), sem_pe)
        p.wait("vector", sem_pe)
        p.op("vector", lambda e: e.tensor_scalar(
            rowA[0:1, 0:256], psf[6][0:1, 0:256], 1.0 / H, float(EPS_ATTN),
            ALU.mult, ALU.add), sem_dv)
        p.wait("scalar", sem_dv)
        p.op("scalar", lambda e: e.activation(rowA[0:1, 0:256], rowA[0:1, 0:256], AF.Ln))
        p.op("scalar", lambda e: e.activation(
            rowA[0:1, 256:512], rowA[0:1, 0:256], AF.Exp, scale=-0.5), sem_ac)
        p.wait("tensor", sem_ac)
        p.op("tensor", lambda e: e.matmul(
            psf[6][:, 256:512], ones_row_f[:, :], rowA[0:1, 256:512],
            start=True, stop=True), sem_pe)
        p.wait("vector", sem_pe)
        p.op("vector", lambda e: e.tensor_copy(bcq1[:, :], psf[6][:, 256:512]), sem_dv)
        p.wait("vector", sem_g)
        for hh in range(NH):
            p.op("vector", lambda e, hh=hh, j=j: e.scalar_tensor_tensor(
                ptmp[:, :], o_acc[:, hh * CH + j * 256:hh * CH + (j + 1) * 256],
                normw_s[:, hh:hh + 1], bcq1[:, :], ALU.mult, ALU.mult))
            p.op("vector", lambda e, hh=hh: e.tensor_tensor(
                p_bf[:, hh * 256:(hh + 1) * 256], ptmp[:, :],
                gate_blk[:, hh * 256:(hh + 1) * 256], ALU.mult), sem_dv)
        p.wait("tensor", sem_dv)
        for fc in range(FC):
            bank = fc % 2
            p.wait("tensor", sem_dv)
            for og in range(FC):
                p.op("tensor", lambda e, og=og, fc=fc, bank=bank: e.matmul(
                    psf[bank][:, 0:256],
                    woT_bf[:, og * CH + fc * 128:og * CH + (fc + 1) * 128],
                    p_bf[:, og * 256:(og + 1) * 256],
                    start=(og == 0), stop=(og == FC - 1)),
                    sem=(sem_pe if og == FC - 1 else None))
            p.wait("vector", sem_pe)
            p.op("vector", lambda e, fc=fc, bank=bank: e.tensor_copy(
                ybf_blk[:, fc * 256:(fc + 1) * 256], psf[bank][:, 0:256]), sem_dv)
        p.wait("scalar", sem_dv)
        p.wait("tensor", sem_dv)
        pe2 = sem_pe.val
        for fc in range(FC):
            if fc >= 2:
                p.wait("scalar", sem_pe, pe2 + fc - 1)
            p.op("scalar", lambda e, fc=fc: e.activation(
                sq_parE[fc % 2][:, :], ybf_blk[:, fc * 256:(fc + 1) * 256],
                AF.Square), sem_ac)
            p.wait("tensor", sem_ac)
            p.op("tensor", lambda e, fc=fc: e.matmul(
                psf[2][0:1, 0:256], ones_col_bf[:, :], sq_parE[fc % 2][:, :],
                start=(fc == 0), stop=(fc == FC - 1)), sem_pe)
        for fc in range(FC):
            p.op("tensor", lambda e, fc=fc: e.matmul(
                psf[3][0:1, 0:256], ones_col_bf[:, :],
                ybf_blk[:, fc * 256:(fc + 1) * 256],
                start=(fc == 0), stop=(fc == FC - 1)),
                sem=(sem_pe if fc == FC - 1 else None))
        p.wait("vector", sem_pe)
        p.op("vector", lambda e: e.tensor_scalar(
            rowA[0:1, 0:256], psf[3][0:1, 0:256], 1.0 / H, None, ALU.mult))
        p.op("vector", lambda e: e.tensor_scalar(
            rowB_[0:1, 0:256], psf[2][0:1, 0:256], 1.0 / H, None, ALU.mult))
        p.op("vector", lambda e: e.tensor_tensor(
            rowC[0:1, 0:256], rowA[0:1, 0:256], rowA[0:1, 0:256], ALU.mult))
        p.op("vector", lambda e: e.tensor_tensor(
            rowB_[0:1, 0:256], rowB_[0:1, 0:256], rowC[0:1, 0:256], ALU.subtract))
        p.op("vector", lambda e: e.tensor_scalar(
            rowB_[0:1, 0:256], rowB_[0:1, 0:256], float(EPS_LN), None, ALU.add), sem_dv)
        p.wait("scalar", sem_dv)
        p.op("scalar", lambda e: e.activation(rowB_[0:1, 0:256], rowB_[0:1, 0:256], AF.Ln))
        p.op("scalar", lambda e: e.activation(
            rowC[0:1, 0:256], rowB_[0:1, 0:256], AF.Exp, scale=-0.5), sem_ac)
        p.wait("tensor", sem_ac)
        p.wait("tensor", sem_dv)
        p.op("tensor", lambda e: e.matmul(
            psf[4][:, 0:256], ones_row_f[:, :], rowA[0:1, 0:256], start=True, stop=True))
        p.op("tensor", lambda e: e.matmul(
            psf[4][:, 256:512], ones_row_f[:, :], rowC[0:1, 0:256],
            start=True, stop=True), sem_pe)
        p.wait("vector", sem_pe)
        p.op("vector", lambda e: e.tensor_copy(bcq2[:, 0:256], psf[4][:, 0:256]))
        p.op("vector", lambda e: e.tensor_copy(bcq2[:, 256:512], psf[4][:, 256:512]), sem_dv)
        for fc in range(FC):
            bpar = fc % 2
            p.wait("sync", sem_dv)
            p.dma("sync", x4_par[bpar][:, :],
                  xT[fc * 128:(fc + 1) * 128, j * 256:(j + 1) * 256], sem_x)
            p.wait("vector", sem_x)
            p.wait("vector", sem_g)
            p.op("vector", lambda e, fc=fc, bpar=bpar: e.tensor_tensor(
                o4_par[bpar][:, :], ybf_blk[:, fc * 256:(fc + 1) * 256],
                bcq2[:, 0:256], ALU.subtract))
            p.op("vector", lambda e, bpar=bpar: e.tensor_tensor(
                o4_par[bpar][:, :], o4_par[bpar][:, :], bcq2[:, 256:512], ALU.mult))
            p.op("vector", lambda e, bpar=bpar: e.tensor_tensor(
                o4_par[bpar][:, :], o4_par[bpar][:, :], x4_par[bpar][:, :],
                ALU.add), sem_dv)
            p.wait("gpsimd", sem_dv)
            p.dma("gpsimd", x2_sp[fc * 128:(fc + 1) * 128, j * 256:(j + 1) * 256],
                  o4_par[bpar][:, :], sem_g)
    p.barrier()
    if upto == "x2":
        if debug == "x2":
            for fc in range(FC):
                p.chain_dma("sync", outs["dbg"][:, fc * CH:(fc + 1) * CH],
                            x2_sp[fc * 128:(fc + 1) * 128, :])
        if debug == "phe":
            off = 4096
            for nm, buf in (("p", p_bf), ("g", gate_blk), ("y", ybf_blk)):
                for hh in range(2):
                    p.chain_op("vector", lambda e, buf=buf, hh=hh: e.tensor_copy(
                        wstr[0][:, :], buf[:, hh * 2048:(hh + 1) * 2048]))
                    p.chain_dma("sync", outs["dbg"][:, off:off + 2048], wstr[0][:, :])
                    off += 2048
            p.chain_dma("sync", outs["dbg"][:, off:off + 256], bcq1[:, :])
            p.chain_dma("sync", outs["dbg"][:, off + 256:off + 768], bcq2[:, :])
        return _finish(nc, p, stack)

    env = dict(locals())
    emit_ffn(p, env)
    p.barrier()
    if upto in ("xq", "mid"):
        if debug == "xq":
            for fc in range(FC):
                p.chain_op("vector", lambda e, fc=fc: e.tensor_copy(
                    wstr[0][:, :], xq[:, fc * CH:(fc + 1) * CH]))
                p.chain_dma("sync", outs["dbg"][:, fc * CH:(fc + 1) * CH], wstr[0][:, :])
        if debug == "bfly":
            for q in range(4):
                p.chain_op("vector", lambda e, q=q: e.tensor_copy(
                    wstr[0][:, :], zbuf[:, q * 2048:(q + 1) * 2048]))
                p.chain_dma("sync", outs["dbg"][:, q * 2048:(q + 1) * 2048], wstr[0][:, :])
            for q in range(4):
                p.chain_op("vector", lambda e, q=q: e.tensor_copy(
                    wstr[0][:, :], zpong[:, q * 2048:(q + 1) * 2048]))
                p.chain_dma("sync", outs["dbg"][:, 8192 + q * 2048:8192 + (q + 1) * 2048],
                            wstr[0][:, :])
            p.chain_dma("sync", outs["dbg"][:, 16384:16896], s1Bt[:, :])
        if debug == "mid":
            for oc in range(FC):
                p.chain_dma("sync", sq_parD[0][:, :], mid_sp[oc * 128:(oc + 1) * 128, 0:512])
                p.chain_op("vector", lambda e: e.tensor_copy(qtmp1[:, :], sq_parD[0][:, :]))
                p.chain_dma("sync", outs["dbg"][:, oc * 512:(oc + 1) * 512], qtmp1[:, :])
        if debug == "d2a":
            p.chain_dma("sync", outs["dbg"][:, 0:2048], csgB[:, :])
            p.chain_dma("sync", outs["dbg"][:, 2048:4096], csuB[:, :])
            for nm, tb, off in (("g", terng, 4096), ("u", ternu, 6144)):
                p.chain_op("vector", lambda e, tb=tb: e.tensor_copy(wstr[0][:, :], tb[:, :]))
                p.chain_dma("sync", outs["dbg"][:, off:off + 2048], wstr[0][:, :])
            for oc in range(FC):
                p.chain_dma("sync", sq_parD[0][:, :], mid_sp[oc * 128:(oc + 1) * 128, 0:512])
                p.chain_op("vector", lambda e: e.tensor_copy(qtmp1[:, :], sq_parD[0][:, :]))
                p.chain_dma("sync", outs["dbg"][:, 8192 + oc * 512:8192 + (oc + 1) * 512],
                            qtmp1[:, :])
            p.chain_dma("sync", outs["dbg"][:, 16384:18432], rcp1B[:, :])
            p.chain_dma("sync", outs["dbg"][:, 18432:20480], wgu_par[0][:, :])
            for fc4 in range(4):
                p.chain_op("vector", lambda e, fc4=fc4: e.tensor_copy(
                    wstr[0][:, 0:512], xq[:, fc4 * 4 * CH:fc4 * 4 * CH + 512]))
                p.chain_dma("sync", outs["dbg"][:, 20480 + fc4 * 512:20480 + (fc4 + 1) * 512],
                            wstr[0][:, 0:512])
        if debug == "ws":
            p.chain_dma("sync", outs["dbg"][0:1, 0:8], wsrow[0:1, 0:8])
    return _finish(nc, p, stack)


def emit_ffn(p, L):
    g = L
    upto = g["upto"]
    psf, pst = g["psf"], g["pst"]
    sem_x, sem_g, sem_pe, sem_dv, sem_ac, sem_pl, sem_st = (
        g["sem_x"], g["sem_g"], g["sem_pe"], g["sem_dv"], g["sem_ac"], g["sem_pl"],
        g["sem_st"])
    ones_col_bf, ones_row_f, id_f, h128_bf = (
        g["ones_col_bf"], g["ones_row_f"], g["id_f"], g["h128_bf"])
    ones_col_f = g["ones_col_f"]
    id_bf, pst_ = g["id_bf"], g["pst"]
    rowA, rowB_, rowC = g["rowA"], g["rowB_"], g["rowC"]
    wstr, redcols = g["wstr"], g["redcols"]
    wsrow, wscol, thrcol, rnd_off = g["wsrow"], g["wscol"], g["thrcol"], g["rnd_off"]
    wgateT, wupT, wdownT, x2_sp, mid_sp, wdt_sp = (
        g["wgateT"], g["wupT"], g["wdownT"], g["x2_sp"], g["mid_sp"], g["wdt_sp"])
    x2st, zbuf, zpong, d1bc1, d1bc2, s1Bt = (
        g["x2st"], g["zbuf"], g["zpong"], g["d1bc1"], g["d1bc2"], g["s1Bt"])
    xq, qtmp1, qtmp2, qi32, scol = g["xq"], g["qtmp1"], g["qtmp2"], g["qi32"], g["scol"]
    sq_parD = g["sq_parD"]
    rcp1B = g["rcp1B"]
    terng, ternu, tb_tern = g["terng"], g["ternu"], g["tb_tern"]
    csgB, csuB = g["csgB"], g["csuB"]
    wgu_par, wuu_par = g["wgu_par"], g["wuu_par"]
    mtmp1, mtmp2, mtmp3, mid_par = g["mtmp1"], g["mtmp2"], g["mtmp3"], g["mid_par"]
    bufA, bufB = g["bufA"], g["bufB"]
    y4_blk, ybf4 = g["y4_blk"], g["ybf4"]
    wdf_par, ternd_a, ternd_b, wdt_ft = (
        g["wdf_par"], g["ternd_a"], g["ternd_b"], g["wdt_ft"])
    s2Bt, rcp2Bt, csdBt, e4bc = g["s2Bt"], g["rcp2Bt"], g["csdBt"], g["e4bc"]
    xT, outs = g["xT"], g["outs"]
    outT = outs.get("outT")

    for mi, wmat in enumerate((wgateT, wupT, wdownT)):
        nrows = H if mi < 2 else I8192
        ncols = I8192 if mi < 2 else H
        nch = (nrows // 128) * (ncols // 2048)
        ci = 0
        dv0 = sem_dv.val
        x0 = sem_x.val
        for rr in range(nrows // 128):
            for cc in range(ncols // 2048):
                b = ci % 2
                p.wait("sync", sem_dv, dv0 + max(0, ci - 1))
                p.dma("sync", wstr[b][:, :],
                      wmat[rr * 128:(rr + 1) * 128, cc * 2048:(cc + 1) * 2048], sem_x)
                p.wait("vector", sem_x, x0 + 16 * (ci + 1))
                p.op("vector", lambda e, b=b, ci=ci: e.tensor_reduce(
                    redcols[:, ci:ci + 1], wstr[b][:, :], AX.X, ALU.add,
                    apply_absolute_value=True), sem_dv)
                ci += 1
        p.op("vector", lambda e: e.tensor_copy(qtmp1[:, 0:128], qtmp1[:, 0:128]))
        p.op("vector", lambda e: e.tensor_copy(qtmp1[:, 0:128], qtmp1[:, 0:128]))
        p.op("vector", lambda e, nch=nch: e.tensor_reduce(
            redcols[:, 0:1], redcols[:, 0:nch], AX.X, ALU.add), sem_dv)
        p.wait("tensor", sem_dv)
        p.op("tensor", lambda e: e.matmul(
            psf[5][0:1, 0:1], redcols[:, 0:1], ones_col_f[:, :],
            start=True, stop=True), sem_pe)
        p.wait("vector", sem_pe)
        p.op("vector", lambda e, mi=mi, nr=nrows, ncl=ncols: e.tensor_scalar(
            wsrow[0:1, mi:mi + 1], psf[5][0:1, 0:1], 1.0 / (nr * ncl), None,
            ALU.mult), sem_dv)
    p.wait("tensor", sem_dv)
    p.op("tensor", lambda e: e.matmul(
        psf[5][:, 0:8], ones_row_f[:, :], wsrow[0:1, 0:8], start=True, stop=True), sem_pe)
    p.wait("vector", sem_pe)
    p.op("vector", lambda e: e.tensor_copy(wscol[:, 0:4], psf[5][:, 0:4]))
    p.op("vector", lambda e: e.tensor_scalar(
        thrcol[:, 0:4], psf[5][:, 0:4], 0.5, None, ALU.mult))
    p.op("vector", lambda e: e.tensor_scalar(
        thrcol[:, 4:8], psf[5][:, 0:4], -0.5, None, ALU.mult), sem_dv)

    def quant_scales(cur, twidth, nchunks, sBdst, rcpdst, rcp_extra=None):
        nfull = 128 * nchunks
        p.op("vector", lambda e: e.tensor_reduce(
            qtmp1[:, 0:twidth],
            bass.AP(cur, 0, [[nchunks * twidth, 128], [1, twidth], [twidth, nchunks]]),
            AX.X, ALU.max))
        p.op("vector", lambda e: e.tensor_reduce(
            qtmp2[:, 0:twidth],
            bass.AP(cur, 0, [[nchunks * twidth, 128], [1, twidth], [twidth, nchunks]]),
            AX.X, ALU.min))
        p.op("vector", lambda e: e.tensor_scalar(
            qtmp2[:, 0:twidth], qtmp2[:, 0:twidth], -1.0, None, ALU.mult))
        p.op("vector", lambda e: e.tensor_tensor(
            qtmp1[:, 0:twidth], qtmp1[:, 0:twidth], qtmp2[:, 0:twidth], ALU.max))
        p.op("vector", lambda e, nfull=nfull: e.tensor_scalar(
            qtmp1[:, 0:twidth], qtmp1[:, 0:twidth], 1.0 / math.sqrt(nfull), None,
            ALU.mult), sem_dv)
        ngq = twidth // 128
        p.op("vector", lambda e, tw=twidth: e.tensor_copy(
            sq_parD[0][:, 0:tw], qtmp1[:, 0:tw]), sem_dv)
        p.wait("tensor", sem_dv)
        for q4 in range(ngq):
            p.op("tensor", lambda e, q4=q4: e.transpose(
                pst[:, q4 * 128:(q4 + 1) * 128],
                sq_parD[0][:, q4 * 128:(q4 + 1) * 128], id_bf[:, :]),
                sem=(sem_pe if q4 == ngq - 1 else None))
        p.wait("vector", sem_pe)
        for q4 in range(ngq):
            p.op("vector", lambda e, q4=q4: e.tensor_reduce(
                scol[:, q4:q4 + 1], pst[:, q4 * 128:(q4 + 1) * 128], AX.X, ALU.max))
        p.op("vector", lambda e: e.tensor_copy(qtmp2[:, 0:128], qtmp2[:, 0:128]))
        p.op("vector", lambda e: e.tensor_copy(qtmp2[:, 0:128], qtmp2[:, 0:128]))
        p.op("vector", lambda e, ngq=ngq: e.tensor_scalar(
            scol[:, 0:ngq], scol[:, 0:ngq], float(EPS_FFN), None, ALU.max))
        p.op("vector", lambda e: e.tensor_copy(qtmp2[:, 0:128], qtmp2[:, 0:128]))
        p.op("vector", lambda e: e.tensor_copy(qtmp2[:, 0:128], qtmp2[:, 0:128]))
        for q4 in range(ngq):
            p.op("vector", lambda e, q4=q4: e.reciprocal(
                scol[:, 4 + q4:5 + q4], scol[:, q4:q4 + 1]))
        p.op("vector", lambda e: e.tensor_copy(qtmp2[:, 0:128], qtmp2[:, 0:128]))
        p.op("vector", lambda e: e.tensor_copy(qtmp2[:, 0:128], qtmp2[:, 0:128]))
        p.op("vector", lambda e, ngq=ngq, nfull=nfull: e.tensor_scalar(
            scol[:, 4:4 + ngq], scol[:, 4:4 + ngq], 127.0 / math.sqrt(nfull), None,
            ALU.mult))
        p.op("vector", lambda e, ngq=ngq: e.tensor_scalar(
            scol[:, 0:ngq], scol[:, 0:ngq], 1.0 / 127.0, None, ALU.mult), sem_dv)
        p.wait("tensor", sem_dv)
        for which, cbase, dst in ((0, 4, "s"), (1, 0, "r")):
            for q4 in range(ngq):
                p.op("tensor", lambda e, q4=q4, cbase=cbase: e.transpose(
                    psf[2][0:1, 0:128],
                    scol[:, cbase + q4:cbase + q4 + 1], id_f[:, :]), sem_pe)
                p.wait("vector", sem_pe)
                p.op("vector", lambda e, q4=q4, which=which: e.tensor_copy(
                    rowA[0:1, 512 + which * 512 + q4 * 128:512 + which * 512 + (q4 + 1) * 128],
                    psf[2][0:1, 0:128]), sem_dv)
                p.wait("tensor", sem_dv)
        p.op("tensor", lambda e, twidth=twidth: e.matmul(
            psf[5][:, 0:twidth], ones_row_f[:, :], rowA[0:1, 512:512 + twidth],
            start=True, stop=True))
        p.op("tensor", lambda e, twidth=twidth: e.matmul(
            psf[6][:, 0:twidth], ones_row_f[:, :], rowA[0:1, 1024:1024 + twidth],
            start=True, stop=True), sem_pe)
        p.wait("vector", sem_pe)
        p.op("vector", lambda e, sBdst=sBdst, twidth=twidth: e.tensor_copy(
            sBdst[:, 0:twidth], psf[5][:, 0:twidth]))
        p.op("vector", lambda e, rcpdst=rcpdst, twidth=twidth: e.tensor_copy(
            rcpdst[:, 0:twidth], psf[6][:, 0:twidth]), sem_dv)

    def emit_quant_slices(cur, nslices, swidth, sB, dst_fn):
        for i in range(nslices):
            p.op("vector", lambda e, i=i, sw=swidth: e.tensor_tensor(
                qtmp1[:, 0:sw], cur(i), sB, ALU.mult))
            p.op("vector", lambda e, sw=swidth: e.tensor_scalar(
                qtmp2[:, 0:sw], qtmp1[:, 0:sw], 0.0, None, ALU.is_ge))
            p.op("vector", lambda e, sw=swidth: e.scalar_tensor_tensor(
                qtmp2[:, 0:sw], qtmp2[:, 0:sw], rnd_off[:, 0:1], qtmp1[:, 0:sw],
                ALU.mult, ALU.add))
            p.op("vector", lambda e, sw=swidth: e.tensor_scalar(
                qtmp2[:, 0:sw], qtmp2[:, 0:sw], rnd_off[0:1, 1:2], None, ALU.add))                 if False else None
            p.op("vector", lambda e, sw=swidth: e.tensor_scalar(
                qtmp2[:, 0:sw], qtmp2[:, 0:sw], rnd_off[:, 1:2], None, ALU.add))
            p.op("vector", lambda e, sw=swidth: e.tensor_copy(
                qi32[:, 0:sw], qtmp2[:, 0:sw]))
            p.op("vector", lambda e, i=i, sw=swidth: e.tensor_copy(
                dst_fn(i), qi32[:, 0:sw]), sem_dv)

    for tg in range(4):
        t0 = tg * 512
        p.wait("gpsimd", sem_pe)
        p.dma("gpsimd", x2st[:, :],
              bass.AP(x2_sp, t0, [[CH, 128], [128 * CH, FC], [1, 512]]), sem_g)
        p.wait("scalar", sem_g)
        p.wait("tensor", sem_g)
        pe_base = sem_pe.val
        for fc in range(FC):
            if fc >= 2:
                p.wait("scalar", sem_pe, pe_base + fc - 1)
            p.op("scalar", lambda e, fc=fc: e.activation(
                sq_parD[fc % 2][:, :], x2st[:, fc * 512:(fc + 1) * 512], AF.Square), sem_ac)
            p.wait("tensor", sem_ac)
            p.op("tensor", lambda e, fc=fc: e.matmul(
                psf[3][0:1, :], ones_col_bf[:, :], sq_parD[fc % 2][:, :],
                start=(fc == 0), stop=(fc == FC - 1)), sem_pe)
        for fc in range(FC):
            p.op("tensor", lambda e, fc=fc: e.matmul(
                psf[2][0:1, :], ones_col_bf[:, :], x2st[:, fc * 512:(fc + 1) * 512],
                start=(fc == 0), stop=(fc == FC - 1)),
                sem=(sem_pe if fc == FC - 1 else None))
        p.wait("vector", sem_pe)
        p.op("vector", lambda e: e.tensor_scalar(
            rowA[0:1, 0:512], psf[2][0:1, :], 1.0 / H, None, ALU.mult))
        p.op("vector", lambda e: e.tensor_scalar(
            rowB_[0:1, 0:512], psf[3][0:1, :], 1.0 / H, None, ALU.mult))
        p.op("vector", lambda e: e.tensor_tensor(
            rowC[0:1, 0:512], rowA[0:1, 0:512], rowA[0:1, 0:512], ALU.mult))
        p.op("vector", lambda e: e.tensor_tensor(
            rowB_[0:1, 0:512], rowB_[0:1, 0:512], rowC[0:1, 0:512], ALU.subtract))
        p.op("vector", lambda e: e.tensor_scalar(
            rowB_[0:1, 0:512], rowB_[0:1, 0:512], float(EPS_LN), None, ALU.add), sem_dv)
        p.wait("scalar", sem_dv)
        p.op("scalar", lambda e: e.activation(rowB_[0:1, 0:512], rowB_[0:1, 0:512], AF.Ln))
        p.op("scalar", lambda e: e.activation(
            rowC[0:1, 0:512], rowB_[0:1, 0:512], AF.Exp, scale=-0.5), sem_ac)
        p.wait("tensor", sem_ac)
        p.op("tensor", lambda e: e.matmul(
            psf[4][:, :], ones_row_f[:, :], rowA[0:1, 0:512], start=True, stop=True))
        p.op("tensor", lambda e: e.matmul(
            psf[5][:, :], ones_row_f[:, :], rowC[0:1, 0:512],
            start=True, stop=True), sem_pe)
        p.wait("vector", sem_pe)
        p.op("vector", lambda e: e.tensor_copy(d1bc1[:, :], psf[4][:, :]))
        p.op("vector", lambda e: e.tensor_copy(d1bc2[:, :], psf[5][:, :]), sem_dv)
        for fc in range(FC):
            p.wait("sync", sem_dv)
            p.dma("sync", qtmp1[:, :], x2_sp[fc * 128:(fc + 1) * 128, t0:t0 + 512], sem_x)
            p.wait("vector", sem_x)
            p.op("vector", lambda e: e.tensor_tensor(
                qtmp1[:, :], qtmp1[:, :], d1bc1[:, :], ALU.subtract))
            p.op("vector", lambda e, fc=fc: e.tensor_tensor(
                x2st[:, fc * 512:(fc + 1) * 512], qtmp1[:, :], d1bc2[:, :],
                ALU.mult), sem_dv)
            p.wait("tensor", sem_dv)
            p.op("tensor", lambda e, fc=fc: e.matmul(
                psf[fc % 2][:, :], h128_bf[:, :], x2st[:, fc * 512:(fc + 1) * 512],
                start=True, stop=True), sem_pe)
            p.wait("vector", sem_pe)
            p.op("vector", lambda e, fc=fc: e.tensor_copy(
                zbuf[:, fc * 512:(fc + 1) * 512], psf[fc % 2][:, :]), sem_dv)
        cur, oth = zbuf, zpong
        for st in range(4):
            stride = 1 << st
            npairs = FC // (2 * stride)
            def mk(tn, base, stride=stride, npairs=npairs):
                return bass.AP(tn, base,
                               [[FC * 512, 128], [2 * stride * 512, npairs],
                                [512, stride], [1, 512]])
            p.op("vector", lambda e, a=mk(cur, 0), b=mk(cur, stride * 512),
                 o=mk(oth, 0): e.tensor_tensor(o, a, b, ALU.add))
            p.op("vector", lambda e, a=mk(cur, 0), b=mk(cur, stride * 512),
                 o=mk(oth, stride * 512): e.tensor_tensor(o, a, b, ALU.subtract), sem_dv)
            cur, oth = oth, cur
        quant_scales(cur, 512, FC, s1Bt, d1bc1)
        p.op("vector", lambda e, t0=t0: e.tensor_copy(
            rcp1B[:, t0:t0 + 512], d1bc1[:, 0:512]), sem_dv)
        emit_quant_slices(
            lambda i, cur=cur: cur[:, i * 512:(i + 1) * 512], FC, 512, s1Bt[:, 0:512],
            lambda i, t0=t0: xq[:, i * CH + t0:i * CH + t0 + 512])
    p.barrier()
    if upto == "xq":
        if g.get("debug") == "qsc":
            dbg = outs["dbg"]
            p.chain_dma("sync", dbg[:, 0:8], scol[:, :])
            p.chain_dma("sync", dbg[0:1, 8:8 + 1024], rowA[0:1, 512:1536])
            p.chain_dma("sync", dbg[:, 2048:2560], s1Bt[:, :])
            p.chain_dma("sync", dbg[:, 2560:3072], qtmp1[:, :])
            p.chain_dma("sync", dbg[:, 3072:3584], d1bc1[:, :])
        return

    p.op("vector", lambda e: e.tensor_scalar(
        csgB[:, :], rcp1B[:, :], wscol[:, 0:1], None, ALU.mult))
    p.op("vector", lambda e: e.tensor_scalar(
        csuB[:, :], rcp1B[:, :], wscol[:, 1:2], None, ALU.mult), sem_dv)
    p.wait("scalar", sem_dv)
    tern_dv = sem_dv.val
    for om in range(OC):
        bw = om % 2
        p.wait("sync", sem_dv, tern_dv)
        p.dma("sync", wgu_par[bw][:, :],
              bass.AP(wgateT, om * 128, [[I8192, 128], [128 * I8192, FC], [1, 128]]),
              sem_x)
        p.wait("gpsimd", sem_dv, tern_dv)
        p.dma("gpsimd", wuu_par[bw][:, :],
              bass.AP(wupT, om * 128, [[I8192, 128], [128 * I8192, FC], [1, 128]]),
              sem_g)
        p.wait("vector", sem_x)
        p.wait("vector", sem_g)
        p.wait("vector", sem_pe)
        for wsrc, tdst, ti in ((wgu_par[bw], terng, 0), (wuu_par[bw], ternu, 1)):
            p.op("vector", lambda e, w=wsrc, ti=ti: e.tensor_scalar(
                tb_tern[:, :], w[:, :], thrcol[:, 4 + ti:5 + ti], None, ALU.is_lt))
            p.op("vector", lambda e, w=wsrc, t=tdst, ti=ti: e.tensor_scalar(
                t[:, :], w[:, :], thrcol[:, ti:ti + 1], None, ALU.is_gt))
            p.op("vector", lambda e, t=tdst: e.tensor_tensor(
                t[:, :], t[:, :], tb_tern[:, :], ALU.subtract), sem_dv)
        tern_dv = sem_dv.val
        p.wait("tensor", sem_dv)
        for tg in range(4):
            for mat, tern, psb, cs in ((0, terng, 0, csgB), (1, ternu, 1, csuB)):
                p.wait("tensor", sem_dv)
                for fc in range(FC):
                    p.op("tensor", lambda e, tern=tern, fc=fc, tg=tg, psb=psb: e.matmul(
                        psf[psb][:, :], tern[:, fc * 128:(fc + 1) * 128],
                        xq[:, fc * CH + tg * 512:fc * CH + (tg + 1) * 512],
                        start=(fc == 0), stop=(fc == FC - 1)),
                        sem=(sem_pe if fc == FC - 1 else None))
            p.wait("vector", sem_pe)
            p.op("vector", lambda e, tg=tg: e.tensor_tensor(
                mtmp1[:, :], psf[0][:, :], csgB[:, tg * 512:(tg + 1) * 512], ALU.mult),
                sem_dv)
            p.wait("scalar", sem_dv)
            p.op("scalar", lambda e: e.activation(mtmp2[:, :], mtmp1[:, :], AF.Silu),
                 sem_ac)
            p.op("vector", lambda e, tg=tg: e.tensor_tensor(
                mtmp3[:, :], psf[1][:, :], csuB[:, tg * 512:(tg + 1) * 512], ALU.mult),
                sem_dv)
            p.wait("vector", sem_ac)
            p.wait("vector", sem_g)
            p.op("vector", lambda e, tg=tg: e.tensor_tensor(
                mid_par[tg % 2][:, :], mtmp2[:, :], mtmp3[:, :], ALU.mult), sem_dv)
            p.wait("gpsimd", sem_dv)
            p.dma("gpsimd", mid_sp[om * 128:(om + 1) * 128, tg * 512:(tg + 1) * 512],
                  mid_par[tg % 2][:, :], sem_g)
    p.barrier()
    if upto == "mid":
        return

    dv0 = sem_dv.val
    x0 = sem_x.val
    g0 = sem_g.val
    for oc in range(OC):
        b = oc % 2
        res = ternd_a if b == 0 else ternd_b
        p.wait("sync", sem_dv, dv0 + max(0, oc - 1))
        p.dma("sync", wdf_par[b][:, :], wdownT[oc * 128:(oc + 1) * 128, :], sem_x)
        p.wait("vector", sem_x, x0 + 16 * (oc + 1))
        p.wait("vector", sem_g, g0 + 16 * max(0, oc - 1))
        p.op("vector", lambda e, b=b: e.tensor_scalar(
            tb_tern[:, :], wdf_par[b][:, :], thrcol[:, 6:7], None, ALU.is_lt))
        p.op("vector", lambda e, b=b, res=res: e.tensor_scalar(
            res[:, :], wdf_par[b][:, :], thrcol[:, 2:3], None, ALU.is_gt))
        p.op("vector", lambda e, res=res: e.tensor_tensor(
            res[:, :], res[:, :], tb_tern[:, :], ALU.subtract), sem_dv)
        p.wait("gpsimd", sem_dv, dv0 + oc + 1)
        p.dma("gpsimd", wdt_sp[oc * 128:(oc + 1) * 128, :], res[:, :], sem_g)
    p.barrier()

    for tg in range(4):
        for sub in range(2):
            t0 = tg * 512 + sub * 256
            p.wait("gpsimd", sem_pe)
            p.dma("gpsimd", bufA[:, :],
                  bass.AP(mid_sp, t0, [[CH, 128], [128 * CH, OC], [1, 256]]), sem_g)
            p.wait("tensor", sem_g)
            for oc in range(OC):
                p.wait("tensor", sem_dv)
                p.op("tensor", lambda e, oc=oc: e.matmul(
                    psf[oc % 2][:, 0:256], h128_bf[:, :], bufA[:, oc * 256:(oc + 1) * 256],
                    start=True, stop=True), sem_pe)
                p.wait("vector", sem_pe)
                p.op("vector", lambda e, oc=oc: e.tensor_copy(
                    bufB[:, oc * 256:(oc + 1) * 256], psf[oc % 2][:, 0:256]), sem_dv)
            cur, oth = bufB, bufA
            for st in range(6):
                stride = 1 << st
                npairs = OC // (2 * stride)
                def mk(tn, base, stride=stride, npairs=npairs):
                    return bass.AP(tn, base,
                                   [[OC * 256, 128], [2 * stride * 256, npairs],
                                    [256, stride], [1, 256]])
                p.op("vector", lambda e, a=mk(cur, 0), b=mk(cur, stride * 256),
                     o=mk(oth, 0): e.tensor_tensor(o, a, b, ALU.add))
                p.op("vector", lambda e, a=mk(cur, 0), b=mk(cur, stride * 256),
                     o=mk(oth, stride * 256): e.tensor_tensor(o, a, b, ALU.subtract),
                     sem_dv)
                cur, oth = oth, cur
            quant_scales(cur, 256, OC, s2Bt, rcp2Bt)
            p.op("vector", lambda e: e.tensor_scalar(
                csdBt[:, :], rcp2Bt[:, 0:256], wscol[:, 2:3], None, ALU.mult), sem_dv)
            emit_quant_slices(
                lambda i, cur=cur: cur[:, i * 256:(i + 1) * 256], OC, 256, s2Bt[:, 0:256],
                lambda i, oth=oth: oth[:, i * 256:(i + 1) * 256])
            midq = oth
            p.wait("gpsimd", sem_dv)
            p.wait("sync", sem_dv)
            for ft in range(FC):
                bank = 2 + (ft % 4)
                p.wait("gpsimd", sem_pe)
                p.wait("sync", sem_pe)
                p.dma("gpsimd", wdt_ft[:, 0:OC * 64],
                      bass.AP(wdt_sp, ft * 128,
                              [[H, 128], [128 * H, OC // 2], [1, 128]]), sem_g)
                p.dma("sync", wdt_ft[:, OC * 64:OC * 128],
                      bass.AP(wdt_sp, ft * 128 + (OC // 2) * 128 * H,
                              [[H, 128], [128 * H, OC // 2], [1, 128]]), sem_x)
                p.wait("tensor", sem_g)
                p.wait("tensor", sem_x)
                p.wait("tensor", sem_dv)
                for oc in range(OC):
                    p.op("tensor", lambda e, oc=oc, bank=bank: e.matmul(
                        psf[bank][:, 0:256], wdt_ft[:, oc * 128:(oc + 1) * 128],
                        midq[:, oc * 256:(oc + 1) * 256],
                        start=(oc == 0), stop=(oc == OC - 1)),
                        sem=(sem_pe if oc == OC - 1 else None))
                p.wait("vector", sem_pe)
                p.op("vector", lambda e, ft=ft, bank=bank: e.tensor_tensor(
                    y4_blk[:, ft * 256:(ft + 1) * 256], psf[bank][:, 0:256],
                    csdBt[:, :], ALU.mult))
                p.op("vector", lambda e, ft=ft: e.tensor_copy(
                    ybf4[:, ft * 256:(ft + 1) * 256], y4_blk[:, ft * 256:(ft + 1) * 256]),
                    sem_dv)
            p.wait("scalar", sem_dv)
            p.wait("tensor", sem_dv)
            pe4 = sem_pe.val
            for fc in range(FC):
                if fc >= 2:
                    p.wait("scalar", sem_pe, pe4 + fc - 1)
                p.op("scalar", lambda e, fc=fc: e.activation(
                    sq_parD[fc % 2][:, 0:256], ybf4[:, fc * 256:(fc + 1) * 256],
                    AF.Square), sem_ac)
                p.wait("tensor", sem_ac)
                p.op("tensor", lambda e, fc=fc: e.matmul(
                    psf[0][0:1, 0:256], ones_col_bf[:, :], sq_parD[fc % 2][:, 0:256],
                    start=(fc == 0), stop=(fc == FC - 1)), sem_pe)
            for fc in range(FC):
                p.op("tensor", lambda e, fc=fc: e.matmul(
                    psf[1][0:1, 0:256], ones_col_bf[:, :],
                    ybf4[:, fc * 256:(fc + 1) * 256],
                    start=(fc == 0), stop=(fc == FC - 1)),
                    sem=(sem_pe if fc == FC - 1 else None))
            p.wait("vector", sem_pe)
            p.op("vector", lambda e: e.tensor_scalar(
                rowA[0:1, 0:256], psf[1][0:1, 0:256], 1.0 / H, None, ALU.mult))
            p.op("vector", lambda e: e.tensor_scalar(
                rowB_[0:1, 0:256], psf[0][0:1, 0:256], 1.0 / H, None, ALU.mult))
            p.op("vector", lambda e: e.tensor_tensor(
                rowC[0:1, 0:256], rowA[0:1, 0:256], rowA[0:1, 0:256], ALU.mult))
            p.op("vector", lambda e: e.tensor_tensor(
                rowB_[0:1, 0:256], rowB_[0:1, 0:256], rowC[0:1, 0:256], ALU.subtract))
            p.op("vector", lambda e: e.tensor_scalar(
                rowB_[0:1, 0:256], rowB_[0:1, 0:256], float(EPS_LN), None, ALU.add),
                sem_dv)
            p.wait("scalar", sem_dv)
            p.op("scalar", lambda e: e.activation(
                rowB_[0:1, 0:256], rowB_[0:1, 0:256], AF.Ln))
            p.op("scalar", lambda e: e.activation(
                rowC[0:1, 0:256], rowB_[0:1, 0:256], AF.Exp, scale=-0.5), sem_ac)
            p.wait("tensor", sem_ac)
            p.wait("tensor", sem_dv)
            p.op("tensor", lambda e: e.matmul(
                psf[0][:, 0:256], ones_row_f[:, :], rowA[0:1, 0:256],
                start=True, stop=True))
            p.op("tensor", lambda e: e.matmul(
                psf[0][:, 256:512], ones_row_f[:, :], rowC[0:1, 0:256],
                start=True, stop=True), sem_pe)
            p.wait("vector", sem_pe)
            p.op("vector", lambda e: e.tensor_copy(e4bc[:, 0:256], psf[0][:, 0:256]))
            p.op("vector", lambda e: e.tensor_copy(e4bc[:, 256:512], psf[0][:, 256:512]),
                 sem_dv)
            for fc in range(FC):
                p.wait("sync", sem_dv)
                p.dma("sync", qtmp1[:, 0:256],
                      x2_sp[fc * 128:(fc + 1) * 128, t0:t0 + 256], sem_x)
                p.wait("vector", sem_x)
                p.wait("vector", sem_st)
                p.op("vector", lambda e, fc=fc: e.tensor_tensor(
                    qtmp2[:, 0:256], y4_blk[:, fc * 256:(fc + 1) * 256],
                    e4bc[:, 0:256], ALU.subtract))
                p.op("vector", lambda e: e.tensor_tensor(
                    qtmp2[:, 0:256], qtmp2[:, 0:256], e4bc[:, 256:512], ALU.mult))
                p.op("vector", lambda e: e.tensor_tensor(
                    qtmp2[:, 0:256], qtmp2[:, 0:256], qtmp1[:, 0:256], ALU.add), sem_dv)
                p.wait("sync", sem_dv)
                p.dma("sync", outT[fc * 128:(fc + 1) * 128, t0:t0 + 256],
                      qtmp2[:, 0:256], sem_st)
    p.wait("sync", sem_st)


def _finish(nc, p, stack):
    p.barrier()
    with nc.Block() as block:
        @block.tensor
        def _(e):
            for f in p.ops["tensor"]:
                f(e)

        @block.vector
        def _(e):
            for f in p.ops["vector"]:
                f(e)

        @block.scalar
        def _(e):
            for f in p.ops["scalar"]:
                f(e)

        @block.gpsimd
        def _(e):
            for f in p.ops["gpsimd"]:
                f(e)

        @block.sync
        def _(e):
            for f in p.ops["sync"]:
                f(e)
    stack.close()
    return nc


def _hmat(n):
    Hm = np.array([[1.0]], np.float32)
    while Hm.shape[0] < n:
        Hm = np.block([[Hm, Hm], [Hm, -Hm]]).astype(np.float32)
    return Hm


def _mt():
    i = np.arange(1, 257, dtype=np.float32)
    d = i[None, :] - i[:, None]
    return np.where(d >= 0, d, np.float32(1e9)).astype(np.float32)


def host_prep(inputs):
    f32 = np.float32
    x = np.asarray(inputs["L0_attn_hidden_states"], f32)[0]
    mt = _mt()
    cons = {
        "wqkvT": np.ascontiguousarray(np.asarray(inputs["L0_attn_w_qkv"], f32).T),
        "wgT": np.ascontiguousarray(np.asarray(inputs["L0_attn_w_gate"], f32).T),
        "woT": np.ascontiguousarray(np.asarray(inputs["L0_attn_w_out"], f32).T),
        "normw": np.ascontiguousarray(np.asarray(inputs["L0_attn_norm_w"], f32).reshape(FC, 128).T),
        "wgateT": np.ascontiguousarray(np.asarray(inputs["L0_ffn_w_gate"], f32).T),
        "wupT": np.ascontiguousarray(np.asarray(inputs["L0_ffn_w_up"], f32).T),
        "wdownT": np.ascontiguousarray(np.asarray(inputs["L0_ffn_w_down"], f32).T),
        "slope": np.asarray(inputs["L0_attn_slope_rate"], f32).reshape(1, NH),
        "h128": _hmat(128),
        "mtA": np.ascontiguousarray(mt[0:128]),
        "mtB": np.ascontiguousarray(mt[128:256]),
        "arange1": np.arange(1, 257, dtype=f32).reshape(1, 256),
        "revrow": (255.0 - np.arange(256, dtype=f32)).reshape(1, 256),
        "onesrow": np.ones((1, 128), f32),
        "ident": np.eye(128, dtype=f32),
    }
    per_core = []
    for c in range(NCORES):
        m = dict(cons)
        xc = x[c * CH:(c + 1) * CH]
        m["xT"] = np.ascontiguousarray(xc.T)
        per_core.append(m)
    return per_core, x


def kernel(**inputs):
    per_core, x = host_prep(inputs)
    ncF = build_program("fused")
    WW = WARM_NB * BLK
    for c in range(NCORES):
        xw = np.zeros((WW, H), np.float32)
        take = min(WW, c * CH)
        if take > 0:
            xw[WW - take:] = x[c * CH - take:c * CH]
        per_core[c]["xwinT"] = np.ascontiguousarray(xw.T)
    res = run_bass_kernel_spmd(ncF, per_core, core_ids=list(range(NCORES)))
    out = np.concatenate([np.asarray(r["outT"]).T for r in res.results], axis=0)
    return out[None].astype(np.float32)


def kernel_v1(**inputs):
    per_core, x = host_prep(inputs)
    ncA = build_program("v1a")
    resA = run_bass_kernel_spmd(ncA, per_core, core_ids=list(range(NCORES)))
    Sall = np.concatenate([np.asarray(r["S"]) for r in resA.results], axis=0)
    ncB = build_program("v1b")
    for c in range(NCORES):
        per_core[c]["Sg"] = Sall
        dv = np.full((1, NCORES), 1e9, np.float32)
        for cp in range(c):
            dv[0, cp] = float(c - 1 - cp)
        per_core[c]["dvec"] = dv
    resB = run_bass_kernel_spmd(ncB, per_core, core_ids=list(range(NCORES)))
    out = np.concatenate([np.asarray(r["outT"]).T for r in resB.results], axis=0)
    return out[None].astype(np.float32)


# version 08
# speedup vs baseline: 1.0336x; 1.0336x over previous
import math
from contextlib import ExitStack
import numpy as np
import ml_dtypes
import concourse.bass as bass
import concourse.mybir as mybir
from concourse.bass_utils import run_bass_kernel_spmd

F32 = mybir.dt.float32
BF16 = mybir.dt.bfloat16
I32 = mybir.dt.int32
AF = mybir.ActivationFunctionType
ALU = mybir.AluOpType
AX = mybir.AxisListType
BFnp = ml_dtypes.bfloat16

NCORES = 8
H = 2048
NH, HD = 16, 128
BLK = 256
CH = 2048
NB = CH // BLK
FC = H // 128
I8192 = 8192
OC = I8192 // 128
EPS_LN = 1e-5
EPS_ATTN = 1e-6
EPS_FFN = 1e-5

SLOPES = [2.0 ** (-(h + 1) / 2.0) for h in range(NH)]
WARM_DEPTH = [min(56, int(math.ceil(103.6 / (256.0 * s)))) for s in SLOPES]
WARM_NB = 56


class Sem:
    def __init__(self, handle, name):
        self.h = handle
        self.name = name
        self.val = 0


class Prog:
    ENGS = ("tensor", "vector", "scalar", "gpsimd", "sync")

    def __init__(self, nc, stack):
        self.nc = nc
        self.ops = {e: [] for e in self.ENGS}
        self._sems = {}
        self._stack = stack

    def sem(self, name):
        if name not in self._sems:
            handle = self._stack.enter_context(self.nc.semaphore(name))
            self._sems[name] = Sem(handle, name)
        return self._sems[name]

    def emit(self, eng, fn):
        self.ops[eng].append(fn)

    def dma(self, eng, out, in_, sem=None, **kw):
        if sem is not None:
            sem.val += 16
            self.emit(eng, lambda e, o=out, i=in_, s=sem, kw=kw:
                      e.dma_start(out=o, in_=i, **kw).then_inc(s.h, 16))
        else:
            self.emit(eng, lambda e, o=out, i=in_, kw=kw: e.dma_start(out=o, in_=i, **kw))

    def wait(self, eng, sem, val=None):
        v = sem.val if val is None else val
        self.emit(eng, lambda e, s=sem, v=v: e.wait_ge(s.h, v))

    def op(self, eng, fn, sem=None, inc=1):
        if sem is not None:
            sem.val += inc
            self.emit(eng, lambda e, f=fn, s=sem, k=inc: f(e).then_inc(s.h, k))
        else:
            self.emit(eng, fn)

    def barrier(self):
        snap = [(s, s.val) for s in self._sems.values() if s.val > 0]
        for eng in self.ENGS:
            for s, v in snap:
                self.emit(eng, lambda e, s=s, v=v: e.wait_ge(s.h, v))

    def chain_op(self, eng, fn):
        s = self.sem("chain")
        prev = s.val
        s.val += 1
        self.emit(eng, lambda e, s=s, v=prev: e.wait_ge(s.h, v))
        self.emit(eng, lambda e, f=fn, s=s: f(e).then_inc(s.h, 1))

    def chain_dma(self, eng, out, in_, **kw):
        s = self.sem("chain")
        prev = s.val
        s.val += 16
        self.emit(eng, lambda e, s=s, v=prev: e.wait_ge(s.h, v))
        self.emit(eng, lambda e, o=out, i=in_, s=s, kw=kw:
                  e.dma_start(out=o, in_=i, **kw).then_inc(s.h, 16))

    def chain_wait_all(self, eng):
        s = self.sem("chain")
        self.emit(eng, lambda e, s=s, v=s.val: e.wait_ge(s.h, v))


class SAlloc:
    def __init__(self, nc):
        self.nc = nc
        self.regions = {}
        self.cursor = {}

    def region(self, name, base, size):
        self.regions[name] = (base, size)
        self.cursor[name] = base

    def reset(self, name):
        self.cursor[name] = self.regions[name][0]

    def alloc(self, name, reg, shape, dtype, align=64):
        esz = {F32: 4, BF16: 2, I32: 4}[dtype]
        nb = shape[-1] * esz
        cur = (self.cursor[reg] + align - 1) // align * align
        base, size = self.regions[reg]
        if cur + nb > base + size:
            raise MemoryError(
                f"SBUF region {reg} overflow: {name} needs {nb}B at rel {cur - base}, size {size}")
        self.cursor[reg] = cur + nb
        return self.nc.alloc_sbuf_tensor_at(name, shape, dtype, offset=cur)


def build_program(mode, upto="full", debug=None):
    nc = bass.Bass("TRN2", target_bir_lowering=False)
    stack = ExitStack()
    p = Prog(nc, stack)

    xT = nc.dram_tensor("xT", [H, CH], F32, kind="ExternalInput")
    wqkvT = nc.dram_tensor("wqkvT", [H, 3 * NH * HD], F32, kind="ExternalInput")
    wgT = nc.dram_tensor("wgT", [H, H], F32, kind="ExternalInput")
    woT = nc.dram_tensor("woT", [H, H], F32, kind="ExternalInput")
    normw = nc.dram_tensor("normw", [128, FC], F32, kind="ExternalInput")
    wgateT = nc.dram_tensor("wgateT", [H, I8192], F32, kind="ExternalInput")
    wupT = nc.dram_tensor("wupT", [H, I8192], F32, kind="ExternalInput")
    wdownT = nc.dram_tensor("wdownT", [I8192, H], F32, kind="ExternalInput")
    slope = nc.dram_tensor("slope", [1, NH], F32, kind="ExternalInput")
    h128 = nc.dram_tensor("h128", [128, 128], F32, kind="ExternalInput")
    mtA = nc.dram_tensor("mtA", [128, 256], F32, kind="ExternalInput")
    mtB = nc.dram_tensor("mtB", [128, 256], F32, kind="ExternalInput")
    arange1 = nc.dram_tensor("arange1", [1, 256], F32, kind="ExternalInput")
    revrow = nc.dram_tensor("revrow", [1, 256], F32, kind="ExternalInput")
    onesrow = nc.dram_tensor("onesrow", [1, 128], F32, kind="ExternalInput")
    ident = nc.dram_tensor("ident", [128, 128], F32, kind="ExternalInput")

    outs = {}
    if mode == "v1a":
        outs["S"] = nc.dram_tensor("S", [NH * HD, HD], F32, kind="ExternalOutput")
    else:
        outs["outT"] = nc.dram_tensor("outT", [H, CH], F32, kind="ExternalOutput")
    Sg = dvec = xwinT = None
    if mode == "v1b":
        Sg = nc.dram_tensor("Sg", [NCORES * NH * HD, HD], F32, kind="ExternalInput")
        dvec = nc.dram_tensor("dvec", [1, NCORES], F32, kind="ExternalInput")
    if mode == "fused":
        xwinT = nc.dram_tensor("xwinT", [H, WARM_NB * BLK], F32, kind="ExternalInput")
    gate_sp = nc.dram_tensor("gate_sp", [H, CH], BF16)
    x2_sp = nc.dram_tensor("x2_sp", [H, CH], F32)
    mid_sp = nc.dram_tensor("mid_sp", [I8192, CH], BF16)
    wdt_sp = nc.dram_tensor("wdt_sp", [I8192, H], BF16)

    if debug:
        dbg = nc.dram_tensor("dbg", [128, FC * CH], F32, kind="ExternalOutput")
        outs["dbg"] = dbg

    psf = [nc.alloc_psum_tensor(f"psf{i}", [128, 512], F32) for i in range(7)]
    pst = nc.alloc_psum_tensor("pst", [128, 1024], BF16)

    sa = SAlloc(nc)
    SB0 = (nc.sbuf_base + 63) // 64 * 64
    SBTOP = nc.sbuf_top
    cur = SB0
    for nm, sz in (("CONST", 8 * 1024), ("ROWS", 24 * 1024), ("ATTN", 15 * 1024),
                   ("A", 64 * 1024), ("B", 64 * 1024)):
        sa.region(nm, cur, sz)
        cur += sz
    sa.region("W", cur, SBTOP - cur)

    ones_col_bf = sa.alloc("ones_col_bf", "CONST", [128, 1], BF16)
    ones_col_f = sa.alloc("ones_col_f", "CONST", [128, 1], F32)
    ones_row_f = sa.alloc("ones_row_f", "CONST", [1, 128], F32)
    id_bf = sa.alloc("id_bf", "CONST", [128, 128], BF16)
    id_f = sa.alloc("id_f", "CONST", [128, 128], F32)
    h128_bf = sa.alloc("h128_bf", "CONST", [128, 128], BF16)
    slope_s = sa.alloc("slope_s", "CONST", [1, NH], F32)
    slopeB = sa.alloc("slopeB", "CONST", [128, NH], F32)
    lamB = sa.alloc("lamB", "CONST", [128, NH], F32)
    normw_s = sa.alloc("normw_s", "CONST", [128, FC], F32)
    mt_s = sa.alloc("mt_s", "CONST", [128, 2 * 256], F32)
    ar_s = sa.alloc("ar_s", "CONST", [1, 256], F32)
    rv_s = sa.alloc("rv_s", "CONST", [1, 256], F32)
    rnd_off = sa.alloc("rnd_off", "CONST", [128, 2], F32)
    tmp_i32c = sa.alloc("tmp_i32c", "CONST", [128, 2], I32)
    wsrow = sa.alloc("wsrow", "CONST", [1, 8], F32)
    wscol = sa.alloc("wscol", "CONST", [128, 4], F32)
    thrcol = sa.alloc("thrcol", "CONST", [128, 8], F32)
    dv_s = sa.alloc("dv_s", "CONST", [1, NCORES], F32)
    wB = sa.alloc("wB", "CONST", [128, NH], F32)

    rowA = sa.alloc("rowA", "ROWS", [1, CH], F32)
    rowB_ = sa.alloc("rowB_", "ROWS", [1, CH], F32)
    rowC = sa.alloc("rowC", "ROWS", [1, CH], F32)

    kv_all = sa.alloc("kv_all", "ATTN", [128, NH * 128], F32)
    kv_bf = [sa.alloc(f"kv_bf{i}", "ATTN", [128, 128], BF16) for i in range(2)]
    qd_t = [sa.alloc(f"qd_t{i}", "ATTN", [128, 256], BF16) for i in range(2)]
    kd_t = [sa.alloc(f"kd_t{i}", "ATTN", [128, 256], BF16) for i in range(2)]
    vT_s = sa.alloc("vT_s", "ATTN", [128, 256], BF16)
    kdT_s = sa.alloc("kdT_s", "ATTN", [128, 256], BF16)
    qkd_s = sa.alloc("qkd_s", "ATTN", [128, 512], BF16)
    qdecB_h = sa.alloc("qdecB_h", "ATTN", [128, 256], BF16)
    kdecB_h = sa.alloc("kdecB_h", "ATTN", [128, 256], BF16)
    ddT_h = sa.alloc("ddT_h", "ATTN", [128, 512], BF16)
    sa.reset("ATTN")
    _attn_hold = sa.alloc("_attn_hold", "ATTN", [128, NH * 128], F32)
    sa.reset("ATTN")
    ybf_blk = sa.alloc("ybf_blk", "ATTN", [128, FC * 256], BF16)

    lnx = sa.alloc("lnx", "A", [128, FC * CH], BF16)
    sa.reset("A")
    woT_bf = sa.alloc("woT_bf", "A", [128, FC * CH], BF16)
    sa.reset("A")
    xq = sa.alloc("xq", "A", [128, FC * CH], BF16)
    sa.reset("A")
    y4_blk = sa.alloc("y4_blk", "A", [128, FC * 256], F32)
    ybf4 = sa.alloc("ybf4", "A", [128, FC * 256], BF16)
    wdf_par = [sa.alloc(f"wdf_par{i}", "A", [128, 2048], F32) for i in range(2)]
    ternd_a = sa.alloc("ternd_a", "A", [128, 2048], BF16)
    ternd_b = sa.alloc("ternd_b", "A", [128, 2048], BF16)
    wdt_ft = sa.alloc("wdt_ft", "A", [128, OC * 128], BF16)

    muB = sa.alloc("muB", "B", [128, CH], F32)
    rB = sa.alloc("rB", "B", [128, CH], F32)
    sa.reset("B")
    o_acc = sa.alloc("o_acc", "B", [128, NH * CH], BF16)
    sa.reset("B")
    x2st = sa.alloc("x2st", "B", [128, FC * 512], BF16)
    zbuf = sa.alloc("zbuf", "B", [128, FC * 512], BF16)
    zpong = sa.alloc("zpong", "B", [128, FC * 512], BF16)
    d1bc1 = sa.alloc("d1bc1", "B", [128, 512], F32)
    d1bc2 = sa.alloc("d1bc2", "B", [128, 512], F32)
    s1Bt = sa.alloc("s1Bt", "B", [128, 512], F32)
    rcp1B = sa.alloc("rcp1B", "B", [128, CH], F32)
    sa.reset("B")
    bufA = sa.alloc("bufA", "B", [128, OC * 256], BF16)
    bufB = sa.alloc("bufB", "B", [128, OC * 256], BF16)
    sa.reset("B")
    terng = sa.alloc("terng", "B", [128, 2048], BF16)
    ternu = sa.alloc("ternu", "B", [128, 2048], BF16)
    csgB = sa.alloc("csgB", "B", [128, CH], F32)
    csuB = sa.alloc("csuB", "B", [128, CH], F32)
    tb_tern = sa.alloc("tb_tern", "B", [128, 2048], BF16)
    wgu_par = [sa.alloc(f"wgu_par{i}", "B", [128, FC * 128], F32) for i in range(1)] * 2
    wuu_par = [sa.alloc(f"wuu_par{i}", "B", [128, FC * 128], F32) for i in range(1)] * 2
    mtmp1 = sa.alloc("mtmp1", "B", [128, 512], F32)
    mtmp2 = sa.alloc("mtmp2", "B", [128, 512], F32)
    mtmp3 = sa.alloc("mtmp3", "B", [128, 512], F32)
    mid_par = [sa.alloc(f"mid_par{i}", "B", [128, 512], BF16) for i in range(2)]

    xb_f = sa.alloc("xb_f", "W", [128, CH], F32)
    sq_par = [sa.alloc(f"sq_par{i}", "W", [128, 512], BF16) for i in range(2)]
    sgt = sa.alloc("sgt", "W", [128, 128], F32)
    wg_sl = [sa.alloc(f"wg_sl{i}", "W", [128, FC * 128], BF16) for i in range(2)]
    sa.reset("W")
    wslice = sa.alloc("wslice", "W", [128, FC * 384], BF16)
    qkv_h = sa.alloc("qkv_h", "W", [128, 3 * CH], BF16)
    sq_parA = [sa.alloc(f"sq_parA{i}", "W", [128, 512], BF16) for i in range(2)]
    ddscr = sa.alloc("ddscr", "W", [128, 256], F32)
    sa.reset("W")
    gate_blk = sa.alloc("gate_blk", "W", [128, FC * 256], BF16)
    p_bf = sa.alloc("p_bf", "W", [128, FC * 256], BF16)
    sq_parE = [sa.alloc(f"sq_parE{i}", "W", [128, 256], BF16) for i in range(2)]
    bcq1 = sa.alloc("bcq1", "W", [128, 256], F32)
    bcq2 = sa.alloc("bcq2", "W", [128, 512], F32)
    ptmp = sa.alloc("ptmp", "W", [128, 256], F32)
    x4_par = [sa.alloc(f"x4_par{i}", "W", [128, 256], F32) for i in range(2)]
    o4_par = [sa.alloc(f"o4_par{i}", "W", [128, 256], F32) for i in range(2)]
    sa.reset("W")
    wstr = [sa.alloc(f"wstr{i}", "W", [128, 2048], F32) for i in range(2)]
    redcols = sa.alloc("redcols", "W", [128, 64], F32)
    qtmp1 = sa.alloc("qtmp1", "W", [128, 512], F32)
    qtmp2 = sa.alloc("qtmp2", "W", [128, 512], F32)
    qi32 = sa.alloc("qi32", "W", [128, 512], I32)
    scol = sa.alloc("scol", "W", [128, 8], F32)
    sq_parD = [sa.alloc(f"sq_parD{i}", "W", [128, 512], BF16) for i in range(2)]
    s2Bt = sa.alloc("s2Bt", "W", [128, 512], F32)
    rcp2Bt = sa.alloc("rcp2Bt", "W", [128, 512], F32)
    csdBt = sa.alloc("csdBt", "W", [128, 256], F32)
    e4bc = sa.alloc("e4bc", "W", [128, 512], F32)

    MAGIC = float(np.float32(12582912.0))

    sem_x = p.sem("sem_x")
    sem_g = p.sem("sem_g")
    sem_pe = p.sem("sem_pe")
    sem_dv = p.sem("sem_dv")
    sem_ac = p.sem("sem_ac")
    sem_pl = p.sem("sem_pl")
    sem_st = p.sem("sem_st")

    p.chain_dma("sync", ones_row_f[:, :], onesrow[:, :])
    p.chain_dma("sync", slope_s[:, :], slope[:, :])
    p.chain_dma("sync", mt_s[:, 0:256], mtA[:, :])
    p.chain_dma("sync", mt_s[:, 256:512], mtB[:, :])
    p.chain_dma("sync", ar_s[:, :], arange1[:, :])
    p.chain_dma("sync", rv_s[:, :], revrow[:, :])
    p.chain_dma("sync", normw_s[:, :], normw[:, :])
    p.chain_dma("sync", id_f[:, :], ident[:, :])
    p.chain_dma("gpsimd", ones_col_bf[:, :], onesrow.ap().rearrange("1 p -> p 1"))
    p.chain_dma("sync", ones_col_f[:, :], onesrow.ap().rearrange("1 p -> p 1"))
    p.chain_dma("gpsimd", id_bf[:, :], ident[:, :])
    p.chain_dma("gpsimd", h128_bf[:, :], h128[:, :])
    p.chain_op("tensor", lambda e: e.matmul(psf[0][:, 0:NH], ones_row_f[:, :],
                                            slope_s[:, :], start=True, stop=True))
    p.chain_op("vector", lambda e: e.tensor_copy(slopeB[:, :], psf[0][:, 0:NH]))
    p.chain_op("vector", lambda e: e.memset(rnd_off[:, 0:1], 0.7))
    p.chain_op("vector", lambda e: e.memset(kv_all[:, 0:NH * 64], 0.0))
    p.chain_op("vector", lambda e: e.tensor_copy(tmp_i32c[:, 0:1], rnd_off[:, 0:1]))
    p.chain_op("vector", lambda e: e.memset(kv_all[:, NH * 64:NH * 128], 0.0))
    p.chain_op("vector", lambda e: e.tensor_copy(rnd_off[:, 0:1], tmp_i32c[:, 0:1]))
    p.chain_op("scalar", lambda e: e.activation(rowA[0:1, 0:NH], slope_s[:, :],
                                                AF.Exp, scale=-float(BLK)))
    p.chain_op("vector", lambda e: e.tensor_scalar(rnd_off[:, 0:1], rnd_off[:, 0:1],
                                                   -1.0, 1.0, ALU.mult, ALU.add))
    p.chain_op("tensor", lambda e: e.matmul(psf[1][:, 0:NH], ones_row_f[:, :],
                                            rowA[0:1, 0:NH], start=True, stop=True))
    p.chain_op("vector", lambda e: e.tensor_scalar(rnd_off[:, 1:2], rnd_off[:, 0:1],
                                                   -0.5, None, ALU.mult))
    p.chain_op("vector", lambda e: e.tensor_copy(lamB[:, :], psf[1][:, 0:NH]))
    if mode == "v1b":
        p.chain_dma("sync", dv_s[:, :], dvec[:, :])
        for cp in range(NCORES):
            p.chain_op("vector", lambda e, cp=cp: e.tensor_scalar(
                rowA[0:1, 0:NH], slope_s[:, :], dv_s[0:1, cp:cp + 1], None, ALU.mult))
            p.chain_op("scalar", lambda e: e.activation(
                rowA[0:1, 256:256 + NH], rowA[0:1, 0:NH], AF.Exp, scale=-2048.0))
            p.chain_op("tensor", lambda e: e.matmul(
                psf[0][:, 0:NH], ones_row_f[:, :], rowA[0:1, 256:256 + NH],
                start=True, stop=True))
            p.chain_op("vector", lambda e: e.tensor_copy(wB[:, :], psf[0][:, 0:NH]))
            for h in range(NH):
                p.chain_dma("sync", sgt[:, :],
                            Sg[cp * NH * HD + h * HD:cp * NH * HD + (h + 1) * HD, :])
                p.chain_op("vector", lambda e, h=h: e.scalar_tensor_tensor(
                    kv_all[:, h * 128:(h + 1) * 128], sgt[:, :], wB[:, h:h + 1],
                    kv_all[:, h * 128:(h + 1) * 128], ALU.mult, ALU.add))
    for eng in Prog.ENGS:
        p.chain_wait_all(eng)

    def emit_ln_sweep(x_src, lnx_dst, tokens, eps, reuse_staged=False):
        ngr = tokens // 512
        for fc in range(FC):
            p.dma("gpsimd", lnx_dst[:, fc * tokens:(fc + 1) * tokens],
                  x_src(fc), sem_g)
        p.wait("scalar", sem_g)
        p.wait("tensor", sem_g)
        for g in range(ngr):
            bs = (2 * g) % 6
            p.wait("tensor", sem_dv)
            pe_base = sem_pe.val
            for fc in range(FC):
                if fc >= 2:
                    p.wait("scalar", sem_pe, pe_base + fc - 1)
                p.op("scalar", lambda e, g=g, fc=fc, t=tokens: e.activation(
                    sq_par[fc % 2][:, :],
                    lnx_dst[:, fc * t + g * 512:fc * t + (g + 1) * 512],
                    AF.Square), sem_ac)
                p.wait("tensor", sem_ac)
                p.op("tensor", lambda e, fc=fc, bs=bs: e.matmul(
                    psf[bs + 1][0:1, :], ones_col_bf[:, :], sq_par[fc % 2][:, :],
                    start=(fc == 0), stop=(fc == FC - 1)), sem_pe)
            for fc in range(FC):
                p.op("tensor", lambda e, g=g, fc=fc, bs=bs, t=tokens: e.matmul(
                    psf[bs][0:1, :], ones_col_bf[:, :],
                    lnx_dst[:, fc * t + g * 512:fc * t + (g + 1) * 512],
                    start=(fc == 0), stop=(fc == FC - 1)),
                    sem=(sem_pe if fc == FC - 1 else None))
            p.wait("vector", sem_pe)
            p.op("vector", lambda e, g=g, bs=bs: e.tensor_scalar(
                rowA[0:1, g * 512:(g + 1) * 512], psf[bs][0:1, :],
                1.0 / H, None, ALU.mult))
            p.op("vector", lambda e, g=g, bs=bs: e.tensor_scalar(
                rowB_[0:1, g * 512:(g + 1) * 512], psf[bs + 1][0:1, :],
                1.0 / H, None, ALU.mult), sem_dv)
        p.op("vector", lambda e, t=tokens: e.tensor_tensor(
            rowC[0:1, 0:t], rowA[0:1, 0:t], rowA[0:1, 0:t], ALU.mult))
        p.op("vector", lambda e, t=tokens: e.tensor_tensor(
            rowB_[0:1, 0:t], rowB_[0:1, 0:t], rowC[0:1, 0:t], ALU.subtract))
        p.op("vector", lambda e, t=tokens, eps=eps: e.tensor_scalar(
            rowB_[0:1, 0:t], rowB_[0:1, 0:t], float(eps), None, ALU.add), sem_dv)
        p.wait("scalar", sem_dv)
        p.op("scalar", lambda e, t=tokens: e.activation(
            rowB_[0:1, 0:t], rowB_[0:1, 0:t], AF.Ln), sem_ac)
        p.op("scalar", lambda e, t=tokens: e.activation(
            rowC[0:1, 0:t], rowB_[0:1, 0:t], AF.Exp, scale=-0.5), sem_ac)
        p.wait("tensor", sem_ac)
        for g in range(ngr):
            p.wait("tensor", sem_dv)
            p.op("tensor", lambda e, g=g: e.matmul(
                psf[0][:, :], ones_row_f[:, :], rowA[0:1, g * 512:(g + 1) * 512],
                start=True, stop=True))
            p.op("tensor", lambda e, g=g: e.matmul(
                psf[1][:, :], ones_row_f[:, :], rowC[0:1, g * 512:(g + 1) * 512],
                start=True, stop=True), sem_pe)
            p.wait("vector", sem_pe)
            p.op("vector", lambda e, g=g: e.tensor_copy(
                muB[:, g * 512:(g + 1) * 512], psf[0][:, :]))
            p.op("vector", lambda e, g=g: e.tensor_copy(
                rB[:, g * 512:(g + 1) * 512], psf[1][:, :]), sem_dv)
        p.wait("tensor", sem_dv)
        for fc in range(FC):
            if reuse_staged:
                p.op("vector", lambda e, fc=fc, t=tokens, dst=lnx_dst: e.tensor_tensor(
                    xb_f[:, 0:t], dst[:, fc * t:(fc + 1) * t], muB[:, 0:t],
                    ALU.subtract))
            else:
                p.wait("sync", sem_dv)
                p.dma("sync", xb_f[:, 0:tokens], x_src(fc), sem_x)
                p.wait("vector", sem_x)
                p.op("vector", lambda e, t=tokens: e.tensor_tensor(
                    xb_f[:, 0:t], xb_f[:, 0:t], muB[:, 0:t], ALU.subtract))
            p.op("vector", lambda e, fc=fc, t=tokens, dst=lnx_dst: e.tensor_tensor(
                dst[:, fc * t:(fc + 1) * t], xb_f[:, 0:t], rB[:, 0:t], ALU.mult), sem_dv)

    proj_pe = [0]

    def emit_head(h, kv_only, jrange=None, tgs=None):
        if jrange is None:
            jrange = range(NB)
        if tgs is None:
            tgs = range(4)
        p.wait("vector", sem_pe)
        if not kv_only:
            for nt in range(2):
                p.wait("vector", sem_ac)
                p.op("vector", lambda e, h=h, nt=nt: e.tensor_scalar(
                    ddscr[:, :], mt_s[:, nt * 256:(nt + 1) * 256],
                    slopeB[:, h:h + 1], None, ALU.mult), sem_dv)
                p.wait("scalar", sem_dv)
                p.op("scalar", lambda e, nt=nt: e.activation(
                    ddT_h[:, nt * 256:(nt + 1) * 256], ddscr[:, :],
                    AF.Exp, scale=-1.0), sem_ac)
            p.wait("vector", sem_ac)
        for src, dst in (((ar_s, qdecB_h),) if not kv_only else ()) + ((rv_s, kdecB_h),):
            p.wait("vector", sem_ac)
            p.op("vector", lambda e, src=src, h=h: e.tensor_scalar(
                rowA[0:1, 0:256], src[:, :], slopeB[0:1, h:h + 1], None, ALU.mult), sem_dv)
            p.wait("scalar", sem_dv)
            p.wait("scalar", sem_pe)
            p.op("scalar", lambda e: e.activation(
                rowA[0:1, 256:512], rowA[0:1, 0:256], AF.Exp, scale=-1.0), sem_ac)
            p.wait("tensor", sem_ac)
            p.op("tensor", lambda e: e.matmul(
                psf[6][:, 0:256], ones_row_f[:, :], rowA[0:1, 256:512],
                start=True, stop=True), sem_pe)
            p.wait("vector", sem_pe)
            p.op("vector", lambda e, dst=dst: e.tensor_copy(dst[:, :], psf[6][:, 0:256]),
                 sem_dv)
        p.wait("gpsimd", sem_pe, proj_pe[0])
        wwid = 256 if kv_only else 384
        woff = 384 * h + (128 if kv_only else 0)
        p.dma("gpsimd", wslice[:, 0:FC * wwid],
              bass.AP(wqkvT, woff,
                      [[3 * NH * HD, 128], [128 * 3 * NH * HD, FC], [1, wwid]]), sem_g)
        p.wait("tensor", sem_g)
        ots = (1, 2) if kv_only else (0, 1, 2)
        for oi, ot in enumerate(ots):
            for tg in tgs:
                bank = (oi * 4 + tg) % 2
                p.wait("tensor", sem_ac)
                for fc in range(FC):
                    off = fc * wwid + (ot - (1 if kv_only else 0)) * 128
                    p.op("tensor", lambda e, off=off, tg=tg, fc=fc, bank=bank: e.matmul(
                        psf[bank][:, :], wslice[:, off:off + 128],
                        lnx[:, fc * CH + tg * 512:fc * CH + (tg + 1) * 512],
                        start=(fc == 0), stop=(fc == FC - 1)),
                        sem=(sem_pe if fc == FC - 1 else None))
                p.wait("scalar", sem_pe)
                p.op("scalar", lambda e, ot=ot, tg=tg, bank=bank: e.activation(
                    qkv_h[:, ot * CH + tg * 512:ot * CH + (tg + 1) * 512],
                    psf[bank][:, :], AF.Silu), sem_ac)
        proj_pe[0] = sem_pe.val
        jl = list(jrange)

        def prep(j):
            q0, k0 = 0 * CH + j * 256, 1 * CH + j * 256
            if not kv_only:
                p.op("vector", lambda e, q0=q0, j=j: e.tensor_tensor(
                    qd_t[j % 2][:, :], qkv_h[:, q0:q0 + 256], qdecB_h[:, :],
                    ALU.mult), sem_dv)
            p.op("vector", lambda e, k0=k0, j=j: e.tensor_tensor(
                kd_t[j % 2][:, :], qkv_h[:, k0:k0 + 256], kdecB_h[:, :],
                ALU.mult), sem_dv)
            return sem_dv.val

        p.wait("vector", sem_ac)
        p.op("vector", lambda e, h=h: e.tensor_copy(
            kv_bf[0][:, :], kv_all[:, h * 128:(h + 1) * 128]), sem_dv)
        dv_prep = prep(jl[0])
        for ji, j in enumerate(jl):
            q0, k0, v0 = (0 * CH + j * 256, 1 * CH + j * 256, 2 * CH + j * 256)
            qkb = 2 + (j % 2)
            oab = 4 + (j % 2)
            p.wait("tensor", sem_dv, dv_prep)
            for nt in range(2):
                p.op("tensor", lambda e, v0=v0, nt=nt: e.transpose(
                    pst[:, nt * 128:(nt + 1) * 128],
                    qkv_h[:, v0 + nt * 128:v0 + (nt + 1) * 128], id_bf[:, :]))
            for nt in range(2):
                p.op("tensor", lambda e, nt=nt, j=j: e.transpose(
                    pst[:, 256 + nt * 128:256 + (nt + 1) * 128],
                    kd_t[j % 2][:, nt * 128:(nt + 1) * 128], id_bf[:, :]),
                    sem=(sem_pe if nt == 1 else None))
            if not kv_only:
                for nt in range(2):
                    p.op("tensor", lambda e, k0=k0, q0=q0, nt=nt, qkb=qkb: e.matmul(
                        psf[qkb][:, nt * 256:(nt + 1) * 256],
                        qkv_h[:, k0 + nt * 128:k0 + (nt + 1) * 128],
                        qkv_h[:, q0:q0 + 256], start=True, stop=True),
                        sem=(sem_pe if nt == 1 else None))
            p.wait("vector", sem_pe)
            p.op("vector", lambda e: e.tensor_copy(vT_s[:, :], pst[:, 0:256]))
            p.op("vector", lambda e: e.tensor_copy(kdT_s[:, :], pst[:, 256:512]), sem_dv)
            if not kv_only:
                p.op("vector", lambda e, qkb=qkb: e.tensor_tensor(
                    qkd_s[:, :], psf[qkb][:, :], ddT_h[:, :], ALU.mult), sem_dv)
            dv_qkd = sem_dv.val
            if ji + 1 < len(jl):
                dv_prep = prep(jl[ji + 1])
            p.wait("tensor", sem_dv, dv_qkd)
            if not kv_only:
                p.op("tensor", lambda e, j=j, oab=oab: e.matmul(
                    psf[oab][:, 0:256], kv_bf[j % 2][:, :], qd_t[j % 2][:, :],
                    start=True, stop=False))
                for nt in range(2):
                    p.op("tensor", lambda e, nt=nt, oab=oab: e.matmul(
                        psf[oab][:, 0:256], vT_s[:, nt * 128:(nt + 1) * 128],
                        qkd_s[:, nt * 256:(nt + 1) * 256],
                        start=False, stop=(nt == 1)))
            for nt in range(2):
                p.op("tensor", lambda e, nt=nt, oab=oab: e.matmul(
                    psf[oab][:, 256:384], kdT_s[:, nt * 128:(nt + 1) * 128],
                    vT_s[:, nt * 128:(nt + 1) * 128],
                    start=(nt == 0), stop=(nt == 1)), sem=(sem_pe if nt == 1 else None))
            p.wait("vector", sem_pe)
            if not kv_only:
                p.op("vector", lambda e, h=h, j=j, oab=oab: e.tensor_copy(
                    o_acc[:, h * CH + j * 256:h * CH + (j + 1) * 256],
                    psf[oab][:, 0:256]))
            p.op("vector", lambda e, h=h, oab=oab: e.scalar_tensor_tensor(
                kv_all[:, h * 128:(h + 1) * 128], kv_all[:, h * 128:(h + 1) * 128],
                lamB[:, h:h + 1], psf[oab][:, 256:384], ALU.mult, ALU.add),
                sem=(sem_dv if kv_only else None))
            if not kv_only:
                p.op("vector", lambda e, h=h, j=j: e.tensor_copy(
                    kv_bf[(j + 1) % 2][:, :], kv_all[:, h * 128:(h + 1) * 128]),
                    sem_dv)


    if mode == "fused":
        for slab in range(WARM_NB // NB):
            heads = [h for h in range(NH)
                     if (slab + 1) * NB > WARM_NB - WARM_DEPTH[h]]
            if not heads:
                continue
            emit_ln_sweep(lambda fc, slab=slab: xwinT[
                fc * 128:(fc + 1) * 128, slab * CH:(slab + 1) * CH],
                lnx, CH, EPS_LN, reuse_staged=True)
            p.barrier()
            for h in heads:
                jr = [j for j in range(NB)
                      if slab * NB + j >= WARM_NB - WARM_DEPTH[h]]
                emit_head(h, kv_only=True, jrange=jr,
                          tgs=sorted({j // 2 for j in jr}))
            p.barrier()
    emit_ln_sweep(lambda fc: xT[fc * 128:(fc + 1) * 128, :], lnx, CH, EPS_LN)
    p.barrier()
    if upto == "ln":
        if debug == "lnx":
            for fc in range(FC):
                p.chain_op("vector", lambda e, fc=fc: e.tensor_copy(
                    xb_f[:, 0:CH], lnx[:, fc * CH:(fc + 1) * CH]))
                p.chain_dma("sync", outs["dbg"][:, fc * CH:(fc + 1) * CH], xb_f[:, 0:CH])
        return _finish(nc, p, stack)

    if mode != "v1a":
        for og in range(FC):
            b = og % 2
            p.wait("gpsimd", sem_pe)
            p.dma("gpsimd", wg_sl[b][:, :],
                  bass.AP(wgT, og * 128, [[H, 128], [128 * H, FC], [1, 128]]), sem_g)
            p.wait("tensor", sem_g)
            for tg in range(4):
                bank = 2 + (tg % 2)
                p.wait("tensor", sem_ac)
                for fc in range(FC):
                    p.op("tensor", lambda e, b=b, tg=tg, fc=fc, bank=bank: e.matmul(
                        psf[bank][:, :], wg_sl[b][:, fc * 128:(fc + 1) * 128],
                        lnx[:, fc * CH + tg * 512:fc * CH + (tg + 1) * 512],
                        start=(fc == 0), stop=(fc == FC - 1)),
                        sem=(sem_pe if fc == FC - 1 else None))
                p.wait("scalar", sem_pe)
                p.wait("scalar", sem_g)
                p.op("scalar", lambda e, tg=tg, bank=bank: e.activation(
                    sq_par[tg % 2][:, :], psf[bank][:, :], AF.Sigmoid), sem_ac)
                p.wait("gpsimd", sem_ac)
                p.dma("gpsimd", gate_sp[og * 128:(og + 1) * 128, tg * 512:(tg + 1) * 512],
                      sq_par[tg % 2][:, :], sem_g)
        p.barrier()
    if upto == "gate":
        if debug == "gate":
            for fc in range(FC):
                p.chain_dma("sync", sq_par[0][:, :], gate_sp[fc * 128:(fc + 1) * 128, 0:512])
                p.chain_op("vector", lambda e: e.tensor_copy(qtmp1[:, :], sq_par[0][:, :]))
                p.chain_dma("sync", outs["dbg"][:, fc * 512:(fc + 1) * 512], qtmp1[:, :])
        return _finish(nc, p, stack)

    for h in range(NH):
        emit_head(h, kv_only=(mode == "v1a"))
    p.barrier()
    if mode == "v1a":
        for h in range(NH):
            p.dma("sync", outs["S"][h * HD:(h + 1) * HD, :],
                  kv_all[:, h * 128:(h + 1) * 128], sem_st)
        p.wait("sync", sem_st)
        return _finish(nc, p, stack)

    if upto == "attn":
        if debug == "oacc":
            for hh in range(NH):
                p.chain_op("vector", lambda e, hh=hh: e.tensor_copy(
                    wstr[0][:, :], o_acc[:, hh * CH:(hh + 1) * CH]))
                p.chain_dma("sync", outs["dbg"][:, hh * CH:(hh + 1) * CH], wstr[0][:, :])
        return _finish(nc, p, stack)

    for og in range(FC):
        p.dma("gpsimd", woT_bf[:, og * CH:(og + 1) * CH],
              bass.AP(woT, og * 128 * H, [[H, 128], [128, FC], [1, 128]]), sem_g)
    p.wait("tensor", sem_g)
    p.wait("scalar", sem_g)
    p.wait("vector", sem_g)
    for j in range(NB):
        p.wait("gpsimd", sem_dv)
        p.dma("gpsimd", gate_blk[:, :],
              bass.AP(gate_sp, j * 256, [[CH, 128], [128 * CH, FC], [1, 256]]), sem_g)
        pe_base = sem_pe.val
        for hh in range(NH):
            if hh >= 2:
                p.wait("scalar", sem_pe, pe_base + hh - 1)
            p.op("scalar", lambda e, hh=hh, j=j: e.activation(
                sq_parE[hh % 2][:, :],
                o_acc[:, hh * CH + j * 256:hh * CH + (j + 1) * 256], AF.Square), sem_ac)
            p.wait("tensor", sem_ac)
            p.op("tensor", lambda e, hh=hh: e.matmul(
                psf[6][0:1, 0:256], ones_col_bf[:, :], sq_parE[hh % 2][:, :],
                start=(hh == 0), stop=(hh == NH - 1)), sem_pe)
        p.wait("vector", sem_pe)
        p.op("vector", lambda e: e.tensor_scalar(
            rowA[0:1, 0:256], psf[6][0:1, 0:256], 1.0 / H, float(EPS_ATTN),
            ALU.mult, ALU.add), sem_dv)
        p.wait("scalar", sem_dv)
        p.op("scalar", lambda e: e.activation(rowA[0:1, 0:256], rowA[0:1, 0:256], AF.Ln))
        p.op("scalar", lambda e: e.activation(
            rowA[0:1, 256:512], rowA[0:1, 0:256], AF.Exp, scale=-0.5), sem_ac)
        p.wait("tensor", sem_ac)
        p.op("tensor", lambda e: e.matmul(
            psf[6][:, 256:512], ones_row_f[:, :], rowA[0:1, 256:512],
            start=True, stop=True), sem_pe)
        p.wait("vector", sem_pe)
        p.op("vector", lambda e: e.tensor_copy(bcq1[:, :], psf[6][:, 256:512]), sem_dv)
        p.wait("vector", sem_g)
        for hh in range(NH):
            p.op("vector", lambda e, hh=hh, j=j: e.scalar_tensor_tensor(
                ptmp[:, :], o_acc[:, hh * CH + j * 256:hh * CH + (j + 1) * 256],
                normw_s[:, hh:hh + 1], bcq1[:, :], ALU.mult, ALU.mult))
            p.op("vector", lambda e, hh=hh: e.tensor_tensor(
                p_bf[:, hh * 256:(hh + 1) * 256], ptmp[:, :],
                gate_blk[:, hh * 256:(hh + 1) * 256], ALU.mult), sem_dv)
        p.wait("tensor", sem_dv)
        for fc in range(FC):
            bank = fc % 2
            p.wait("tensor", sem_dv)
            for og in range(FC):
                p.op("tensor", lambda e, og=og, fc=fc, bank=bank: e.matmul(
                    psf[bank][:, 0:256],
                    woT_bf[:, og * CH + fc * 128:og * CH + (fc + 1) * 128],
                    p_bf[:, og * 256:(og + 1) * 256],
                    start=(og == 0), stop=(og == FC - 1)),
                    sem=(sem_pe if og == FC - 1 else None))
            p.wait("vector", sem_pe)
            p.op("vector", lambda e, fc=fc, bank=bank: e.tensor_copy(
                ybf_blk[:, fc * 256:(fc + 1) * 256], psf[bank][:, 0:256]), sem_dv)
        p.wait("scalar", sem_dv)
        p.wait("tensor", sem_dv)
        pe2 = sem_pe.val
        for fc in range(FC):
            if fc >= 2:
                p.wait("scalar", sem_pe, pe2 + fc - 1)
            p.op("scalar", lambda e, fc=fc: e.activation(
                sq_parE[fc % 2][:, :], ybf_blk[:, fc * 256:(fc + 1) * 256],
                AF.Square), sem_ac)
            p.wait("tensor", sem_ac)
            p.op("tensor", lambda e, fc=fc: e.matmul(
                psf[2][0:1, 0:256], ones_col_bf[:, :], sq_parE[fc % 2][:, :],
                start=(fc == 0), stop=(fc == FC - 1)), sem_pe)
        for fc in range(FC):
            p.op("tensor", lambda e, fc=fc: e.matmul(
                psf[3][0:1, 0:256], ones_col_bf[:, :],
                ybf_blk[:, fc * 256:(fc + 1) * 256],
                start=(fc == 0), stop=(fc == FC - 1)),
                sem=(sem_pe if fc == FC - 1 else None))
        p.wait("vector", sem_pe)
        p.op("vector", lambda e: e.tensor_scalar(
            rowA[0:1, 0:256], psf[3][0:1, 0:256], 1.0 / H, None, ALU.mult))
        p.op("vector", lambda e: e.tensor_scalar(
            rowB_[0:1, 0:256], psf[2][0:1, 0:256], 1.0 / H, None, ALU.mult))
        p.op("vector", lambda e: e.tensor_tensor(
            rowC[0:1, 0:256], rowA[0:1, 0:256], rowA[0:1, 0:256], ALU.mult))
        p.op("vector", lambda e: e.tensor_tensor(
            rowB_[0:1, 0:256], rowB_[0:1, 0:256], rowC[0:1, 0:256], ALU.subtract))
        p.op("vector", lambda e: e.tensor_scalar(
            rowB_[0:1, 0:256], rowB_[0:1, 0:256], float(EPS_LN), None, ALU.add), sem_dv)
        p.wait("scalar", sem_dv)
        p.op("scalar", lambda e: e.activation(rowB_[0:1, 0:256], rowB_[0:1, 0:256], AF.Ln))
        p.op("scalar", lambda e: e.activation(
            rowC[0:1, 0:256], rowB_[0:1, 0:256], AF.Exp, scale=-0.5), sem_ac)
        p.wait("tensor", sem_ac)
        p.wait("tensor", sem_dv)
        p.op("tensor", lambda e: e.matmul(
            psf[4][:, 0:256], ones_row_f[:, :], rowA[0:1, 0:256], start=True, stop=True))
        p.op("tensor", lambda e: e.matmul(
            psf[4][:, 256:512], ones_row_f[:, :], rowC[0:1, 0:256],
            start=True, stop=True), sem_pe)
        p.wait("vector", sem_pe)
        p.op("vector", lambda e: e.tensor_copy(bcq2[:, 0:256], psf[4][:, 0:256]))
        p.op("vector", lambda e: e.tensor_copy(bcq2[:, 256:512], psf[4][:, 256:512]), sem_dv)
        for fc in range(FC):
            bpar = fc % 2
            p.wait("sync", sem_dv)
            p.dma("sync", x4_par[bpar][:, :],
                  xT[fc * 128:(fc + 1) * 128, j * 256:(j + 1) * 256], sem_x)
            p.wait("vector", sem_x)
            p.wait("vector", sem_g)
            p.op("vector", lambda e, fc=fc, bpar=bpar: e.tensor_tensor(
                o4_par[bpar][:, :], ybf_blk[:, fc * 256:(fc + 1) * 256],
                bcq2[:, 0:256], ALU.subtract))
            p.op("vector", lambda e, bpar=bpar: e.tensor_tensor(
                o4_par[bpar][:, :], o4_par[bpar][:, :], bcq2[:, 256:512], ALU.mult))
            p.op("vector", lambda e, bpar=bpar: e.tensor_tensor(
                o4_par[bpar][:, :], o4_par[bpar][:, :], x4_par[bpar][:, :],
                ALU.add), sem_dv)
            p.wait("gpsimd", sem_dv)
            p.dma("gpsimd", x2_sp[fc * 128:(fc + 1) * 128, j * 256:(j + 1) * 256],
                  o4_par[bpar][:, :], sem_g)
    p.barrier()
    if upto == "x2":
        if debug == "x2":
            for fc in range(FC):
                p.chain_dma("sync", outs["dbg"][:, fc * CH:(fc + 1) * CH],
                            x2_sp[fc * 128:(fc + 1) * 128, :])
        if debug == "phe":
            off = 4096
            for nm, buf in (("p", p_bf), ("g", gate_blk), ("y", ybf_blk)):
                for hh in range(2):
                    p.chain_op("vector", lambda e, buf=buf, hh=hh: e.tensor_copy(
                        wstr[0][:, :], buf[:, hh * 2048:(hh + 1) * 2048]))
                    p.chain_dma("sync", outs["dbg"][:, off:off + 2048], wstr[0][:, :])
                    off += 2048
            p.chain_dma("sync", outs["dbg"][:, off:off + 256], bcq1[:, :])
            p.chain_dma("sync", outs["dbg"][:, off + 256:off + 768], bcq2[:, :])
        return _finish(nc, p, stack)

    env = dict(locals())
    emit_ffn(p, env)
    p.barrier()
    if upto in ("xq", "mid"):
        if debug == "xq":
            for fc in range(FC):
                p.chain_op("vector", lambda e, fc=fc: e.tensor_copy(
                    wstr[0][:, :], xq[:, fc * CH:(fc + 1) * CH]))
                p.chain_dma("sync", outs["dbg"][:, fc * CH:(fc + 1) * CH], wstr[0][:, :])
        if debug == "bfly":
            for q in range(4):
                p.chain_op("vector", lambda e, q=q: e.tensor_copy(
                    wstr[0][:, :], zbuf[:, q * 2048:(q + 1) * 2048]))
                p.chain_dma("sync", outs["dbg"][:, q * 2048:(q + 1) * 2048], wstr[0][:, :])
            for q in range(4):
                p.chain_op("vector", lambda e, q=q: e.tensor_copy(
                    wstr[0][:, :], zpong[:, q * 2048:(q + 1) * 2048]))
                p.chain_dma("sync", outs["dbg"][:, 8192 + q * 2048:8192 + (q + 1) * 2048],
                            wstr[0][:, :])
            p.chain_dma("sync", outs["dbg"][:, 16384:16896], s1Bt[:, :])
        if debug == "mid":
            for oc in range(FC):
                p.chain_dma("sync", sq_parD[0][:, :], mid_sp[oc * 128:(oc + 1) * 128, 0:512])
                p.chain_op("vector", lambda e: e.tensor_copy(qtmp1[:, :], sq_parD[0][:, :]))
                p.chain_dma("sync", outs["dbg"][:, oc * 512:(oc + 1) * 512], qtmp1[:, :])
        if debug == "d2a":
            p.chain_dma("sync", outs["dbg"][:, 0:2048], csgB[:, :])
            p.chain_dma("sync", outs["dbg"][:, 2048:4096], csuB[:, :])
            for nm, tb, off in (("g", terng, 4096), ("u", ternu, 6144)):
                p.chain_op("vector", lambda e, tb=tb: e.tensor_copy(wstr[0][:, :], tb[:, :]))
                p.chain_dma("sync", outs["dbg"][:, off:off + 2048], wstr[0][:, :])
            for oc in range(FC):
                p.chain_dma("sync", sq_parD[0][:, :], mid_sp[oc * 128:(oc + 1) * 128, 0:512])
                p.chain_op("vector", lambda e: e.tensor_copy(qtmp1[:, :], sq_parD[0][:, :]))
                p.chain_dma("sync", outs["dbg"][:, 8192 + oc * 512:8192 + (oc + 1) * 512],
                            qtmp1[:, :])
            p.chain_dma("sync", outs["dbg"][:, 16384:18432], rcp1B[:, :])
            p.chain_dma("sync", outs["dbg"][:, 18432:20480], wgu_par[0][:, :])
            for fc4 in range(4):
                p.chain_op("vector", lambda e, fc4=fc4: e.tensor_copy(
                    wstr[0][:, 0:512], xq[:, fc4 * 4 * CH:fc4 * 4 * CH + 512]))
                p.chain_dma("sync", outs["dbg"][:, 20480 + fc4 * 512:20480 + (fc4 + 1) * 512],
                            wstr[0][:, 0:512])
        if debug == "ws":
            p.chain_dma("sync", outs["dbg"][0:1, 0:8], wsrow[0:1, 0:8])
    return _finish(nc, p, stack)


def emit_ffn(p, L):
    g = L
    upto = g["upto"]
    psf, pst = g["psf"], g["pst"]
    sem_x, sem_g, sem_pe, sem_dv, sem_ac, sem_pl, sem_st = (
        g["sem_x"], g["sem_g"], g["sem_pe"], g["sem_dv"], g["sem_ac"], g["sem_pl"],
        g["sem_st"])
    ones_col_bf, ones_row_f, id_f, h128_bf = (
        g["ones_col_bf"], g["ones_row_f"], g["id_f"], g["h128_bf"])
    ones_col_f = g["ones_col_f"]
    id_bf, pst_ = g["id_bf"], g["pst"]
    rowA, rowB_, rowC = g["rowA"], g["rowB_"], g["rowC"]
    wstr, redcols = g["wstr"], g["redcols"]
    wsrow, wscol, thrcol, rnd_off = g["wsrow"], g["wscol"], g["thrcol"], g["rnd_off"]
    wgateT, wupT, wdownT, x2_sp, mid_sp, wdt_sp = (
        g["wgateT"], g["wupT"], g["wdownT"], g["x2_sp"], g["mid_sp"], g["wdt_sp"])
    x2st, zbuf, zpong, d1bc1, d1bc2, s1Bt = (
        g["x2st"], g["zbuf"], g["zpong"], g["d1bc1"], g["d1bc2"], g["s1Bt"])
    xq, qtmp1, qtmp2, qi32, scol = g["xq"], g["qtmp1"], g["qtmp2"], g["qi32"], g["scol"]
    sq_parD = g["sq_parD"]
    rcp1B = g["rcp1B"]
    terng, ternu, tb_tern = g["terng"], g["ternu"], g["tb_tern"]
    csgB, csuB = g["csgB"], g["csuB"]
    wgu_par, wuu_par = g["wgu_par"], g["wuu_par"]
    mtmp1, mtmp2, mtmp3, mid_par = g["mtmp1"], g["mtmp2"], g["mtmp3"], g["mid_par"]
    bufA, bufB = g["bufA"], g["bufB"]
    y4_blk, ybf4 = g["y4_blk"], g["ybf4"]
    wdf_par, ternd_a, ternd_b, wdt_ft = (
        g["wdf_par"], g["ternd_a"], g["ternd_b"], g["wdt_ft"])
    s2Bt, rcp2Bt, csdBt, e4bc = g["s2Bt"], g["rcp2Bt"], g["csdBt"], g["e4bc"]
    xT, outs = g["xT"], g["outs"]
    outT = outs.get("outT")

    for mi, wmat in enumerate((wgateT, wupT, wdownT)):
        nrows = H if mi < 2 else I8192
        ncols = I8192 if mi < 2 else H
        nch = (nrows // 128) * (ncols // 2048)
        ci = 0
        dv0 = sem_dv.val
        x0 = sem_x.val
        for rr in range(nrows // 128):
            for cc in range(ncols // 2048):
                b = ci % 2
                p.wait("sync", sem_dv, dv0 + max(0, ci - 1))
                p.dma("sync", wstr[b][:, :],
                      wmat[rr * 128:(rr + 1) * 128, cc * 2048:(cc + 1) * 2048], sem_x)
                p.wait("vector", sem_x, x0 + 16 * (ci + 1))
                p.op("vector", lambda e, b=b, ci=ci: e.tensor_reduce(
                    redcols[:, ci:ci + 1], wstr[b][:, :], AX.X, ALU.add,
                    apply_absolute_value=True), sem_dv)
                ci += 1
        p.op("vector", lambda e: e.tensor_copy(qtmp1[:, 0:128], qtmp1[:, 0:128]))
        p.op("vector", lambda e: e.tensor_copy(qtmp1[:, 0:128], qtmp1[:, 0:128]))
        p.op("vector", lambda e, nch=nch: e.tensor_reduce(
            redcols[:, 0:1], redcols[:, 0:nch], AX.X, ALU.add), sem_dv)
        p.wait("tensor", sem_dv)
        p.op("tensor", lambda e: e.matmul(
            psf[5][0:1, 0:1], redcols[:, 0:1], ones_col_f[:, :],
            start=True, stop=True), sem_pe)
        p.wait("vector", sem_pe)
        p.op("vector", lambda e, mi=mi, nr=nrows, ncl=ncols: e.tensor_scalar(
            wsrow[0:1, mi:mi + 1], psf[5][0:1, 0:1], 1.0 / (nr * ncl), None,
            ALU.mult), sem_dv)
    p.wait("tensor", sem_dv)
    p.op("tensor", lambda e: e.matmul(
        psf[5][:, 0:8], ones_row_f[:, :], wsrow[0:1, 0:8], start=True, stop=True), sem_pe)
    p.wait("vector", sem_pe)
    p.op("vector", lambda e: e.tensor_copy(wscol[:, 0:4], psf[5][:, 0:4]))
    p.op("vector", lambda e: e.tensor_scalar(
        thrcol[:, 0:4], psf[5][:, 0:4], 0.5, None, ALU.mult))
    p.op("vector", lambda e: e.tensor_scalar(
        thrcol[:, 4:8], psf[5][:, 0:4], -0.5, None, ALU.mult), sem_dv)

    def quant_scales(cur, twidth, nchunks, sBdst, rcpdst, rcp_extra=None):
        nfull = 128 * nchunks
        p.op("vector", lambda e: e.tensor_reduce(
            qtmp1[:, 0:twidth],
            bass.AP(cur, 0, [[nchunks * twidth, 128], [1, twidth], [twidth, nchunks]]),
            AX.X, ALU.max))
        p.op("vector", lambda e: e.tensor_reduce(
            qtmp2[:, 0:twidth],
            bass.AP(cur, 0, [[nchunks * twidth, 128], [1, twidth], [twidth, nchunks]]),
            AX.X, ALU.min))
        p.op("vector", lambda e: e.tensor_scalar(
            qtmp2[:, 0:twidth], qtmp2[:, 0:twidth], -1.0, None, ALU.mult))
        p.op("vector", lambda e: e.tensor_tensor(
            qtmp1[:, 0:twidth], qtmp1[:, 0:twidth], qtmp2[:, 0:twidth], ALU.max))
        p.op("vector", lambda e, nfull=nfull: e.tensor_scalar(
            qtmp1[:, 0:twidth], qtmp1[:, 0:twidth], 1.0 / math.sqrt(nfull), None,
            ALU.mult), sem_dv)
        ngq = twidth // 128
        p.op("vector", lambda e, tw=twidth: e.tensor_copy(
            sq_parD[0][:, 0:tw], qtmp1[:, 0:tw]), sem_dv)
        p.wait("tensor", sem_dv)
        for q4 in range(ngq):
            p.op("tensor", lambda e, q4=q4: e.transpose(
                pst[:, q4 * 128:(q4 + 1) * 128],
                sq_parD[0][:, q4 * 128:(q4 + 1) * 128], id_bf[:, :]),
                sem=(sem_pe if q4 == ngq - 1 else None))
        p.wait("vector", sem_pe)
        for q4 in range(ngq):
            p.op("vector", lambda e, q4=q4: e.tensor_reduce(
                scol[:, q4:q4 + 1], pst[:, q4 * 128:(q4 + 1) * 128], AX.X, ALU.max))
        p.op("vector", lambda e: e.tensor_copy(qtmp2[:, 0:128], qtmp2[:, 0:128]))
        p.op("vector", lambda e: e.tensor_copy(qtmp2[:, 0:128], qtmp2[:, 0:128]))
        p.op("vector", lambda e, ngq=ngq: e.tensor_scalar(
            scol[:, 0:ngq], scol[:, 0:ngq], float(EPS_FFN), None, ALU.max))
        p.op("vector", lambda e: e.tensor_copy(qtmp2[:, 0:128], qtmp2[:, 0:128]))
        p.op("vector", lambda e: e.tensor_copy(qtmp2[:, 0:128], qtmp2[:, 0:128]))
        for q4 in range(ngq):
            p.op("vector", lambda e, q4=q4: e.reciprocal(
                scol[:, 4 + q4:5 + q4], scol[:, q4:q4 + 1]))
        p.op("vector", lambda e: e.tensor_copy(qtmp2[:, 0:128], qtmp2[:, 0:128]))
        p.op("vector", lambda e: e.tensor_copy(qtmp2[:, 0:128], qtmp2[:, 0:128]))
        p.op("vector", lambda e, ngq=ngq, nfull=nfull: e.tensor_scalar(
            scol[:, 4:4 + ngq], scol[:, 4:4 + ngq], 127.0 / math.sqrt(nfull), None,
            ALU.mult))
        p.op("vector", lambda e, ngq=ngq: e.tensor_scalar(
            scol[:, 0:ngq], scol[:, 0:ngq], 1.0 / 127.0, None, ALU.mult), sem_dv)
        p.wait("tensor", sem_dv)
        for which, cbase, dst in ((0, 4, "s"), (1, 0, "r")):
            for q4 in range(ngq):
                p.op("tensor", lambda e, q4=q4, cbase=cbase: e.transpose(
                    psf[2][0:1, 0:128],
                    scol[:, cbase + q4:cbase + q4 + 1], id_f[:, :]), sem_pe)
                p.wait("vector", sem_pe)
                p.op("vector", lambda e, q4=q4, which=which: e.tensor_copy(
                    rowA[0:1, 512 + which * 512 + q4 * 128:512 + which * 512 + (q4 + 1) * 128],
                    psf[2][0:1, 0:128]), sem_dv)
                p.wait("tensor", sem_dv)
        p.op("tensor", lambda e, twidth=twidth: e.matmul(
            psf[5][:, 0:twidth], ones_row_f[:, :], rowA[0:1, 512:512 + twidth],
            start=True, stop=True))
        p.op("tensor", lambda e, twidth=twidth: e.matmul(
            psf[6][:, 0:twidth], ones_row_f[:, :], rowA[0:1, 1024:1024 + twidth],
            start=True, stop=True), sem_pe)
        p.wait("vector", sem_pe)
        p.op("vector", lambda e, sBdst=sBdst, twidth=twidth: e.tensor_copy(
            sBdst[:, 0:twidth], psf[5][:, 0:twidth]))
        p.op("vector", lambda e, rcpdst=rcpdst, twidth=twidth: e.tensor_copy(
            rcpdst[:, 0:twidth], psf[6][:, 0:twidth]), sem_dv)

    def emit_quant_slices(cur, nslices, swidth, sB, dst_fn):
        for i in range(nslices):
            p.op("vector", lambda e, i=i, sw=swidth: e.tensor_tensor(
                qtmp1[:, 0:sw], cur(i), sB, ALU.mult))
            p.op("vector", lambda e, sw=swidth: e.tensor_scalar(
                qtmp2[:, 0:sw], qtmp1[:, 0:sw], 0.0, None, ALU.is_ge))
            p.op("vector", lambda e, sw=swidth: e.scalar_tensor_tensor(
                qtmp2[:, 0:sw], qtmp2[:, 0:sw], rnd_off[:, 0:1], qtmp1[:, 0:sw],
                ALU.mult, ALU.add))
            p.op("vector", lambda e, sw=swidth: e.tensor_scalar(
                qtmp2[:, 0:sw], qtmp2[:, 0:sw], rnd_off[0:1, 1:2], None, ALU.add))                 if False else None
            p.op("vector", lambda e, sw=swidth: e.tensor_scalar(
                qtmp2[:, 0:sw], qtmp2[:, 0:sw], rnd_off[:, 1:2], None, ALU.add))
            p.op("vector", lambda e, sw=swidth: e.tensor_copy(
                qi32[:, 0:sw], qtmp2[:, 0:sw]))
            p.op("vector", lambda e, i=i, sw=swidth: e.tensor_copy(
                dst_fn(i), qi32[:, 0:sw]), sem_dv)

    for tg in range(4):
        t0 = tg * 512
        p.wait("gpsimd", sem_pe)
        p.dma("gpsimd", x2st[:, :],
              bass.AP(x2_sp, t0, [[CH, 128], [128 * CH, FC], [1, 512]]), sem_g)
        p.wait("scalar", sem_g)
        p.wait("tensor", sem_g)
        pe_base = sem_pe.val
        for fc in range(FC):
            if fc >= 2:
                p.wait("scalar", sem_pe, pe_base + fc - 1)
            p.op("scalar", lambda e, fc=fc: e.activation(
                sq_parD[fc % 2][:, :], x2st[:, fc * 512:(fc + 1) * 512], AF.Square), sem_ac)
            p.wait("tensor", sem_ac)
            p.op("tensor", lambda e, fc=fc: e.matmul(
                psf[3][0:1, :], ones_col_bf[:, :], sq_parD[fc % 2][:, :],
                start=(fc == 0), stop=(fc == FC - 1)), sem_pe)
        for fc in range(FC):
            p.op("tensor", lambda e, fc=fc: e.matmul(
                psf[2][0:1, :], ones_col_bf[:, :], x2st[:, fc * 512:(fc + 1) * 512],
                start=(fc == 0), stop=(fc == FC - 1)),
                sem=(sem_pe if fc == FC - 1 else None))
        p.wait("vector", sem_pe)
        p.op("vector", lambda e: e.tensor_scalar(
            rowA[0:1, 0:512], psf[2][0:1, :], 1.0 / H, None, ALU.mult))
        p.op("vector", lambda e: e.tensor_scalar(
            rowB_[0:1, 0:512], psf[3][0:1, :], 1.0 / H, None, ALU.mult))
        p.op("vector", lambda e: e.tensor_tensor(
            rowC[0:1, 0:512], rowA[0:1, 0:512], rowA[0:1, 0:512], ALU.mult))
        p.op("vector", lambda e: e.tensor_tensor(
            rowB_[0:1, 0:512], rowB_[0:1, 0:512], rowC[0:1, 0:512], ALU.subtract))
        p.op("vector", lambda e: e.tensor_scalar(
            rowB_[0:1, 0:512], rowB_[0:1, 0:512], float(EPS_LN), None, ALU.add), sem_dv)
        p.wait("scalar", sem_dv)
        p.op("scalar", lambda e: e.activation(rowB_[0:1, 0:512], rowB_[0:1, 0:512], AF.Ln))
        p.op("scalar", lambda e: e.activation(
            rowC[0:1, 0:512], rowB_[0:1, 0:512], AF.Exp, scale=-0.5), sem_ac)
        p.wait("tensor", sem_ac)
        p.op("tensor", lambda e: e.matmul(
            psf[4][:, :], ones_row_f[:, :], rowA[0:1, 0:512], start=True, stop=True))
        p.op("tensor", lambda e: e.matmul(
            psf[5][:, :], ones_row_f[:, :], rowC[0:1, 0:512],
            start=True, stop=True), sem_pe)
        p.wait("vector", sem_pe)
        p.op("vector", lambda e: e.tensor_copy(d1bc1[:, :], psf[4][:, :]))
        p.op("vector", lambda e: e.tensor_copy(d1bc2[:, :], psf[5][:, :]), sem_dv)
        for fc in range(FC):
            p.wait("sync", sem_dv)
            p.dma("sync", qtmp1[:, :], x2_sp[fc * 128:(fc + 1) * 128, t0:t0 + 512], sem_x)
            p.wait("vector", sem_x)
            p.op("vector", lambda e: e.tensor_tensor(
                qtmp1[:, :], qtmp1[:, :], d1bc1[:, :], ALU.subtract))
            p.op("vector", lambda e, fc=fc: e.tensor_tensor(
                x2st[:, fc * 512:(fc + 1) * 512], qtmp1[:, :], d1bc2[:, :],
                ALU.mult), sem_dv)
            p.wait("tensor", sem_dv)
            p.op("tensor", lambda e, fc=fc: e.matmul(
                psf[fc % 2][:, :], h128_bf[:, :], x2st[:, fc * 512:(fc + 1) * 512],
                start=True, stop=True), sem_pe)
            p.wait("vector", sem_pe)
            p.op("vector", lambda e, fc=fc: e.tensor_copy(
                zbuf[:, fc * 512:(fc + 1) * 512], psf[fc % 2][:, :]), sem_dv)
        cur, oth = zbuf, zpong
        for st in range(4):
            stride = 1 << st
            npairs = FC // (2 * stride)
            def mk(tn, base, stride=stride, npairs=npairs):
                return bass.AP(tn, base,
                               [[FC * 512, 128], [2 * stride * 512, npairs],
                                [512, stride], [1, 512]])
            p.op("vector", lambda e, a=mk(cur, 0), b=mk(cur, stride * 512),
                 o=mk(oth, 0): e.tensor_tensor(o, a, b, ALU.add))
            p.op("vector", lambda e, a=mk(cur, 0), b=mk(cur, stride * 512),
                 o=mk(oth, stride * 512): e.tensor_tensor(o, a, b, ALU.subtract), sem_dv)
            cur, oth = oth, cur
        quant_scales(cur, 512, FC, s1Bt, d1bc1)
        p.op("vector", lambda e, t0=t0: e.tensor_copy(
            rcp1B[:, t0:t0 + 512], d1bc1[:, 0:512]), sem_dv)
        emit_quant_slices(
            lambda i, cur=cur: cur[:, i * 512:(i + 1) * 512], FC, 512, s1Bt[:, 0:512],
            lambda i, t0=t0: xq[:, i * CH + t0:i * CH + t0 + 512])
    p.barrier()
    if upto == "xq":
        if g.get("debug") == "qsc":
            dbg = outs["dbg"]
            p.chain_dma("sync", dbg[:, 0:8], scol[:, :])
            p.chain_dma("sync", dbg[0:1, 8:8 + 1024], rowA[0:1, 512:1536])
            p.chain_dma("sync", dbg[:, 2048:2560], s1Bt[:, :])
            p.chain_dma("sync", dbg[:, 2560:3072], qtmp1[:, :])
            p.chain_dma("sync", dbg[:, 3072:3584], d1bc1[:, :])
        return

    p.op("vector", lambda e: e.tensor_scalar(
        csgB[:, :], rcp1B[:, :], wscol[:, 0:1], None, ALU.mult))
    p.op("vector", lambda e: e.tensor_scalar(
        csuB[:, :], rcp1B[:, :], wscol[:, 1:2], None, ALU.mult), sem_dv)
    p.wait("scalar", sem_dv)
    tern_dv = sem_dv.val
    for om in range(OC):
        bw = om % 2
        p.wait("sync", sem_dv, tern_dv)
        p.dma("sync", wgu_par[bw][:, :],
              bass.AP(wgateT, om * 128, [[I8192, 128], [128 * I8192, FC], [1, 128]]),
              sem_x)
        p.dma("sync", wuu_par[bw][:, :],
              bass.AP(wupT, om * 128, [[I8192, 128], [128 * I8192, FC], [1, 128]]),
              sem_x)
        p.wait("vector", sem_x)
        p.wait("vector", sem_pe)
        for wsrc, tdst, ti in ((wgu_par[bw], terng, 0), (wuu_par[bw], ternu, 1)):
            p.op("vector", lambda e, w=wsrc, ti=ti: e.tensor_scalar(
                tb_tern[:, :], w[:, :], thrcol[:, 4 + ti:5 + ti], None, ALU.is_lt))
            p.op("vector", lambda e, w=wsrc, t=tdst, ti=ti: e.tensor_scalar(
                t[:, :], w[:, :], thrcol[:, ti:ti + 1], None, ALU.is_gt))
            p.op("vector", lambda e, t=tdst: e.tensor_tensor(
                t[:, :], t[:, :], tb_tern[:, :], ALU.subtract), sem_dv)
        tern_dv = sem_dv.val
        p.wait("tensor", sem_dv)
        for tg in range(4):
            for mat, tern, psb, cs in ((0, terng, 0, csgB), (1, ternu, 1, csuB)):
                p.wait("tensor", sem_dv)
                for fc in range(FC):
                    p.op("tensor", lambda e, tern=tern, fc=fc, tg=tg, psb=psb: e.matmul(
                        psf[psb][:, :], tern[:, fc * 128:(fc + 1) * 128],
                        xq[:, fc * CH + tg * 512:fc * CH + (tg + 1) * 512],
                        start=(fc == 0), stop=(fc == FC - 1)),
                        sem=(sem_pe if fc == FC - 1 else None))
            p.wait("vector", sem_pe)
            p.op("vector", lambda e, tg=tg: e.tensor_tensor(
                mtmp1[:, :], psf[0][:, :], csgB[:, tg * 512:(tg + 1) * 512], ALU.mult),
                sem_dv)
            p.wait("scalar", sem_dv)
            p.op("scalar", lambda e: e.activation(mtmp2[:, :], mtmp1[:, :], AF.Silu),
                 sem_ac)
            p.op("vector", lambda e, tg=tg: e.tensor_tensor(
                mtmp3[:, :], psf[1][:, :], csuB[:, tg * 512:(tg + 1) * 512], ALU.mult),
                sem_dv)
            p.wait("vector", sem_ac)
            p.wait("vector", sem_g)
            p.op("vector", lambda e, tg=tg: e.tensor_tensor(
                mid_par[tg % 2][:, :], mtmp2[:, :], mtmp3[:, :], ALU.mult), sem_dv)
            p.wait("gpsimd", sem_dv)
            p.dma("gpsimd", mid_sp[om * 128:(om + 1) * 128, tg * 512:(tg + 1) * 512],
                  mid_par[tg % 2][:, :], sem_g)
    p.barrier()
    if upto == "mid":
        return

    dv0 = sem_dv.val
    x0 = sem_x.val
    g0 = sem_g.val
    for oc in range(OC):
        b = oc % 2
        res = ternd_a if b == 0 else ternd_b
        p.wait("sync", sem_dv, dv0 + max(0, oc - 1))
        p.dma("sync", wdf_par[b][:, :], wdownT[oc * 128:(oc + 1) * 128, :], sem_x)
        p.wait("vector", sem_x, x0 + 16 * (oc + 1))
        p.wait("vector", sem_g, g0 + 16 * max(0, oc - 1))
        p.op("vector", lambda e, b=b: e.tensor_scalar(
            tb_tern[:, :], wdf_par[b][:, :], thrcol[:, 6:7], None, ALU.is_lt))
        p.op("vector", lambda e, b=b, res=res: e.tensor_scalar(
            res[:, :], wdf_par[b][:, :], thrcol[:, 2:3], None, ALU.is_gt))
        p.op("vector", lambda e, res=res: e.tensor_tensor(
            res[:, :], res[:, :], tb_tern[:, :], ALU.subtract), sem_dv)
        p.wait("gpsimd", sem_dv, dv0 + oc + 1)
        p.dma("gpsimd", wdt_sp[oc * 128:(oc + 1) * 128, :], res[:, :], sem_g)
    p.barrier()

    for tg in range(4):
        for sub in range(2):
            t0 = tg * 512 + sub * 256
            p.wait("gpsimd", sem_pe)
            p.dma("gpsimd", bufA[:, :],
                  bass.AP(mid_sp, t0, [[CH, 128], [128 * CH, OC], [1, 256]]), sem_g)
            p.wait("tensor", sem_g)
            for oc in range(OC):
                p.wait("tensor", sem_dv)
                p.op("tensor", lambda e, oc=oc: e.matmul(
                    psf[oc % 2][:, 0:256], h128_bf[:, :], bufA[:, oc * 256:(oc + 1) * 256],
                    start=True, stop=True), sem_pe)
                p.wait("vector", sem_pe)
                p.op("vector", lambda e, oc=oc: e.tensor_copy(
                    bufB[:, oc * 256:(oc + 1) * 256], psf[oc % 2][:, 0:256]), sem_dv)
            cur, oth = bufB, bufA
            for st in range(6):
                stride = 1 << st
                npairs = OC // (2 * stride)
                def mk(tn, base, stride=stride, npairs=npairs):
                    return bass.AP(tn, base,
                                   [[OC * 256, 128], [2 * stride * 256, npairs],
                                    [256, stride], [1, 256]])
                p.op("vector", lambda e, a=mk(cur, 0), b=mk(cur, stride * 256),
                     o=mk(oth, 0): e.tensor_tensor(o, a, b, ALU.add))
                p.op("vector", lambda e, a=mk(cur, 0), b=mk(cur, stride * 256),
                     o=mk(oth, stride * 256): e.tensor_tensor(o, a, b, ALU.subtract),
                     sem_dv)
                cur, oth = oth, cur
            quant_scales(cur, 256, OC, s2Bt, rcp2Bt)
            p.op("vector", lambda e: e.tensor_scalar(
                csdBt[:, :], rcp2Bt[:, 0:256], wscol[:, 2:3], None, ALU.mult), sem_dv)
            emit_quant_slices(
                lambda i, cur=cur: cur[:, i * 256:(i + 1) * 256], OC, 256, s2Bt[:, 0:256],
                lambda i, oth=oth: oth[:, i * 256:(i + 1) * 256])
            midq = oth
            p.wait("gpsimd", sem_dv)
            for ft in range(FC):
                bank = 2 + (ft % 4)
                p.wait("gpsimd", sem_pe)
                p.dma("gpsimd", wdt_ft[:, :],
                      bass.AP(wdt_sp, ft * 128, [[H, 128], [128 * H, OC], [1, 128]]),
                      sem_g)
                p.wait("tensor", sem_g)
                p.wait("tensor", sem_dv)
                for oc in range(OC):
                    p.op("tensor", lambda e, oc=oc, bank=bank: e.matmul(
                        psf[bank][:, 0:256], wdt_ft[:, oc * 128:(oc + 1) * 128],
                        midq[:, oc * 256:(oc + 1) * 256],
                        start=(oc == 0), stop=(oc == OC - 1)),
                        sem=(sem_pe if oc == OC - 1 else None))
                p.wait("vector", sem_pe)
                p.op("vector", lambda e, ft=ft, bank=bank: e.tensor_tensor(
                    y4_blk[:, ft * 256:(ft + 1) * 256], psf[bank][:, 0:256],
                    csdBt[:, :], ALU.mult))
                p.op("vector", lambda e, ft=ft: e.tensor_copy(
                    ybf4[:, ft * 256:(ft + 1) * 256], y4_blk[:, ft * 256:(ft + 1) * 256]),
                    sem_dv)
            p.wait("scalar", sem_dv)
            p.wait("tensor", sem_dv)
            pe4 = sem_pe.val
            for fc in range(FC):
                if fc >= 2:
                    p.wait("scalar", sem_pe, pe4 + fc - 1)
                p.op("scalar", lambda e, fc=fc: e.activation(
                    sq_parD[fc % 2][:, 0:256], ybf4[:, fc * 256:(fc + 1) * 256],
                    AF.Square), sem_ac)
                p.wait("tensor", sem_ac)
                p.op("tensor", lambda e, fc=fc: e.matmul(
                    psf[0][0:1, 0:256], ones_col_bf[:, :], sq_parD[fc % 2][:, 0:256],
                    start=(fc == 0), stop=(fc == FC - 1)), sem_pe)
            for fc in range(FC):
                p.op("tensor", lambda e, fc=fc: e.matmul(
                    psf[1][0:1, 0:256], ones_col_bf[:, :],
                    ybf4[:, fc * 256:(fc + 1) * 256],
                    start=(fc == 0), stop=(fc == FC - 1)),
                    sem=(sem_pe if fc == FC - 1 else None))
            p.wait("vector", sem_pe)
            p.op("vector", lambda e: e.tensor_scalar(
                rowA[0:1, 0:256], psf[1][0:1, 0:256], 1.0 / H, None, ALU.mult))
            p.op("vector", lambda e: e.tensor_scalar(
                rowB_[0:1, 0:256], psf[0][0:1, 0:256], 1.0 / H, None, ALU.mult))
            p.op("vector", lambda e: e.tensor_tensor(
                rowC[0:1, 0:256], rowA[0:1, 0:256], rowA[0:1, 0:256], ALU.mult))
            p.op("vector", lambda e: e.tensor_tensor(
                rowB_[0:1, 0:256], rowB_[0:1, 0:256], rowC[0:1, 0:256], ALU.subtract))
            p.op("vector", lambda e: e.tensor_scalar(
                rowB_[0:1, 0:256], rowB_[0:1, 0:256], float(EPS_LN), None, ALU.add),
                sem_dv)
            p.wait("scalar", sem_dv)
            p.op("scalar", lambda e: e.activation(
                rowB_[0:1, 0:256], rowB_[0:1, 0:256], AF.Ln))
            p.op("scalar", lambda e: e.activation(
                rowC[0:1, 0:256], rowB_[0:1, 0:256], AF.Exp, scale=-0.5), sem_ac)
            p.wait("tensor", sem_ac)
            p.wait("tensor", sem_dv)
            p.op("tensor", lambda e: e.matmul(
                psf[0][:, 0:256], ones_row_f[:, :], rowA[0:1, 0:256],
                start=True, stop=True))
            p.op("tensor", lambda e: e.matmul(
                psf[0][:, 256:512], ones_row_f[:, :], rowC[0:1, 0:256],
                start=True, stop=True), sem_pe)
            p.wait("vector", sem_pe)
            p.op("vector", lambda e: e.tensor_copy(e4bc[:, 0:256], psf[0][:, 0:256]))
            p.op("vector", lambda e: e.tensor_copy(e4bc[:, 256:512], psf[0][:, 256:512]),
                 sem_dv)
            for fc in range(FC):
                p.wait("sync", sem_dv)
                p.dma("sync", qtmp1[:, 0:256],
                      x2_sp[fc * 128:(fc + 1) * 128, t0:t0 + 256], sem_x)
                p.wait("vector", sem_x)
                p.wait("vector", sem_st)
                p.op("vector", lambda e, fc=fc: e.tensor_tensor(
                    qtmp2[:, 0:256], y4_blk[:, fc * 256:(fc + 1) * 256],
                    e4bc[:, 0:256], ALU.subtract))
                p.op("vector", lambda e: e.tensor_tensor(
                    qtmp2[:, 0:256], qtmp2[:, 0:256], e4bc[:, 256:512], ALU.mult))
                p.op("vector", lambda e: e.tensor_tensor(
                    qtmp2[:, 0:256], qtmp2[:, 0:256], qtmp1[:, 0:256], ALU.add), sem_dv)
                p.wait("sync", sem_dv)
                p.dma("sync", outT[fc * 128:(fc + 1) * 128, t0:t0 + 256],
                      qtmp2[:, 0:256], sem_st)
    p.wait("sync", sem_st)


def _finish(nc, p, stack):
    p.barrier()
    with nc.Block() as block:
        @block.tensor
        def _(e):
            for f in p.ops["tensor"]:
                f(e)

        @block.vector
        def _(e):
            for f in p.ops["vector"]:
                f(e)

        @block.scalar
        def _(e):
            for f in p.ops["scalar"]:
                f(e)

        @block.gpsimd
        def _(e):
            for f in p.ops["gpsimd"]:
                f(e)

        @block.sync
        def _(e):
            for f in p.ops["sync"]:
                f(e)
    stack.close()
    return nc


def _hmat(n):
    Hm = np.array([[1.0]], np.float32)
    while Hm.shape[0] < n:
        Hm = np.block([[Hm, Hm], [Hm, -Hm]]).astype(np.float32)
    return Hm


def _mt():
    i = np.arange(1, 257, dtype=np.float32)
    d = i[None, :] - i[:, None]
    return np.where(d >= 0, d, np.float32(1e9)).astype(np.float32)


def host_prep(inputs):
    f32 = np.float32
    x = np.asarray(inputs["L0_attn_hidden_states"], f32)[0]
    mt = _mt()
    cons = {
        "wqkvT": np.ascontiguousarray(np.asarray(inputs["L0_attn_w_qkv"], f32).T),
        "wgT": np.ascontiguousarray(np.asarray(inputs["L0_attn_w_gate"], f32).T),
        "woT": np.ascontiguousarray(np.asarray(inputs["L0_attn_w_out"], f32).T),
        "normw": np.ascontiguousarray(np.asarray(inputs["L0_attn_norm_w"], f32).reshape(FC, 128).T),
        "wgateT": np.ascontiguousarray(np.asarray(inputs["L0_ffn_w_gate"], f32).T),
        "wupT": np.ascontiguousarray(np.asarray(inputs["L0_ffn_w_up"], f32).T),
        "wdownT": np.ascontiguousarray(np.asarray(inputs["L0_ffn_w_down"], f32).T),
        "slope": np.asarray(inputs["L0_attn_slope_rate"], f32).reshape(1, NH),
        "h128": _hmat(128),
        "mtA": np.ascontiguousarray(mt[0:128]),
        "mtB": np.ascontiguousarray(mt[128:256]),
        "arange1": np.arange(1, 257, dtype=f32).reshape(1, 256),
        "revrow": (255.0 - np.arange(256, dtype=f32)).reshape(1, 256),
        "onesrow": np.ones((1, 128), f32),
        "ident": np.eye(128, dtype=f32),
    }
    per_core = []
    for c in range(NCORES):
        m = dict(cons)
        xc = x[c * CH:(c + 1) * CH]
        m["xT"] = np.ascontiguousarray(xc.T)
        per_core.append(m)
    return per_core, x


def kernel(**inputs):
    per_core, x = host_prep(inputs)
    ncF = build_program("fused")
    WW = WARM_NB * BLK
    for c in range(NCORES):
        xw = np.zeros((WW, H), np.float32)
        take = min(WW, c * CH)
        if take > 0:
            xw[WW - take:] = x[c * CH - take:c * CH]
        per_core[c]["xwinT"] = np.ascontiguousarray(xw.T)
    res = run_bass_kernel_spmd(ncF, per_core, core_ids=list(range(NCORES)))
    out = np.concatenate([np.asarray(r["outT"]).T for r in res.results], axis=0)
    return out[None].astype(np.float32)


def kernel_v1(**inputs):
    per_core, x = host_prep(inputs)
    ncA = build_program("v1a")
    resA = run_bass_kernel_spmd(ncA, per_core, core_ids=list(range(NCORES)))
    Sall = np.concatenate([np.asarray(r["S"]) for r in resA.results], axis=0)
    ncB = build_program("v1b")
    for c in range(NCORES):
        per_core[c]["Sg"] = Sall
        dv = np.full((1, NCORES), 1e9, np.float32)
        for cp in range(c):
            dv[0, cp] = float(c - 1 - cp)
        per_core[c]["dvec"] = dv
    resB = run_bass_kernel_spmd(ncB, per_core, core_ids=list(range(NCORES)))
    out = np.concatenate([np.asarray(r["outT"]).T for r in resB.results], axis=0)
    return out[None].astype(np.float32)


# version 09 (final)
# speedup vs baseline: 1.0463x; 1.0123x over previous
import math
from contextlib import ExitStack
import numpy as np
import ml_dtypes
import concourse.bass as bass
import concourse.mybir as mybir
from concourse.bass_utils import run_bass_kernel_spmd

F32 = mybir.dt.float32
BF16 = mybir.dt.bfloat16
I32 = mybir.dt.int32
AF = mybir.ActivationFunctionType
ALU = mybir.AluOpType
AX = mybir.AxisListType
BFnp = ml_dtypes.bfloat16

NCORES = 8
H = 2048
NH, HD = 16, 128
BLK = 256
CH = 2048
NB = CH // BLK
FC = H // 128
I8192 = 8192
OC = I8192 // 128
EPS_LN = 1e-5
EPS_ATTN = 1e-6
EPS_FFN = 1e-5

SLOPES = [2.0 ** (-(h + 1) / 2.0) for h in range(NH)]
WARM_DEPTH = [min(56, int(math.ceil(103.6 / (256.0 * s)))) for s in SLOPES]
WARM_NB = 56


class Sem:
    def __init__(self, handle, name):
        self.h = handle
        self.name = name
        self.val = 0


class Prog:
    ENGS = ("tensor", "vector", "scalar", "gpsimd", "sync")

    def __init__(self, nc, stack):
        self.nc = nc
        self.ops = {e: [] for e in self.ENGS}
        self._sems = {}
        self._stack = stack

    def sem(self, name):
        if name not in self._sems:
            handle = self._stack.enter_context(self.nc.semaphore(name))
            self._sems[name] = Sem(handle, name)
        return self._sems[name]

    def emit(self, eng, fn):
        self.ops[eng].append(fn)

    def dma(self, eng, out, in_, sem=None, **kw):
        if sem is not None:
            sem.val += 16
            self.emit(eng, lambda e, o=out, i=in_, s=sem, kw=kw:
                      e.dma_start(out=o, in_=i, **kw).then_inc(s.h, 16))
        else:
            self.emit(eng, lambda e, o=out, i=in_, kw=kw: e.dma_start(out=o, in_=i, **kw))

    def wait(self, eng, sem, val=None):
        v = sem.val if val is None else val
        self.emit(eng, lambda e, s=sem, v=v: e.wait_ge(s.h, v))

    def op(self, eng, fn, sem=None, inc=1):
        if sem is not None:
            sem.val += inc
            self.emit(eng, lambda e, f=fn, s=sem, k=inc: f(e).then_inc(s.h, k))
        else:
            self.emit(eng, fn)

    def barrier(self):
        snap = [(s, s.val) for s in self._sems.values() if s.val > 0]
        for eng in self.ENGS:
            for s, v in snap:
                self.emit(eng, lambda e, s=s, v=v: e.wait_ge(s.h, v))

    def chain_op(self, eng, fn):
        s = self.sem("chain")
        prev = s.val
        s.val += 1
        self.emit(eng, lambda e, s=s, v=prev: e.wait_ge(s.h, v))
        self.emit(eng, lambda e, f=fn, s=s: f(e).then_inc(s.h, 1))

    def chain_dma(self, eng, out, in_, **kw):
        s = self.sem("chain")
        prev = s.val
        s.val += 16
        self.emit(eng, lambda e, s=s, v=prev: e.wait_ge(s.h, v))
        self.emit(eng, lambda e, o=out, i=in_, s=s, kw=kw:
                  e.dma_start(out=o, in_=i, **kw).then_inc(s.h, 16))

    def chain_wait_all(self, eng):
        s = self.sem("chain")
        self.emit(eng, lambda e, s=s, v=s.val: e.wait_ge(s.h, v))


class SAlloc:
    def __init__(self, nc):
        self.nc = nc
        self.regions = {}
        self.cursor = {}

    def region(self, name, base, size):
        self.regions[name] = (base, size)
        self.cursor[name] = base

    def reset(self, name):
        self.cursor[name] = self.regions[name][0]

    def alloc(self, name, reg, shape, dtype, align=64):
        esz = {F32: 4, BF16: 2, I32: 4}[dtype]
        nb = shape[-1] * esz
        cur = (self.cursor[reg] + align - 1) // align * align
        base, size = self.regions[reg]
        if cur + nb > base + size:
            raise MemoryError(
                f"SBUF region {reg} overflow: {name} needs {nb}B at rel {cur - base}, size {size}")
        self.cursor[reg] = cur + nb
        return self.nc.alloc_sbuf_tensor_at(name, shape, dtype, offset=cur)


def build_program(mode, upto="full", debug=None):
    nc = bass.Bass("TRN2", target_bir_lowering=False)
    stack = ExitStack()
    p = Prog(nc, stack)

    xT = nc.dram_tensor("xT", [H, CH], F32, kind="ExternalInput")
    wqkvT = nc.dram_tensor("wqkvT", [H, 3 * NH * HD], F32, kind="ExternalInput")
    wgT = nc.dram_tensor("wgT", [H, H], F32, kind="ExternalInput")
    woT = nc.dram_tensor("woT", [H, H], F32, kind="ExternalInput")
    normw = nc.dram_tensor("normw", [128, FC], F32, kind="ExternalInput")
    wgateT = nc.dram_tensor("wgateT", [H, I8192], F32, kind="ExternalInput")
    wupT = nc.dram_tensor("wupT", [H, I8192], F32, kind="ExternalInput")
    wdownT = nc.dram_tensor("wdownT", [I8192, H], F32, kind="ExternalInput")
    slope = nc.dram_tensor("slope", [1, NH], F32, kind="ExternalInput")
    h128 = nc.dram_tensor("h128", [128, 128], F32, kind="ExternalInput")
    mtA = nc.dram_tensor("mtA", [128, 256], F32, kind="ExternalInput")
    mtB = nc.dram_tensor("mtB", [128, 256], F32, kind="ExternalInput")
    arange1 = nc.dram_tensor("arange1", [1, 256], F32, kind="ExternalInput")
    revrow = nc.dram_tensor("revrow", [1, 256], F32, kind="ExternalInput")
    onesrow = nc.dram_tensor("onesrow", [1, 128], F32, kind="ExternalInput")
    ident = nc.dram_tensor("ident", [128, 128], F32, kind="ExternalInput")

    outs = {}
    if mode == "v1a":
        outs["S"] = nc.dram_tensor("S", [NH * HD, HD], F32, kind="ExternalOutput")
    else:
        outs["outT"] = nc.dram_tensor("outT", [H, CH], F32, kind="ExternalOutput")
    Sg = dvec = xwinT = None
    if mode == "v1b":
        Sg = nc.dram_tensor("Sg", [NCORES * NH * HD, HD], F32, kind="ExternalInput")
        dvec = nc.dram_tensor("dvec", [1, NCORES], F32, kind="ExternalInput")
    if mode == "fused":
        xwinT = nc.dram_tensor("xwinT", [H, WARM_NB * BLK], F32, kind="ExternalInput")
    gate_sp = nc.dram_tensor("gate_sp", [H, CH], BF16)
    x2_sp = nc.dram_tensor("x2_sp", [H, CH], F32)
    mid_sp = nc.dram_tensor("mid_sp", [I8192, CH], BF16)
    wdt_sp = nc.dram_tensor("wdt_sp", [I8192, H], BF16)

    if debug:
        dbg = nc.dram_tensor("dbg", [128, FC * CH], F32, kind="ExternalOutput")
        outs["dbg"] = dbg

    psf = [nc.alloc_psum_tensor(f"psf{i}", [128, 512], F32) for i in range(7)]
    pst = nc.alloc_psum_tensor("pst", [128, 1024], BF16)

    sa = SAlloc(nc)
    SB0 = (nc.sbuf_base + 63) // 64 * 64
    SBTOP = nc.sbuf_top
    cur = SB0
    for nm, sz in (("CONST", 8 * 1024), ("ROWS", 24 * 1024), ("ATTN", 15 * 1024),
                   ("A", 64 * 1024), ("B", 64 * 1024)):
        sa.region(nm, cur, sz)
        cur += sz
    sa.region("W", cur, SBTOP - cur)

    ones_col_bf = sa.alloc("ones_col_bf", "CONST", [128, 1], BF16)
    ones_col_f = sa.alloc("ones_col_f", "CONST", [128, 1], F32)
    ones_row_f = sa.alloc("ones_row_f", "CONST", [1, 128], F32)
    id_bf = sa.alloc("id_bf", "CONST", [128, 128], BF16)
    id_f = sa.alloc("id_f", "CONST", [128, 128], F32)
    h128_bf = sa.alloc("h128_bf", "CONST", [128, 128], BF16)
    slope_s = sa.alloc("slope_s", "CONST", [1, NH], F32)
    slopeB = sa.alloc("slopeB", "CONST", [128, NH], F32)
    lamB = sa.alloc("lamB", "CONST", [128, NH], F32)
    normw_s = sa.alloc("normw_s", "CONST", [128, FC], F32)
    mt_s = sa.alloc("mt_s", "CONST", [128, 2 * 256], F32)
    ar_s = sa.alloc("ar_s", "CONST", [1, 256], F32)
    rv_s = sa.alloc("rv_s", "CONST", [1, 256], F32)
    rnd_off = sa.alloc("rnd_off", "CONST", [128, 2], F32)
    tmp_i32c = sa.alloc("tmp_i32c", "CONST", [128, 2], I32)
    wsrow = sa.alloc("wsrow", "CONST", [1, 8], F32)
    wscol = sa.alloc("wscol", "CONST", [128, 4], F32)
    thrcol = sa.alloc("thrcol", "CONST", [128, 8], F32)
    dv_s = sa.alloc("dv_s", "CONST", [1, NCORES], F32)
    wB = sa.alloc("wB", "CONST", [128, NH], F32)

    rowA = sa.alloc("rowA", "ROWS", [1, CH], F32)
    rowB_ = sa.alloc("rowB_", "ROWS", [1, CH], F32)
    rowC = sa.alloc("rowC", "ROWS", [1, CH], F32)

    kv_all = sa.alloc("kv_all", "ATTN", [128, NH * 128], F32)
    kv_bf = [sa.alloc(f"kv_bf{i}", "ATTN", [128, 128], BF16) for i in range(2)]
    qd_t = [sa.alloc(f"qd_t{i}", "ATTN", [128, 256], BF16) for i in range(2)]
    kd_t = [sa.alloc(f"kd_t{i}", "ATTN", [128, 256], BF16) for i in range(2)]
    vT_s = sa.alloc("vT_s", "ATTN", [128, 256], BF16)
    kdT_s = sa.alloc("kdT_s", "ATTN", [128, 256], BF16)
    qkd_s = sa.alloc("qkd_s", "ATTN", [128, 512], BF16)
    qdecB_h = sa.alloc("qdecB_h", "ATTN", [128, 256], BF16)
    kdecB_h = sa.alloc("kdecB_h", "ATTN", [128, 256], BF16)
    ddT_h = sa.alloc("ddT_h", "ATTN", [128, 512], BF16)
    sa.reset("ATTN")
    _attn_hold = sa.alloc("_attn_hold", "ATTN", [128, NH * 128], F32)
    sa.reset("ATTN")
    ybf_blk = sa.alloc("ybf_blk", "ATTN", [128, FC * 256], BF16)

    lnx = sa.alloc("lnx", "A", [128, FC * CH], BF16)
    sa.reset("A")
    woT_bf = sa.alloc("woT_bf", "A", [128, FC * CH], BF16)
    sa.reset("A")
    xq = sa.alloc("xq", "A", [128, FC * CH], BF16)
    sa.reset("A")
    y4_blk = sa.alloc("y4_blk", "A", [128, FC * 256], F32)
    ybf4 = sa.alloc("ybf4", "A", [128, FC * 256], BF16)
    wdf_par = [sa.alloc(f"wdf_par{i}", "A", [128, 2048], F32) for i in range(2)]
    ternd_a = sa.alloc("ternd_a", "A", [128, 2048], BF16)
    ternd_b = sa.alloc("ternd_b", "A", [128, 2048], BF16)
    wdt_ft = sa.alloc("wdt_ft", "A", [128, OC * 128], BF16)

    muB = sa.alloc("muB", "B", [128, CH], F32)
    rB = sa.alloc("rB", "B", [128, CH], F32)
    kdec_all = sa.alloc("kdec_all", "B", [128, NH * 256], BF16)
    sa.reset("B")
    o_acc = sa.alloc("o_acc", "B", [128, NH * CH], BF16)
    sa.reset("B")
    x2st = sa.alloc("x2st", "B", [128, FC * 512], BF16)
    zbuf = sa.alloc("zbuf", "B", [128, FC * 512], BF16)
    zpong = sa.alloc("zpong", "B", [128, FC * 512], BF16)
    d1bc1 = sa.alloc("d1bc1", "B", [128, 512], F32)
    d1bc2 = sa.alloc("d1bc2", "B", [128, 512], F32)
    s1Bt = sa.alloc("s1Bt", "B", [128, 512], F32)
    rcp1B = sa.alloc("rcp1B", "B", [128, CH], F32)
    sa.reset("B")
    bufA = sa.alloc("bufA", "B", [128, OC * 256], BF16)
    bufB = sa.alloc("bufB", "B", [128, OC * 256], BF16)
    sa.reset("B")
    terng = sa.alloc("terng", "B", [128, 2048], BF16)
    ternu = sa.alloc("ternu", "B", [128, 2048], BF16)
    csgB = sa.alloc("csgB", "B", [128, CH], F32)
    csuB = sa.alloc("csuB", "B", [128, CH], F32)
    tb_tern = sa.alloc("tb_tern", "B", [128, 2048], BF16)
    wgu_par = [sa.alloc(f"wgu_par{i}", "B", [128, FC * 128], F32) for i in range(1)] * 2
    wuu_par = [sa.alloc(f"wuu_par{i}", "B", [128, FC * 128], F32) for i in range(1)] * 2
    mtmp1 = sa.alloc("mtmp1", "B", [128, 512], F32)
    mtmp2 = sa.alloc("mtmp2", "B", [128, 512], F32)
    mtmp3 = sa.alloc("mtmp3", "B", [128, 512], F32)
    mid_par = [sa.alloc(f"mid_par{i}", "B", [128, 512], BF16) for i in range(2)]

    xb_f = sa.alloc("xb_f", "W", [128, CH], F32)
    sq_par = [sa.alloc(f"sq_par{i}", "W", [128, 512], BF16) for i in range(2)]
    sgt = sa.alloc("sgt", "W", [128, 128], F32)
    wg_sl = [sa.alloc(f"wg_sl{i}", "W", [128, FC * 128], BF16) for i in range(2)]
    sa.reset("W")
    wslice = sa.alloc("wslice", "W", [128, FC * 384], BF16)
    qkv_h = sa.alloc("qkv_h", "W", [128, 3 * CH], BF16)
    sq_parA = [sa.alloc(f"sq_parA{i}", "W", [128, 512], BF16) for i in range(2)]
    ddscr = sa.alloc("ddscr", "W", [128, 256], F32)
    sa.reset("W")
    gate_blk = sa.alloc("gate_blk", "W", [128, FC * 256], BF16)
    p_bf = sa.alloc("p_bf", "W", [128, FC * 256], BF16)
    sq_parE = [sa.alloc(f"sq_parE{i}", "W", [128, 256], BF16) for i in range(2)]
    bcq1 = sa.alloc("bcq1", "W", [128, 256], F32)
    bcq2 = sa.alloc("bcq2", "W", [128, 512], F32)
    ptmp = sa.alloc("ptmp", "W", [128, 256], F32)
    x4_par = [sa.alloc(f"x4_par{i}", "W", [128, 256], F32) for i in range(2)]
    o4_par = [sa.alloc(f"o4_par{i}", "W", [128, 256], F32) for i in range(2)]
    sa.reset("W")
    wstr = [sa.alloc(f"wstr{i}", "W", [128, 2048], F32) for i in range(2)]
    redcols = sa.alloc("redcols", "W", [128, 64], F32)
    qtmp1 = sa.alloc("qtmp1", "W", [128, 512], F32)
    qtmp2 = sa.alloc("qtmp2", "W", [128, 512], F32)
    qi32 = sa.alloc("qi32", "W", [128, 512], I32)
    scol = sa.alloc("scol", "W", [128, 8], F32)
    sq_parD = [sa.alloc(f"sq_parD{i}", "W", [128, 512], BF16) for i in range(2)]
    s2Bt = sa.alloc("s2Bt", "W", [128, 512], F32)
    rcp2Bt = sa.alloc("rcp2Bt", "W", [128, 512], F32)
    csdBt = sa.alloc("csdBt", "W", [128, 256], F32)
    e4bc = sa.alloc("e4bc", "W", [128, 512], F32)

    MAGIC = float(np.float32(12582912.0))

    sem_x = p.sem("sem_x")
    sem_g = p.sem("sem_g")
    sem_pe = p.sem("sem_pe")
    sem_dv = p.sem("sem_dv")
    sem_ac = p.sem("sem_ac")
    sem_pl = p.sem("sem_pl")
    sem_st = p.sem("sem_st")

    p.chain_dma("sync", ones_row_f[:, :], onesrow[:, :])
    p.chain_dma("sync", slope_s[:, :], slope[:, :])
    p.chain_dma("sync", mt_s[:, 0:256], mtA[:, :])
    p.chain_dma("sync", mt_s[:, 256:512], mtB[:, :])
    p.chain_dma("sync", ar_s[:, :], arange1[:, :])
    p.chain_dma("sync", rv_s[:, :], revrow[:, :])
    p.chain_dma("sync", normw_s[:, :], normw[:, :])
    p.chain_dma("sync", id_f[:, :], ident[:, :])
    p.chain_dma("gpsimd", ones_col_bf[:, :], onesrow.ap().rearrange("1 p -> p 1"))
    p.chain_dma("sync", ones_col_f[:, :], onesrow.ap().rearrange("1 p -> p 1"))
    p.chain_dma("gpsimd", id_bf[:, :], ident[:, :])
    p.chain_dma("gpsimd", h128_bf[:, :], h128[:, :])
    p.chain_op("tensor", lambda e: e.matmul(psf[0][:, 0:NH], ones_row_f[:, :],
                                            slope_s[:, :], start=True, stop=True))
    p.chain_op("vector", lambda e: e.tensor_copy(slopeB[:, :], psf[0][:, 0:NH]))
    p.chain_op("vector", lambda e: e.memset(rnd_off[:, 0:1], 0.7))
    p.chain_op("vector", lambda e: e.memset(kv_all[:, 0:NH * 64], 0.0))
    p.chain_op("vector", lambda e: e.tensor_copy(tmp_i32c[:, 0:1], rnd_off[:, 0:1]))
    p.chain_op("vector", lambda e: e.memset(kv_all[:, NH * 64:NH * 128], 0.0))
    p.chain_op("vector", lambda e: e.tensor_copy(rnd_off[:, 0:1], tmp_i32c[:, 0:1]))
    p.chain_op("scalar", lambda e: e.activation(rowA[0:1, 0:NH], slope_s[:, :],
                                                AF.Exp, scale=-float(BLK)))
    p.chain_op("vector", lambda e: e.tensor_scalar(rnd_off[:, 0:1], rnd_off[:, 0:1],
                                                   -1.0, 1.0, ALU.mult, ALU.add))
    p.chain_op("tensor", lambda e: e.matmul(psf[1][:, 0:NH], ones_row_f[:, :],
                                            rowA[0:1, 0:NH], start=True, stop=True))
    p.chain_op("vector", lambda e: e.tensor_scalar(rnd_off[:, 1:2], rnd_off[:, 0:1],
                                                   -0.5, None, ALU.mult))
    p.chain_op("vector", lambda e: e.tensor_copy(lamB[:, :], psf[1][:, 0:NH]))
    if mode == "v1b":
        p.chain_dma("sync", dv_s[:, :], dvec[:, :])
        for cp in range(NCORES):
            p.chain_op("vector", lambda e, cp=cp: e.tensor_scalar(
                rowA[0:1, 0:NH], slope_s[:, :], dv_s[0:1, cp:cp + 1], None, ALU.mult))
            p.chain_op("scalar", lambda e: e.activation(
                rowA[0:1, 256:256 + NH], rowA[0:1, 0:NH], AF.Exp, scale=-2048.0))
            p.chain_op("tensor", lambda e: e.matmul(
                psf[0][:, 0:NH], ones_row_f[:, :], rowA[0:1, 256:256 + NH],
                start=True, stop=True))
            p.chain_op("vector", lambda e: e.tensor_copy(wB[:, :], psf[0][:, 0:NH]))
            for h in range(NH):
                p.chain_dma("sync", sgt[:, :],
                            Sg[cp * NH * HD + h * HD:cp * NH * HD + (h + 1) * HD, :])
                p.chain_op("vector", lambda e, h=h: e.scalar_tensor_tensor(
                    kv_all[:, h * 128:(h + 1) * 128], sgt[:, :], wB[:, h:h + 1],
                    kv_all[:, h * 128:(h + 1) * 128], ALU.mult, ALU.add))
    for eng in Prog.ENGS:
        p.chain_wait_all(eng)

    def emit_ln_sweep(x_src, lnx_dst, tokens, eps, reuse_staged=False):
        ngr = tokens // 512
        for fc in range(FC):
            p.dma("gpsimd", lnx_dst[:, fc * tokens:(fc + 1) * tokens],
                  x_src(fc), sem_g)
        p.wait("scalar", sem_g)
        p.wait("tensor", sem_g)
        for g in range(ngr):
            bs = (2 * g) % 6
            p.wait("tensor", sem_dv)
            pe_base = sem_pe.val
            for fc in range(FC):
                if fc >= 2:
                    p.wait("scalar", sem_pe, pe_base + fc - 1)
                p.op("scalar", lambda e, g=g, fc=fc, t=tokens: e.activation(
                    sq_par[fc % 2][:, :],
                    lnx_dst[:, fc * t + g * 512:fc * t + (g + 1) * 512],
                    AF.Square), sem_ac)
                p.wait("tensor", sem_ac)
                p.op("tensor", lambda e, fc=fc, bs=bs: e.matmul(
                    psf[bs + 1][0:1, :], ones_col_bf[:, :], sq_par[fc % 2][:, :],
                    start=(fc == 0), stop=(fc == FC - 1)), sem_pe)
            for fc in range(FC):
                p.op("tensor", lambda e, g=g, fc=fc, bs=bs, t=tokens: e.matmul(
                    psf[bs][0:1, :], ones_col_bf[:, :],
                    lnx_dst[:, fc * t + g * 512:fc * t + (g + 1) * 512],
                    start=(fc == 0), stop=(fc == FC - 1)),
                    sem=(sem_pe if fc == FC - 1 else None))
            p.wait("vector", sem_pe)
            p.op("vector", lambda e, g=g, bs=bs: e.tensor_scalar(
                rowA[0:1, g * 512:(g + 1) * 512], psf[bs][0:1, :],
                1.0 / H, None, ALU.mult))
            p.op("vector", lambda e, g=g, bs=bs: e.tensor_scalar(
                rowB_[0:1, g * 512:(g + 1) * 512], psf[bs + 1][0:1, :],
                1.0 / H, None, ALU.mult), sem_dv)
        p.op("vector", lambda e, t=tokens: e.tensor_tensor(
            rowC[0:1, 0:t], rowA[0:1, 0:t], rowA[0:1, 0:t], ALU.mult))
        p.op("vector", lambda e, t=tokens: e.tensor_tensor(
            rowB_[0:1, 0:t], rowB_[0:1, 0:t], rowC[0:1, 0:t], ALU.subtract))
        p.op("vector", lambda e, t=tokens, eps=eps: e.tensor_scalar(
            rowB_[0:1, 0:t], rowB_[0:1, 0:t], float(eps), None, ALU.add), sem_dv)
        p.wait("scalar", sem_dv)
        p.op("scalar", lambda e, t=tokens: e.activation(
            rowB_[0:1, 0:t], rowB_[0:1, 0:t], AF.Ln), sem_ac)
        p.op("scalar", lambda e, t=tokens: e.activation(
            rowC[0:1, 0:t], rowB_[0:1, 0:t], AF.Exp, scale=-0.5), sem_ac)
        p.wait("tensor", sem_ac)
        for g in range(ngr):
            p.wait("tensor", sem_dv)
            p.op("tensor", lambda e, g=g: e.matmul(
                psf[0][:, :], ones_row_f[:, :], rowA[0:1, g * 512:(g + 1) * 512],
                start=True, stop=True))
            p.op("tensor", lambda e, g=g: e.matmul(
                psf[1][:, :], ones_row_f[:, :], rowC[0:1, g * 512:(g + 1) * 512],
                start=True, stop=True), sem_pe)
            p.wait("vector", sem_pe)
            p.op("vector", lambda e, g=g: e.tensor_copy(
                muB[:, g * 512:(g + 1) * 512], psf[0][:, :]))
            p.op("vector", lambda e, g=g: e.tensor_copy(
                rB[:, g * 512:(g + 1) * 512], psf[1][:, :]), sem_dv)
        p.wait("tensor", sem_dv)
        for fc in range(FC):
            if reuse_staged:
                p.op("vector", lambda e, fc=fc, t=tokens, dst=lnx_dst: e.tensor_tensor(
                    xb_f[:, 0:t], dst[:, fc * t:(fc + 1) * t], muB[:, 0:t],
                    ALU.subtract))
            else:
                p.wait("sync", sem_dv)
                p.dma("sync", xb_f[:, 0:tokens], x_src(fc), sem_x)
                p.wait("vector", sem_x)
                p.op("vector", lambda e, t=tokens: e.tensor_tensor(
                    xb_f[:, 0:t], xb_f[:, 0:t], muB[:, 0:t], ALU.subtract))
            p.op("vector", lambda e, fc=fc, t=tokens, dst=lnx_dst: e.tensor_tensor(
                dst[:, fc * t:(fc + 1) * t], xb_f[:, 0:t], rB[:, 0:t], ALU.mult), sem_dv)

    proj_pe = [0]

    def emit_head(h, kv_only, jrange=None, tgs=None, kdec_src=None):
        if jrange is None:
            jrange = range(NB)
        if tgs is None:
            tgs = range(4)
        kd_dec = kdecB_h if kdec_src is None else kdec_src
        p.wait("vector", sem_pe)
        if not kv_only:
            for nt in range(2):
                p.wait("vector", sem_ac)
                p.op("vector", lambda e, h=h, nt=nt: e.tensor_scalar(
                    ddscr[:, :], mt_s[:, nt * 256:(nt + 1) * 256],
                    slopeB[:, h:h + 1], None, ALU.mult), sem_dv)
                p.wait("scalar", sem_dv)
                p.op("scalar", lambda e, nt=nt: e.activation(
                    ddT_h[:, nt * 256:(nt + 1) * 256], ddscr[:, :],
                    AF.Exp, scale=-1.0), sem_ac)
            p.wait("vector", sem_ac)
        for src, dst in (((ar_s, qdecB_h),) if not kv_only else ()) + (
                ((rv_s, kdecB_h),) if kdec_src is None else ()):
            p.wait("vector", sem_ac)
            p.op("vector", lambda e, src=src, h=h: e.tensor_scalar(
                rowA[0:1, 0:256], src[:, :], slopeB[0:1, h:h + 1], None, ALU.mult), sem_dv)
            p.wait("scalar", sem_dv)
            p.wait("scalar", sem_pe)
            p.op("scalar", lambda e: e.activation(
                rowA[0:1, 256:512], rowA[0:1, 0:256], AF.Exp, scale=-1.0), sem_ac)
            p.wait("tensor", sem_ac)
            p.op("tensor", lambda e: e.matmul(
                psf[6][:, 0:256], ones_row_f[:, :], rowA[0:1, 256:512],
                start=True, stop=True), sem_pe)
            p.wait("vector", sem_pe)
            p.op("vector", lambda e, dst=dst: e.tensor_copy(dst[:, :], psf[6][:, 0:256]),
                 sem_dv)
        p.wait("gpsimd", sem_pe, proj_pe[0])
        wwid = 256 if kv_only else 384
        woff = 384 * h + (128 if kv_only else 0)
        p.dma("gpsimd", wslice[:, 0:FC * wwid],
              bass.AP(wqkvT, woff,
                      [[3 * NH * HD, 128], [128 * 3 * NH * HD, FC], [1, wwid]]), sem_g)
        p.wait("tensor", sem_g)
        ots = (1, 2) if kv_only else (0, 1, 2)
        for oi, ot in enumerate(ots):
            for tg in tgs:
                bank = (oi * 4 + tg) % 2
                p.wait("tensor", sem_ac)
                for fc in range(FC):
                    off = fc * wwid + (ot - (1 if kv_only else 0)) * 128
                    p.op("tensor", lambda e, off=off, tg=tg, fc=fc, bank=bank: e.matmul(
                        psf[bank][:, :], wslice[:, off:off + 128],
                        lnx[:, fc * CH + tg * 512:fc * CH + (tg + 1) * 512],
                        start=(fc == 0), stop=(fc == FC - 1)),
                        sem=(sem_pe if fc == FC - 1 else None))
                p.wait("scalar", sem_pe)
                p.op("scalar", lambda e, ot=ot, tg=tg, bank=bank: e.activation(
                    qkv_h[:, ot * CH + tg * 512:ot * CH + (tg + 1) * 512],
                    psf[bank][:, :], AF.Silu), sem_ac)
        proj_pe[0] = sem_pe.val
        jl = list(jrange)

        def prep(j):
            q0, k0 = 0 * CH + j * 256, 1 * CH + j * 256
            if not kv_only:
                p.op("vector", lambda e, q0=q0, j=j: e.tensor_tensor(
                    qd_t[j % 2][:, :], qkv_h[:, q0:q0 + 256], qdecB_h[:, :],
                    ALU.mult), sem_dv)
            p.op("vector", lambda e, k0=k0, j=j: e.tensor_tensor(
                kd_t[j % 2][:, :], qkv_h[:, k0:k0 + 256], kd_dec[:, :],
                ALU.mult), sem_dv)
            return sem_dv.val

        p.wait("vector", sem_ac)
        p.op("vector", lambda e, h=h: e.tensor_copy(
            kv_bf[0][:, :], kv_all[:, h * 128:(h + 1) * 128]), sem_dv)
        dv_prep = prep(jl[0])
        for ji, j in enumerate(jl):
            q0, k0, v0 = (0 * CH + j * 256, 1 * CH + j * 256, 2 * CH + j * 256)
            qkb = 2 + (j % 2)
            oab = 4 + (j % 2)
            p.wait("tensor", sem_dv, dv_prep)
            for nt in range(2):
                p.op("tensor", lambda e, v0=v0, nt=nt: e.transpose(
                    pst[:, nt * 128:(nt + 1) * 128],
                    qkv_h[:, v0 + nt * 128:v0 + (nt + 1) * 128], id_bf[:, :]))
            for nt in range(2):
                p.op("tensor", lambda e, nt=nt, j=j: e.transpose(
                    pst[:, 256 + nt * 128:256 + (nt + 1) * 128],
                    kd_t[j % 2][:, nt * 128:(nt + 1) * 128], id_bf[:, :]),
                    sem=(sem_pe if nt == 1 else None))
            if not kv_only:
                for nt in range(2):
                    p.op("tensor", lambda e, k0=k0, q0=q0, nt=nt, qkb=qkb: e.matmul(
                        psf[qkb][:, nt * 256:(nt + 1) * 256],
                        qkv_h[:, k0 + nt * 128:k0 + (nt + 1) * 128],
                        qkv_h[:, q0:q0 + 256], start=True, stop=True),
                        sem=(sem_pe if nt == 1 else None))
            p.wait("vector", sem_pe)
            p.op("vector", lambda e: e.tensor_copy(vT_s[:, :], pst[:, 0:256]))
            p.op("vector", lambda e: e.tensor_copy(kdT_s[:, :], pst[:, 256:512]), sem_dv)
            if not kv_only:
                p.op("vector", lambda e, qkb=qkb: e.tensor_tensor(
                    qkd_s[:, :], psf[qkb][:, :], ddT_h[:, :], ALU.mult), sem_dv)
            dv_qkd = sem_dv.val
            if ji + 1 < len(jl):
                dv_prep = prep(jl[ji + 1])
            p.wait("tensor", sem_dv, dv_qkd)
            if not kv_only:
                p.op("tensor", lambda e, j=j, oab=oab: e.matmul(
                    psf[oab][:, 0:256], kv_bf[j % 2][:, :], qd_t[j % 2][:, :],
                    start=True, stop=False))
                for nt in range(2):
                    p.op("tensor", lambda e, nt=nt, oab=oab: e.matmul(
                        psf[oab][:, 0:256], vT_s[:, nt * 128:(nt + 1) * 128],
                        qkd_s[:, nt * 256:(nt + 1) * 256],
                        start=False, stop=(nt == 1)))
            for nt in range(2):
                p.op("tensor", lambda e, nt=nt, oab=oab: e.matmul(
                    psf[oab][:, 256:384], kdT_s[:, nt * 128:(nt + 1) * 128],
                    vT_s[:, nt * 128:(nt + 1) * 128],
                    start=(nt == 0), stop=(nt == 1)), sem=(sem_pe if nt == 1 else None))
            p.wait("vector", sem_pe)
            if not kv_only:
                p.op("vector", lambda e, h=h, j=j, oab=oab: e.tensor_copy(
                    o_acc[:, h * CH + j * 256:h * CH + (j + 1) * 256],
                    psf[oab][:, 0:256]))
            p.op("vector", lambda e, h=h, oab=oab: e.scalar_tensor_tensor(
                kv_all[:, h * 128:(h + 1) * 128], kv_all[:, h * 128:(h + 1) * 128],
                lamB[:, h:h + 1], psf[oab][:, 256:384], ALU.mult, ALU.add),
                sem=(sem_dv if kv_only else None))
            if not kv_only:
                p.op("vector", lambda e, h=h, j=j: e.tensor_copy(
                    kv_bf[(j + 1) % 2][:, :], kv_all[:, h * 128:(h + 1) * 128]),
                    sem_dv)


    if mode == "fused":
        for h in range(NH):
            p.wait("vector", sem_ac)
            p.op("vector", lambda e, h=h: e.tensor_scalar(
                rowA[0:1, 0:256], rv_s[:, :], slopeB[0:1, h:h + 1], None,
                ALU.mult), sem_dv)
            p.wait("scalar", sem_dv)
            p.wait("scalar", sem_pe)
            p.op("scalar", lambda e: e.activation(
                rowA[0:1, 256:512], rowA[0:1, 0:256], AF.Exp, scale=-1.0), sem_ac)
            p.wait("tensor", sem_ac)
            p.op("tensor", lambda e: e.matmul(
                psf[6][:, 0:256], ones_row_f[:, :], rowA[0:1, 256:512],
                start=True, stop=True), sem_pe)
            p.wait("vector", sem_pe)
            p.op("vector", lambda e, h=h: e.tensor_copy(
                kdec_all[:, h * 256:(h + 1) * 256], psf[6][:, 0:256]), sem_dv)
        for slab in range(WARM_NB // NB):
            heads = [h for h in range(NH)
                     if (slab + 1) * NB > WARM_NB - WARM_DEPTH[h]]
            if not heads:
                continue
            emit_ln_sweep(lambda fc, slab=slab: xwinT[
                fc * 128:(fc + 1) * 128, slab * CH:(slab + 1) * CH],
                lnx, CH, EPS_LN, reuse_staged=True)
            p.barrier()
            for h in heads:
                jr = [j for j in range(NB)
                      if slab * NB + j >= WARM_NB - WARM_DEPTH[h]]
                emit_head(h, kv_only=True, jrange=jr,
                          tgs=sorted({j // 2 for j in jr}),
                          kdec_src=kdec_all[:, h * 256:(h + 1) * 256])
            p.barrier()
    emit_ln_sweep(lambda fc: xT[fc * 128:(fc + 1) * 128, :], lnx, CH, EPS_LN)
    p.barrier()
    if upto == "ln":
        if debug == "lnx":
            for fc in range(FC):
                p.chain_op("vector", lambda e, fc=fc: e.tensor_copy(
                    xb_f[:, 0:CH], lnx[:, fc * CH:(fc + 1) * CH]))
                p.chain_dma("sync", outs["dbg"][:, fc * CH:(fc + 1) * CH], xb_f[:, 0:CH])
        return _finish(nc, p, stack)

    if mode != "v1a":
        for og in range(FC):
            b = og % 2
            p.wait("gpsimd", sem_pe)
            p.dma("gpsimd", wg_sl[b][:, :],
                  bass.AP(wgT, og * 128, [[H, 128], [128 * H, FC], [1, 128]]), sem_g)
            p.wait("tensor", sem_g)
            for tg in range(4):
                bank = 2 + (tg % 2)
                p.wait("tensor", sem_ac)
                for fc in range(FC):
                    p.op("tensor", lambda e, b=b, tg=tg, fc=fc, bank=bank: e.matmul(
                        psf[bank][:, :], wg_sl[b][:, fc * 128:(fc + 1) * 128],
                        lnx[:, fc * CH + tg * 512:fc * CH + (tg + 1) * 512],
                        start=(fc == 0), stop=(fc == FC - 1)),
                        sem=(sem_pe if fc == FC - 1 else None))
                p.wait("scalar", sem_pe)
                p.wait("scalar", sem_g)
                p.op("scalar", lambda e, tg=tg, bank=bank: e.activation(
                    sq_par[tg % 2][:, :], psf[bank][:, :], AF.Sigmoid), sem_ac)
                p.wait("gpsimd", sem_ac)
                p.dma("gpsimd", gate_sp[og * 128:(og + 1) * 128, tg * 512:(tg + 1) * 512],
                      sq_par[tg % 2][:, :], sem_g)
        p.barrier()
    if upto == "gate":
        if debug == "gate":
            for fc in range(FC):
                p.chain_dma("sync", sq_par[0][:, :], gate_sp[fc * 128:(fc + 1) * 128, 0:512])
                p.chain_op("vector", lambda e: e.tensor_copy(qtmp1[:, :], sq_par[0][:, :]))
                p.chain_dma("sync", outs["dbg"][:, fc * 512:(fc + 1) * 512], qtmp1[:, :])
        return _finish(nc, p, stack)

    for h in range(NH):
        emit_head(h, kv_only=(mode == "v1a"))
    p.barrier()
    if mode == "v1a":
        for h in range(NH):
            p.dma("sync", outs["S"][h * HD:(h + 1) * HD, :],
                  kv_all[:, h * 128:(h + 1) * 128], sem_st)
        p.wait("sync", sem_st)
        return _finish(nc, p, stack)

    if upto == "attn":
        if debug == "oacc":
            for hh in range(NH):
                p.chain_op("vector", lambda e, hh=hh: e.tensor_copy(
                    wstr[0][:, :], o_acc[:, hh * CH:(hh + 1) * CH]))
                p.chain_dma("sync", outs["dbg"][:, hh * CH:(hh + 1) * CH], wstr[0][:, :])
        return _finish(nc, p, stack)

    for og in range(FC):
        p.dma("gpsimd", woT_bf[:, og * CH:(og + 1) * CH],
              bass.AP(woT, og * 128 * H, [[H, 128], [128, FC], [1, 128]]), sem_g)
    p.wait("tensor", sem_g)
    p.wait("scalar", sem_g)
    p.wait("vector", sem_g)
    for j in range(NB):
        p.wait("gpsimd", sem_dv)
        p.dma("gpsimd", gate_blk[:, :],
              bass.AP(gate_sp, j * 256, [[CH, 128], [128 * CH, FC], [1, 256]]), sem_g)
        pe_base = sem_pe.val
        for hh in range(NH):
            if hh >= 2:
                p.wait("scalar", sem_pe, pe_base + hh - 1)
            p.op("scalar", lambda e, hh=hh, j=j: e.activation(
                sq_parE[hh % 2][:, :],
                o_acc[:, hh * CH + j * 256:hh * CH + (j + 1) * 256], AF.Square), sem_ac)
            p.wait("tensor", sem_ac)
            p.op("tensor", lambda e, hh=hh: e.matmul(
                psf[6][0:1, 0:256], ones_col_bf[:, :], sq_parE[hh % 2][:, :],
                start=(hh == 0), stop=(hh == NH - 1)), sem_pe)
        p.wait("vector", sem_pe)
        p.op("vector", lambda e: e.tensor_scalar(
            rowA[0:1, 0:256], psf[6][0:1, 0:256], 1.0 / H, float(EPS_ATTN),
            ALU.mult, ALU.add), sem_dv)
        p.wait("scalar", sem_dv)
        p.op("scalar", lambda e: e.activation(rowA[0:1, 0:256], rowA[0:1, 0:256], AF.Ln))
        p.op("scalar", lambda e: e.activation(
            rowA[0:1, 256:512], rowA[0:1, 0:256], AF.Exp, scale=-0.5), sem_ac)
        p.wait("tensor", sem_ac)
        p.op("tensor", lambda e: e.matmul(
            psf[6][:, 256:512], ones_row_f[:, :], rowA[0:1, 256:512],
            start=True, stop=True), sem_pe)
        p.wait("vector", sem_pe)
        p.op("vector", lambda e: e.tensor_copy(bcq1[:, :], psf[6][:, 256:512]), sem_dv)
        p.wait("vector", sem_g)
        for hh in range(NH):
            p.op("vector", lambda e, hh=hh, j=j: e.scalar_tensor_tensor(
                ptmp[:, :], o_acc[:, hh * CH + j * 256:hh * CH + (j + 1) * 256],
                normw_s[:, hh:hh + 1], bcq1[:, :], ALU.mult, ALU.mult))
            p.op("vector", lambda e, hh=hh: e.tensor_tensor(
                p_bf[:, hh * 256:(hh + 1) * 256], ptmp[:, :],
                gate_blk[:, hh * 256:(hh + 1) * 256], ALU.mult), sem_dv)
        p.wait("tensor", sem_dv)
        for fc in range(FC):
            bank = fc % 2
            p.wait("tensor", sem_dv)
            for og in range(FC):
                p.op("tensor", lambda e, og=og, fc=fc, bank=bank: e.matmul(
                    psf[bank][:, 0:256],
                    woT_bf[:, og * CH + fc * 128:og * CH + (fc + 1) * 128],
                    p_bf[:, og * 256:(og + 1) * 256],
                    start=(og == 0), stop=(og == FC - 1)),
                    sem=(sem_pe if og == FC - 1 else None))
            p.wait("vector", sem_pe)
            p.op("vector", lambda e, fc=fc, bank=bank: e.tensor_copy(
                ybf_blk[:, fc * 256:(fc + 1) * 256], psf[bank][:, 0:256]), sem_dv)
        p.wait("scalar", sem_dv)
        p.wait("tensor", sem_dv)
        pe2 = sem_pe.val
        for fc in range(FC):
            if fc >= 2:
                p.wait("scalar", sem_pe, pe2 + fc - 1)
            p.op("scalar", lambda e, fc=fc: e.activation(
                sq_parE[fc % 2][:, :], ybf_blk[:, fc * 256:(fc + 1) * 256],
                AF.Square), sem_ac)
            p.wait("tensor", sem_ac)
            p.op("tensor", lambda e, fc=fc: e.matmul(
                psf[2][0:1, 0:256], ones_col_bf[:, :], sq_parE[fc % 2][:, :],
                start=(fc == 0), stop=(fc == FC - 1)), sem_pe)
        for fc in range(FC):
            p.op("tensor", lambda e, fc=fc: e.matmul(
                psf[3][0:1, 0:256], ones_col_bf[:, :],
                ybf_blk[:, fc * 256:(fc + 1) * 256],
                start=(fc == 0), stop=(fc == FC - 1)),
                sem=(sem_pe if fc == FC - 1 else None))
        p.wait("vector", sem_pe)
        p.op("vector", lambda e: e.tensor_scalar(
            rowA[0:1, 0:256], psf[3][0:1, 0:256], 1.0 / H, None, ALU.mult))
        p.op("vector", lambda e: e.tensor_scalar(
            rowB_[0:1, 0:256], psf[2][0:1, 0:256], 1.0 / H, None, ALU.mult))
        p.op("vector", lambda e: e.tensor_tensor(
            rowC[0:1, 0:256], rowA[0:1, 0:256], rowA[0:1, 0:256], ALU.mult))
        p.op("vector", lambda e: e.tensor_tensor(
            rowB_[0:1, 0:256], rowB_[0:1, 0:256], rowC[0:1, 0:256], ALU.subtract))
        p.op("vector", lambda e: e.tensor_scalar(
            rowB_[0:1, 0:256], rowB_[0:1, 0:256], float(EPS_LN), None, ALU.add), sem_dv)
        p.wait("scalar", sem_dv)
        p.op("scalar", lambda e: e.activation(rowB_[0:1, 0:256], rowB_[0:1, 0:256], AF.Ln))
        p.op("scalar", lambda e: e.activation(
            rowC[0:1, 0:256], rowB_[0:1, 0:256], AF.Exp, scale=-0.5), sem_ac)
        p.wait("tensor", sem_ac)
        p.wait("tensor", sem_dv)
        p.op("tensor", lambda e: e.matmul(
            psf[4][:, 0:256], ones_row_f[:, :], rowA[0:1, 0:256], start=True, stop=True))
        p.op("tensor", lambda e: e.matmul(
            psf[4][:, 256:512], ones_row_f[:, :], rowC[0:1, 0:256],
            start=True, stop=True), sem_pe)
        p.wait("vector", sem_pe)
        p.op("vector", lambda e: e.tensor_copy(bcq2[:, 0:256], psf[4][:, 0:256]))
        p.op("vector", lambda e: e.tensor_copy(bcq2[:, 256:512], psf[4][:, 256:512]), sem_dv)
        for fc in range(FC):
            bpar = fc % 2
            p.wait("sync", sem_dv)
            p.dma("sync", x4_par[bpar][:, :],
                  xT[fc * 128:(fc + 1) * 128, j * 256:(j + 1) * 256], sem_x)
            p.wait("vector", sem_x)
            p.wait("vector", sem_g)
            p.op("vector", lambda e, fc=fc, bpar=bpar: e.tensor_tensor(
                o4_par[bpar][:, :], ybf_blk[:, fc * 256:(fc + 1) * 256],
                bcq2[:, 0:256], ALU.subtract))
            p.op("vector", lambda e, bpar=bpar: e.tensor_tensor(
                o4_par[bpar][:, :], o4_par[bpar][:, :], bcq2[:, 256:512], ALU.mult))
            p.op("vector", lambda e, bpar=bpar: e.tensor_tensor(
                o4_par[bpar][:, :], o4_par[bpar][:, :], x4_par[bpar][:, :],
                ALU.add), sem_dv)
            p.wait("gpsimd", sem_dv)
            p.dma("gpsimd", x2_sp[fc * 128:(fc + 1) * 128, j * 256:(j + 1) * 256],
                  o4_par[bpar][:, :], sem_g)
    p.barrier()
    if upto == "x2":
        if debug == "x2":
            for fc in range(FC):
                p.chain_dma("sync", outs["dbg"][:, fc * CH:(fc + 1) * CH],
                            x2_sp[fc * 128:(fc + 1) * 128, :])
        if debug == "phe":
            off = 4096
            for nm, buf in (("p", p_bf), ("g", gate_blk), ("y", ybf_blk)):
                for hh in range(2):
                    p.chain_op("vector", lambda e, buf=buf, hh=hh: e.tensor_copy(
                        wstr[0][:, :], buf[:, hh * 2048:(hh + 1) * 2048]))
                    p.chain_dma("sync", outs["dbg"][:, off:off + 2048], wstr[0][:, :])
                    off += 2048
            p.chain_dma("sync", outs["dbg"][:, off:off + 256], bcq1[:, :])
            p.chain_dma("sync", outs["dbg"][:, off + 256:off + 768], bcq2[:, :])
        return _finish(nc, p, stack)

    env = dict(locals())
    emit_ffn(p, env)
    p.barrier()
    if upto in ("xq", "mid"):
        if debug == "xq":
            for fc in range(FC):
                p.chain_op("vector", lambda e, fc=fc: e.tensor_copy(
                    wstr[0][:, :], xq[:, fc * CH:(fc + 1) * CH]))
                p.chain_dma("sync", outs["dbg"][:, fc * CH:(fc + 1) * CH], wstr[0][:, :])
        if debug == "bfly":
            for q in range(4):
                p.chain_op("vector", lambda e, q=q: e.tensor_copy(
                    wstr[0][:, :], zbuf[:, q * 2048:(q + 1) * 2048]))
                p.chain_dma("sync", outs["dbg"][:, q * 2048:(q + 1) * 2048], wstr[0][:, :])
            for q in range(4):
                p.chain_op("vector", lambda e, q=q: e.tensor_copy(
                    wstr[0][:, :], zpong[:, q * 2048:(q + 1) * 2048]))
                p.chain_dma("sync", outs["dbg"][:, 8192 + q * 2048:8192 + (q + 1) * 2048],
                            wstr[0][:, :])
            p.chain_dma("sync", outs["dbg"][:, 16384:16896], s1Bt[:, :])
        if debug == "mid":
            for oc in range(FC):
                p.chain_dma("sync", sq_parD[0][:, :], mid_sp[oc * 128:(oc + 1) * 128, 0:512])
                p.chain_op("vector", lambda e: e.tensor_copy(qtmp1[:, :], sq_parD[0][:, :]))
                p.chain_dma("sync", outs["dbg"][:, oc * 512:(oc + 1) * 512], qtmp1[:, :])
        if debug == "d2a":
            p.chain_dma("sync", outs["dbg"][:, 0:2048], csgB[:, :])
            p.chain_dma("sync", outs["dbg"][:, 2048:4096], csuB[:, :])
            for nm, tb, off in (("g", terng, 4096), ("u", ternu, 6144)):
                p.chain_op("vector", lambda e, tb=tb: e.tensor_copy(wstr[0][:, :], tb[:, :]))
                p.chain_dma("sync", outs["dbg"][:, off:off + 2048], wstr[0][:, :])
            for oc in range(FC):
                p.chain_dma("sync", sq_parD[0][:, :], mid_sp[oc * 128:(oc + 1) * 128, 0:512])
                p.chain_op("vector", lambda e: e.tensor_copy(qtmp1[:, :], sq_parD[0][:, :]))
                p.chain_dma("sync", outs["dbg"][:, 8192 + oc * 512:8192 + (oc + 1) * 512],
                            qtmp1[:, :])
            p.chain_dma("sync", outs["dbg"][:, 16384:18432], rcp1B[:, :])
            p.chain_dma("sync", outs["dbg"][:, 18432:20480], wgu_par[0][:, :])
            for fc4 in range(4):
                p.chain_op("vector", lambda e, fc4=fc4: e.tensor_copy(
                    wstr[0][:, 0:512], xq[:, fc4 * 4 * CH:fc4 * 4 * CH + 512]))
                p.chain_dma("sync", outs["dbg"][:, 20480 + fc4 * 512:20480 + (fc4 + 1) * 512],
                            wstr[0][:, 0:512])
        if debug == "ws":
            p.chain_dma("sync", outs["dbg"][0:1, 0:8], wsrow[0:1, 0:8])
    return _finish(nc, p, stack)


def emit_ffn(p, L):
    g = L
    upto = g["upto"]
    psf, pst = g["psf"], g["pst"]
    sem_x, sem_g, sem_pe, sem_dv, sem_ac, sem_pl, sem_st = (
        g["sem_x"], g["sem_g"], g["sem_pe"], g["sem_dv"], g["sem_ac"], g["sem_pl"],
        g["sem_st"])
    ones_col_bf, ones_row_f, id_f, h128_bf = (
        g["ones_col_bf"], g["ones_row_f"], g["id_f"], g["h128_bf"])
    ones_col_f = g["ones_col_f"]
    id_bf, pst_ = g["id_bf"], g["pst"]
    rowA, rowB_, rowC = g["rowA"], g["rowB_"], g["rowC"]
    wstr, redcols = g["wstr"], g["redcols"]
    wsrow, wscol, thrcol, rnd_off = g["wsrow"], g["wscol"], g["thrcol"], g["rnd_off"]
    wgateT, wupT, wdownT, x2_sp, mid_sp, wdt_sp = (
        g["wgateT"], g["wupT"], g["wdownT"], g["x2_sp"], g["mid_sp"], g["wdt_sp"])
    x2st, zbuf, zpong, d1bc1, d1bc2, s1Bt = (
        g["x2st"], g["zbuf"], g["zpong"], g["d1bc1"], g["d1bc2"], g["s1Bt"])
    xq, qtmp1, qtmp2, qi32, scol = g["xq"], g["qtmp1"], g["qtmp2"], g["qi32"], g["scol"]
    sq_parD = g["sq_parD"]
    rcp1B = g["rcp1B"]
    terng, ternu, tb_tern = g["terng"], g["ternu"], g["tb_tern"]
    csgB, csuB = g["csgB"], g["csuB"]
    wgu_par, wuu_par = g["wgu_par"], g["wuu_par"]
    mtmp1, mtmp2, mtmp3, mid_par = g["mtmp1"], g["mtmp2"], g["mtmp3"], g["mid_par"]
    bufA, bufB = g["bufA"], g["bufB"]
    y4_blk, ybf4 = g["y4_blk"], g["ybf4"]
    wdf_par, ternd_a, ternd_b, wdt_ft = (
        g["wdf_par"], g["ternd_a"], g["ternd_b"], g["wdt_ft"])
    s2Bt, rcp2Bt, csdBt, e4bc = g["s2Bt"], g["rcp2Bt"], g["csdBt"], g["e4bc"]
    xT, outs = g["xT"], g["outs"]
    outT = outs.get("outT")

    for mi, wmat in enumerate((wgateT, wupT, wdownT)):
        nrows = H if mi < 2 else I8192
        ncols = I8192 if mi < 2 else H
        nch = (nrows // 128) * (ncols // 2048)
        ci = 0
        dv0 = sem_dv.val
        x0 = sem_x.val
        for rr in range(nrows // 128):
            for cc in range(ncols // 2048):
                b = ci % 2
                p.wait("sync", sem_dv, dv0 + max(0, ci - 1))
                p.dma("sync", wstr[b][:, :],
                      wmat[rr * 128:(rr + 1) * 128, cc * 2048:(cc + 1) * 2048], sem_x)
                p.wait("vector", sem_x, x0 + 16 * (ci + 1))
                p.op("vector", lambda e, b=b, ci=ci: e.tensor_reduce(
                    redcols[:, ci:ci + 1], wstr[b][:, :], AX.X, ALU.add,
                    apply_absolute_value=True), sem_dv)
                ci += 1
        p.op("vector", lambda e: e.tensor_copy(qtmp1[:, 0:128], qtmp1[:, 0:128]))
        p.op("vector", lambda e: e.tensor_copy(qtmp1[:, 0:128], qtmp1[:, 0:128]))
        p.op("vector", lambda e, nch=nch: e.tensor_reduce(
            redcols[:, 0:1], redcols[:, 0:nch], AX.X, ALU.add), sem_dv)
        p.wait("tensor", sem_dv)
        p.op("tensor", lambda e: e.matmul(
            psf[5][0:1, 0:1], redcols[:, 0:1], ones_col_f[:, :],
            start=True, stop=True), sem_pe)
        p.wait("vector", sem_pe)
        p.op("vector", lambda e, mi=mi, nr=nrows, ncl=ncols: e.tensor_scalar(
            wsrow[0:1, mi:mi + 1], psf[5][0:1, 0:1], 1.0 / (nr * ncl), None,
            ALU.mult), sem_dv)
    p.wait("tensor", sem_dv)
    p.op("tensor", lambda e: e.matmul(
        psf[5][:, 0:8], ones_row_f[:, :], wsrow[0:1, 0:8], start=True, stop=True), sem_pe)
    p.wait("vector", sem_pe)
    p.op("vector", lambda e: e.tensor_copy(wscol[:, 0:4], psf[5][:, 0:4]))
    p.op("vector", lambda e: e.tensor_scalar(
        thrcol[:, 0:4], psf[5][:, 0:4], 0.5, None, ALU.mult))
    p.op("vector", lambda e: e.tensor_scalar(
        thrcol[:, 4:8], psf[5][:, 0:4], -0.5, None, ALU.mult), sem_dv)

    def quant_scales(cur, twidth, nchunks, sBdst, rcpdst, rcp_extra=None):
        nfull = 128 * nchunks
        p.op("vector", lambda e: e.tensor_reduce(
            qtmp1[:, 0:twidth],
            bass.AP(cur, 0, [[nchunks * twidth, 128], [1, twidth], [twidth, nchunks]]),
            AX.X, ALU.max))
        p.op("vector", lambda e: e.tensor_reduce(
            qtmp2[:, 0:twidth],
            bass.AP(cur, 0, [[nchunks * twidth, 128], [1, twidth], [twidth, nchunks]]),
            AX.X, ALU.min))
        p.op("vector", lambda e: e.tensor_scalar(
            qtmp2[:, 0:twidth], qtmp2[:, 0:twidth], -1.0, None, ALU.mult))
        p.op("vector", lambda e: e.tensor_tensor(
            qtmp1[:, 0:twidth], qtmp1[:, 0:twidth], qtmp2[:, 0:twidth], ALU.max))
        p.op("vector", lambda e, nfull=nfull: e.tensor_scalar(
            qtmp1[:, 0:twidth], qtmp1[:, 0:twidth], 1.0 / math.sqrt(nfull), None,
            ALU.mult), sem_dv)
        ngq = twidth // 128
        p.op("vector", lambda e, tw=twidth: e.tensor_copy(
            sq_parD[0][:, 0:tw], qtmp1[:, 0:tw]), sem_dv)
        p.wait("tensor", sem_dv)
        for q4 in range(ngq):
            p.op("tensor", lambda e, q4=q4: e.transpose(
                pst[:, q4 * 128:(q4 + 1) * 128],
                sq_parD[0][:, q4 * 128:(q4 + 1) * 128], id_bf[:, :]),
                sem=(sem_pe if q4 == ngq - 1 else None))
        p.wait("vector", sem_pe)
        for q4 in range(ngq):
            p.op("vector", lambda e, q4=q4: e.tensor_reduce(
                scol[:, q4:q4 + 1], pst[:, q4 * 128:(q4 + 1) * 128], AX.X, ALU.max))
        p.op("vector", lambda e: e.tensor_copy(qtmp2[:, 0:128], qtmp2[:, 0:128]))
        p.op("vector", lambda e: e.tensor_copy(qtmp2[:, 0:128], qtmp2[:, 0:128]))
        p.op("vector", lambda e, ngq=ngq: e.tensor_scalar(
            scol[:, 0:ngq], scol[:, 0:ngq], float(EPS_FFN), None, ALU.max))
        p.op("vector", lambda e: e.tensor_copy(qtmp2[:, 0:128], qtmp2[:, 0:128]))
        p.op("vector", lambda e: e.tensor_copy(qtmp2[:, 0:128], qtmp2[:, 0:128]))
        for q4 in range(ngq):
            p.op("vector", lambda e, q4=q4: e.reciprocal(
                scol[:, 4 + q4:5 + q4], scol[:, q4:q4 + 1]))
        p.op("vector", lambda e: e.tensor_copy(qtmp2[:, 0:128], qtmp2[:, 0:128]))
        p.op("vector", lambda e: e.tensor_copy(qtmp2[:, 0:128], qtmp2[:, 0:128]))
        p.op("vector", lambda e, ngq=ngq, nfull=nfull: e.tensor_scalar(
            scol[:, 4:4 + ngq], scol[:, 4:4 + ngq], 127.0 / math.sqrt(nfull), None,
            ALU.mult))
        p.op("vector", lambda e, ngq=ngq: e.tensor_scalar(
            scol[:, 0:ngq], scol[:, 0:ngq], 1.0 / 127.0, None, ALU.mult), sem_dv)
        p.wait("tensor", sem_dv)
        for which, cbase, dst in ((0, 4, "s"), (1, 0, "r")):
            for q4 in range(ngq):
                p.op("tensor", lambda e, q4=q4, cbase=cbase: e.transpose(
                    psf[2][0:1, 0:128],
                    scol[:, cbase + q4:cbase + q4 + 1], id_f[:, :]), sem_pe)
                p.wait("vector", sem_pe)
                p.op("vector", lambda e, q4=q4, which=which: e.tensor_copy(
                    rowA[0:1, 512 + which * 512 + q4 * 128:512 + which * 512 + (q4 + 1) * 128],
                    psf[2][0:1, 0:128]), sem_dv)
                p.wait("tensor", sem_dv)
        p.op("tensor", lambda e, twidth=twidth: e.matmul(
            psf[5][:, 0:twidth], ones_row_f[:, :], rowA[0:1, 512:512 + twidth],
            start=True, stop=True))
        p.op("tensor", lambda e, twidth=twidth: e.matmul(
            psf[6][:, 0:twidth], ones_row_f[:, :], rowA[0:1, 1024:1024 + twidth],
            start=True, stop=True), sem_pe)
        p.wait("vector", sem_pe)
        p.op("vector", lambda e, sBdst=sBdst, twidth=twidth: e.tensor_copy(
            sBdst[:, 0:twidth], psf[5][:, 0:twidth]))
        p.op("vector", lambda e, rcpdst=rcpdst, twidth=twidth: e.tensor_copy(
            rcpdst[:, 0:twidth], psf[6][:, 0:twidth]), sem_dv)

    def emit_quant_slices(cur, nslices, swidth, sB, dst_fn):
        for i in range(nslices):
            p.op("vector", lambda e, i=i, sw=swidth: e.tensor_tensor(
                qtmp1[:, 0:sw], cur(i), sB, ALU.mult))
            p.op("vector", lambda e, sw=swidth: e.tensor_scalar(
                qtmp2[:, 0:sw], qtmp1[:, 0:sw], 0.0, None, ALU.is_ge))
            p.op("vector", lambda e, sw=swidth: e.scalar_tensor_tensor(
                qtmp2[:, 0:sw], qtmp2[:, 0:sw], rnd_off[:, 0:1], qtmp1[:, 0:sw],
                ALU.mult, ALU.add))
            p.op("vector", lambda e, sw=swidth: e.tensor_scalar(
                qtmp2[:, 0:sw], qtmp2[:, 0:sw], rnd_off[0:1, 1:2], None, ALU.add))                 if False else None
            p.op("vector", lambda e, sw=swidth: e.tensor_scalar(
                qtmp2[:, 0:sw], qtmp2[:, 0:sw], rnd_off[:, 1:2], None, ALU.add))
            p.op("vector", lambda e, sw=swidth: e.tensor_copy(
                qi32[:, 0:sw], qtmp2[:, 0:sw]))
            p.op("vector", lambda e, i=i, sw=swidth: e.tensor_copy(
                dst_fn(i), qi32[:, 0:sw]), sem_dv)

    for tg in range(4):
        t0 = tg * 512
        p.wait("gpsimd", sem_pe)
        p.dma("gpsimd", x2st[:, :],
              bass.AP(x2_sp, t0, [[CH, 128], [128 * CH, FC], [1, 512]]), sem_g)
        p.wait("scalar", sem_g)
        p.wait("tensor", sem_g)
        pe_base = sem_pe.val
        for fc in range(FC):
            if fc >= 2:
                p.wait("scalar", sem_pe, pe_base + fc - 1)
            p.op("scalar", lambda e, fc=fc: e.activation(
                sq_parD[fc % 2][:, :], x2st[:, fc * 512:(fc + 1) * 512], AF.Square), sem_ac)
            p.wait("tensor", sem_ac)
            p.op("tensor", lambda e, fc=fc: e.matmul(
                psf[3][0:1, :], ones_col_bf[:, :], sq_parD[fc % 2][:, :],
                start=(fc == 0), stop=(fc == FC - 1)), sem_pe)
        for fc in range(FC):
            p.op("tensor", lambda e, fc=fc: e.matmul(
                psf[2][0:1, :], ones_col_bf[:, :], x2st[:, fc * 512:(fc + 1) * 512],
                start=(fc == 0), stop=(fc == FC - 1)),
                sem=(sem_pe if fc == FC - 1 else None))
        p.wait("vector", sem_pe)
        p.op("vector", lambda e: e.tensor_scalar(
            rowA[0:1, 0:512], psf[2][0:1, :], 1.0 / H, None, ALU.mult))
        p.op("vector", lambda e: e.tensor_scalar(
            rowB_[0:1, 0:512], psf[3][0:1, :], 1.0 / H, None, ALU.mult))
        p.op("vector", lambda e: e.tensor_tensor(
            rowC[0:1, 0:512], rowA[0:1, 0:512], rowA[0:1, 0:512], ALU.mult))
        p.op("vector", lambda e: e.tensor_tensor(
            rowB_[0:1, 0:512], rowB_[0:1, 0:512], rowC[0:1, 0:512], ALU.subtract))
        p.op("vector", lambda e: e.tensor_scalar(
            rowB_[0:1, 0:512], rowB_[0:1, 0:512], float(EPS_LN), None, ALU.add), sem_dv)
        p.wait("scalar", sem_dv)
        p.op("scalar", lambda e: e.activation(rowB_[0:1, 0:512], rowB_[0:1, 0:512], AF.Ln))
        p.op("scalar", lambda e: e.activation(
            rowC[0:1, 0:512], rowB_[0:1, 0:512], AF.Exp, scale=-0.5), sem_ac)
        p.wait("tensor", sem_ac)
        p.op("tensor", lambda e: e.matmul(
            psf[4][:, :], ones_row_f[:, :], rowA[0:1, 0:512], start=True, stop=True))
        p.op("tensor", lambda e: e.matmul(
            psf[5][:, :], ones_row_f[:, :], rowC[0:1, 0:512],
            start=True, stop=True), sem_pe)
        p.wait("vector", sem_pe)
        p.op("vector", lambda e: e.tensor_copy(d1bc1[:, :], psf[4][:, :]))
        p.op("vector", lambda e: e.tensor_copy(d1bc2[:, :], psf[5][:, :]), sem_dv)
        for fc in range(FC):
            p.wait("sync", sem_dv)
            p.dma("sync", qtmp1[:, :], x2_sp[fc * 128:(fc + 1) * 128, t0:t0 + 512], sem_x)
            p.wait("vector", sem_x)
            p.op("vector", lambda e: e.tensor_tensor(
                qtmp1[:, :], qtmp1[:, :], d1bc1[:, :], ALU.subtract))
            p.op("vector", lambda e, fc=fc: e.tensor_tensor(
                x2st[:, fc * 512:(fc + 1) * 512], qtmp1[:, :], d1bc2[:, :],
                ALU.mult), sem_dv)
            p.wait("tensor", sem_dv)
            p.op("tensor", lambda e, fc=fc: e.matmul(
                psf[fc % 2][:, :], h128_bf[:, :], x2st[:, fc * 512:(fc + 1) * 512],
                start=True, stop=True), sem_pe)
            p.wait("vector", sem_pe)
            p.op("vector", lambda e, fc=fc: e.tensor_copy(
                zbuf[:, fc * 512:(fc + 1) * 512], psf[fc % 2][:, :]), sem_dv)
        cur, oth = zbuf, zpong
        for st in range(4):
            stride = 1 << st
            npairs = FC // (2 * stride)
            def mk(tn, base, stride=stride, npairs=npairs):
                return bass.AP(tn, base,
                               [[FC * 512, 128], [2 * stride * 512, npairs],
                                [512, stride], [1, 512]])
            p.op("vector", lambda e, a=mk(cur, 0), b=mk(cur, stride * 512),
                 o=mk(oth, 0): e.tensor_tensor(o, a, b, ALU.add))
            p.op("vector", lambda e, a=mk(cur, 0), b=mk(cur, stride * 512),
                 o=mk(oth, stride * 512): e.tensor_tensor(o, a, b, ALU.subtract), sem_dv)
            cur, oth = oth, cur
        quant_scales(cur, 512, FC, s1Bt, d1bc1)
        p.op("vector", lambda e, t0=t0: e.tensor_copy(
            rcp1B[:, t0:t0 + 512], d1bc1[:, 0:512]), sem_dv)
        emit_quant_slices(
            lambda i, cur=cur: cur[:, i * 512:(i + 1) * 512], FC, 512, s1Bt[:, 0:512],
            lambda i, t0=t0: xq[:, i * CH + t0:i * CH + t0 + 512])
    p.barrier()
    if upto == "xq":
        if g.get("debug") == "qsc":
            dbg = outs["dbg"]
            p.chain_dma("sync", dbg[:, 0:8], scol[:, :])
            p.chain_dma("sync", dbg[0:1, 8:8 + 1024], rowA[0:1, 512:1536])
            p.chain_dma("sync", dbg[:, 2048:2560], s1Bt[:, :])
            p.chain_dma("sync", dbg[:, 2560:3072], qtmp1[:, :])
            p.chain_dma("sync", dbg[:, 3072:3584], d1bc1[:, :])
        return

    p.op("vector", lambda e: e.tensor_scalar(
        csgB[:, :], rcp1B[:, :], wscol[:, 0:1], None, ALU.mult))
    p.op("vector", lambda e: e.tensor_scalar(
        csuB[:, :], rcp1B[:, :], wscol[:, 1:2], None, ALU.mult), sem_dv)
    p.wait("scalar", sem_dv)
    tern_dv = sem_dv.val
    for om in range(OC):
        bw = om % 2
        p.wait("sync", sem_dv, tern_dv)
        p.dma("sync", wgu_par[bw][:, :],
              bass.AP(wgateT, om * 128, [[I8192, 128], [128 * I8192, FC], [1, 128]]),
              sem_x)
        p.dma("sync", wuu_par[bw][:, :],
              bass.AP(wupT, om * 128, [[I8192, 128], [128 * I8192, FC], [1, 128]]),
              sem_x)
        p.wait("vector", sem_x)
        p.wait("vector", sem_pe)
        for wsrc, tdst, ti in ((wgu_par[bw], terng, 0), (wuu_par[bw], ternu, 1)):
            p.op("vector", lambda e, w=wsrc, ti=ti: e.tensor_scalar(
                tb_tern[:, :], w[:, :], thrcol[:, 4 + ti:5 + ti], None, ALU.is_lt))
            p.op("vector", lambda e, w=wsrc, t=tdst, ti=ti: e.tensor_scalar(
                t[:, :], w[:, :], thrcol[:, ti:ti + 1], None, ALU.is_gt))
            p.op("vector", lambda e, t=tdst: e.tensor_tensor(
                t[:, :], t[:, :], tb_tern[:, :], ALU.subtract), sem_dv)
        tern_dv = sem_dv.val
        p.wait("tensor", sem_dv)
        for tg in range(4):
            for mat, tern, psb, cs in ((0, terng, 0, csgB), (1, ternu, 1, csuB)):
                p.wait("tensor", sem_dv)
                for fc in range(FC):
                    p.op("tensor", lambda e, tern=tern, fc=fc, tg=tg, psb=psb: e.matmul(
                        psf[psb][:, :], tern[:, fc * 128:(fc + 1) * 128],
                        xq[:, fc * CH + tg * 512:fc * CH + (tg + 1) * 512],
                        start=(fc == 0), stop=(fc == FC - 1)),
                        sem=(sem_pe if fc == FC - 1 else None))
            p.wait("vector", sem_pe)
            p.op("vector", lambda e, tg=tg: e.tensor_tensor(
                mtmp1[:, :], psf[0][:, :], csgB[:, tg * 512:(tg + 1) * 512], ALU.mult),
                sem_dv)
            p.wait("scalar", sem_dv)
            p.op("scalar", lambda e: e.activation(mtmp2[:, :], mtmp1[:, :], AF.Silu),
                 sem_ac)
            p.op("vector", lambda e, tg=tg: e.tensor_tensor(
                mtmp3[:, :], psf[1][:, :], csuB[:, tg * 512:(tg + 1) * 512], ALU.mult),
                sem_dv)
            p.wait("vector", sem_ac)
            p.wait("vector", sem_g)
            p.op("vector", lambda e, tg=tg: e.tensor_tensor(
                mid_par[tg % 2][:, :], mtmp2[:, :], mtmp3[:, :], ALU.mult), sem_dv)
            p.wait("gpsimd", sem_dv)
            p.dma("gpsimd", mid_sp[om * 128:(om + 1) * 128, tg * 512:(tg + 1) * 512],
                  mid_par[tg % 2][:, :], sem_g)
    p.barrier()
    if upto == "mid":
        return

    dv0 = sem_dv.val
    x0 = sem_x.val
    g0 = sem_g.val
    for oc in range(OC):
        b = oc % 2
        res = ternd_a if b == 0 else ternd_b
        p.wait("sync", sem_dv, dv0 + max(0, oc - 1))
        p.dma("sync", wdf_par[b][:, :], wdownT[oc * 128:(oc + 1) * 128, :], sem_x)
        p.wait("vector", sem_x, x0 + 16 * (oc + 1))
        p.wait("vector", sem_g, g0 + 16 * max(0, oc - 1))
        p.op("vector", lambda e, b=b: e.tensor_scalar(
            tb_tern[:, :], wdf_par[b][:, :], thrcol[:, 6:7], None, ALU.is_lt))
        p.op("vector", lambda e, b=b, res=res: e.tensor_scalar(
            res[:, :], wdf_par[b][:, :], thrcol[:, 2:3], None, ALU.is_gt))
        p.op("vector", lambda e, res=res: e.tensor_tensor(
            res[:, :], res[:, :], tb_tern[:, :], ALU.subtract), sem_dv)
        p.wait("gpsimd", sem_dv, dv0 + oc + 1)
        p.dma("gpsimd", wdt_sp[oc * 128:(oc + 1) * 128, :], res[:, :], sem_g)
    p.barrier()

    for tg in range(4):
        for sub in range(2):
            t0 = tg * 512 + sub * 256
            p.wait("gpsimd", sem_pe)
            p.dma("gpsimd", bufA[:, :],
                  bass.AP(mid_sp, t0, [[CH, 128], [128 * CH, OC], [1, 256]]), sem_g)
            p.wait("tensor", sem_g)
            for oc in range(OC):
                p.wait("tensor", sem_dv)
                p.op("tensor", lambda e, oc=oc: e.matmul(
                    psf[oc % 2][:, 0:256], h128_bf[:, :], bufA[:, oc * 256:(oc + 1) * 256],
                    start=True, stop=True), sem_pe)
                p.wait("vector", sem_pe)
                p.op("vector", lambda e, oc=oc: e.tensor_copy(
                    bufB[:, oc * 256:(oc + 1) * 256], psf[oc % 2][:, 0:256]), sem_dv)
            cur, oth = bufB, bufA
            for st in range(6):
                stride = 1 << st
                npairs = OC // (2 * stride)
                def mk(tn, base, stride=stride, npairs=npairs):
                    return bass.AP(tn, base,
                                   [[OC * 256, 128], [2 * stride * 256, npairs],
                                    [256, stride], [1, 256]])
                p.op("vector", lambda e, a=mk(cur, 0), b=mk(cur, stride * 256),
                     o=mk(oth, 0): e.tensor_tensor(o, a, b, ALU.add))
                p.op("vector", lambda e, a=mk(cur, 0), b=mk(cur, stride * 256),
                     o=mk(oth, stride * 256): e.tensor_tensor(o, a, b, ALU.subtract),
                     sem_dv)
                cur, oth = oth, cur
            quant_scales(cur, 256, OC, s2Bt, rcp2Bt)
            p.op("vector", lambda e: e.tensor_scalar(
                csdBt[:, :], rcp2Bt[:, 0:256], wscol[:, 2:3], None, ALU.mult), sem_dv)
            emit_quant_slices(
                lambda i, cur=cur: cur[:, i * 256:(i + 1) * 256], OC, 256, s2Bt[:, 0:256],
                lambda i, oth=oth: oth[:, i * 256:(i + 1) * 256])
            midq = oth
            p.wait("gpsimd", sem_dv)
            for ft in range(FC):
                bank = 2 + (ft % 4)
                p.wait("gpsimd", sem_pe)
                p.dma("gpsimd", wdt_ft[:, :],
                      bass.AP(wdt_sp, ft * 128, [[H, 128], [128 * H, OC], [1, 128]]),
                      sem_g)
                p.wait("tensor", sem_g)
                p.wait("tensor", sem_dv)
                for oc in range(OC):
                    p.op("tensor", lambda e, oc=oc, bank=bank: e.matmul(
                        psf[bank][:, 0:256], wdt_ft[:, oc * 128:(oc + 1) * 128],
                        midq[:, oc * 256:(oc + 1) * 256],
                        start=(oc == 0), stop=(oc == OC - 1)),
                        sem=(sem_pe if oc == OC - 1 else None))
                p.wait("vector", sem_pe)
                p.op("vector", lambda e, ft=ft, bank=bank: e.tensor_tensor(
                    y4_blk[:, ft * 256:(ft + 1) * 256], psf[bank][:, 0:256],
                    csdBt[:, :], ALU.mult))
                p.op("vector", lambda e, ft=ft: e.tensor_copy(
                    ybf4[:, ft * 256:(ft + 1) * 256], y4_blk[:, ft * 256:(ft + 1) * 256]),
                    sem_dv)
            p.wait("scalar", sem_dv)
            p.wait("tensor", sem_dv)
            pe4 = sem_pe.val
            for fc in range(FC):
                if fc >= 2:
                    p.wait("scalar", sem_pe, pe4 + fc - 1)
                p.op("scalar", lambda e, fc=fc: e.activation(
                    sq_parD[fc % 2][:, 0:256], ybf4[:, fc * 256:(fc + 1) * 256],
                    AF.Square), sem_ac)
                p.wait("tensor", sem_ac)
                p.op("tensor", lambda e, fc=fc: e.matmul(
                    psf[0][0:1, 0:256], ones_col_bf[:, :], sq_parD[fc % 2][:, 0:256],
                    start=(fc == 0), stop=(fc == FC - 1)), sem_pe)
            for fc in range(FC):
                p.op("tensor", lambda e, fc=fc: e.matmul(
                    psf[1][0:1, 0:256], ones_col_bf[:, :],
                    ybf4[:, fc * 256:(fc + 1) * 256],
                    start=(fc == 0), stop=(fc == FC - 1)),
                    sem=(sem_pe if fc == FC - 1 else None))
            p.wait("vector", sem_pe)
            p.op("vector", lambda e: e.tensor_scalar(
                rowA[0:1, 0:256], psf[1][0:1, 0:256], 1.0 / H, None, ALU.mult))
            p.op("vector", lambda e: e.tensor_scalar(
                rowB_[0:1, 0:256], psf[0][0:1, 0:256], 1.0 / H, None, ALU.mult))
            p.op("vector", lambda e: e.tensor_tensor(
                rowC[0:1, 0:256], rowA[0:1, 0:256], rowA[0:1, 0:256], ALU.mult))
            p.op("vector", lambda e: e.tensor_tensor(
                rowB_[0:1, 0:256], rowB_[0:1, 0:256], rowC[0:1, 0:256], ALU.subtract))
            p.op("vector", lambda e: e.tensor_scalar(
                rowB_[0:1, 0:256], rowB_[0:1, 0:256], float(EPS_LN), None, ALU.add),
                sem_dv)
            p.wait("scalar", sem_dv)
            p.op("scalar", lambda e: e.activation(
                rowB_[0:1, 0:256], rowB_[0:1, 0:256], AF.Ln))
            p.op("scalar", lambda e: e.activation(
                rowC[0:1, 0:256], rowB_[0:1, 0:256], AF.Exp, scale=-0.5), sem_ac)
            p.wait("tensor", sem_ac)
            p.wait("tensor", sem_dv)
            p.op("tensor", lambda e: e.matmul(
                psf[0][:, 0:256], ones_row_f[:, :], rowA[0:1, 0:256],
                start=True, stop=True))
            p.op("tensor", lambda e: e.matmul(
                psf[0][:, 256:512], ones_row_f[:, :], rowC[0:1, 0:256],
                start=True, stop=True), sem_pe)
            p.wait("vector", sem_pe)
            p.op("vector", lambda e: e.tensor_copy(e4bc[:, 0:256], psf[0][:, 0:256]))
            p.op("vector", lambda e: e.tensor_copy(e4bc[:, 256:512], psf[0][:, 256:512]),
                 sem_dv)
            for fc in range(FC):
                p.wait("sync", sem_dv)
                p.dma("sync", qtmp1[:, 0:256],
                      x2_sp[fc * 128:(fc + 1) * 128, t0:t0 + 256], sem_x)
                p.wait("vector", sem_x)
                p.wait("vector", sem_st)
                p.op("vector", lambda e, fc=fc: e.tensor_tensor(
                    qtmp2[:, 0:256], y4_blk[:, fc * 256:(fc + 1) * 256],
                    e4bc[:, 0:256], ALU.subtract))
                p.op("vector", lambda e: e.tensor_tensor(
                    qtmp2[:, 0:256], qtmp2[:, 0:256], e4bc[:, 256:512], ALU.mult))
                p.op("vector", lambda e: e.tensor_tensor(
                    qtmp2[:, 0:256], qtmp2[:, 0:256], qtmp1[:, 0:256], ALU.add), sem_dv)
                p.wait("sync", sem_dv)
                p.dma("sync", outT[fc * 128:(fc + 1) * 128, t0:t0 + 256],
                      qtmp2[:, 0:256], sem_st)
    p.wait("sync", sem_st)


def _finish(nc, p, stack):
    p.barrier()
    with nc.Block() as block:
        @block.tensor
        def _(e):
            for f in p.ops["tensor"]:
                f(e)

        @block.vector
        def _(e):
            for f in p.ops["vector"]:
                f(e)

        @block.scalar
        def _(e):
            for f in p.ops["scalar"]:
                f(e)

        @block.gpsimd
        def _(e):
            for f in p.ops["gpsimd"]:
                f(e)

        @block.sync
        def _(e):
            for f in p.ops["sync"]:
                f(e)
    stack.close()
    return nc


def _hmat(n):
    Hm = np.array([[1.0]], np.float32)
    while Hm.shape[0] < n:
        Hm = np.block([[Hm, Hm], [Hm, -Hm]]).astype(np.float32)
    return Hm


def _mt():
    i = np.arange(1, 257, dtype=np.float32)
    d = i[None, :] - i[:, None]
    return np.where(d >= 0, d, np.float32(1e9)).astype(np.float32)


def host_prep(inputs):
    f32 = np.float32
    x = np.asarray(inputs["L0_attn_hidden_states"], f32)[0]
    mt = _mt()
    cons = {
        "wqkvT": np.ascontiguousarray(np.asarray(inputs["L0_attn_w_qkv"], f32).T),
        "wgT": np.ascontiguousarray(np.asarray(inputs["L0_attn_w_gate"], f32).T),
        "woT": np.ascontiguousarray(np.asarray(inputs["L0_attn_w_out"], f32).T),
        "normw": np.ascontiguousarray(np.asarray(inputs["L0_attn_norm_w"], f32).reshape(FC, 128).T),
        "wgateT": np.ascontiguousarray(np.asarray(inputs["L0_ffn_w_gate"], f32).T),
        "wupT": np.ascontiguousarray(np.asarray(inputs["L0_ffn_w_up"], f32).T),
        "wdownT": np.ascontiguousarray(np.asarray(inputs["L0_ffn_w_down"], f32).T),
        "slope": np.asarray(inputs["L0_attn_slope_rate"], f32).reshape(1, NH),
        "h128": _hmat(128),
        "mtA": np.ascontiguousarray(mt[0:128]),
        "mtB": np.ascontiguousarray(mt[128:256]),
        "arange1": np.arange(1, 257, dtype=f32).reshape(1, 256),
        "revrow": (255.0 - np.arange(256, dtype=f32)).reshape(1, 256),
        "onesrow": np.ones((1, 128), f32),
        "ident": np.eye(128, dtype=f32),
    }
    per_core = []
    for c in range(NCORES):
        m = dict(cons)
        xc = x[c * CH:(c + 1) * CH]
        m["xT"] = np.ascontiguousarray(xc.T)
        per_core.append(m)
    return per_core, x


def kernel(**inputs):
    per_core, x = host_prep(inputs)
    ncF = build_program("fused")
    WW = WARM_NB * BLK
    for c in range(NCORES):
        xw = np.zeros((WW, H), np.float32)
        take = min(WW, c * CH)
        if take > 0:
            xw[WW - take:] = x[c * CH - take:c * CH]
        per_core[c]["xwinT"] = np.ascontiguousarray(xw.T)
    res = run_bass_kernel_spmd(ncF, per_core, core_ids=list(range(NCORES)))
    out = np.concatenate([np.asarray(r["outT"]).T for r in res.results], axis=0)
    return out[None].astype(np.float32)


def kernel_v1(**inputs):
    per_core, x = host_prep(inputs)
    ncA = build_program("v1a")
    resA = run_bass_kernel_spmd(ncA, per_core, core_ids=list(range(NCORES)))
    Sall = np.concatenate([np.asarray(r["S"]) for r in resA.results], axis=0)
    ncB = build_program("v1b")
    for c in range(NCORES):
        per_core[c]["Sg"] = Sall
        dv = np.full((1, NCORES), 1e9, np.float32)
        for cp in range(c):
            dv[0, cp] = float(c - 1 - cp)
        per_core[c]["dvec"] = dv
    resB = run_bass_kernel_spmd(ncB, per_core, core_ids=list(range(NCORES)))
    out = np.concatenate([np.asarray(r["outT"]).T for r in resB.results], axis=0)
    return out[None].astype(np.float32)
